# version 02
# baseline (speedup 1.0000x reference)
import contextlib
import numpy as np
import ml_dtypes

import concourse.bass as bass
import concourse.mybir as mybir
from concourse.bass_utils import run_bass_kernel_spmd
from concourse.alu_op_type import AluOpType as ALU

F32 = mybir.dt.float32
BF16 = mybir.dt.bfloat16
AF = mybir.ActivationFunctionType
AX = mybir.AxisListType.X
BF = ml_dtypes.bfloat16

import os
C, L, E = 8, 8192, 1024
NH, HD = 16, 64
T = 512
NT = L // T
NT_EMIT = int(os.environ.get("KERNEL_NT", str(NT)))
P1_ON = os.environ.get("KERNEL_P1", "1") == "1"
P2_ON = os.environ.get("KERNEL_P2", "1") == "1"
TC = T // 128
NEXP, FF = 8, 4096
LN16 = float(np.log(16.0))
EPS = 1e-6
LNEPS = 1e-5

ENGINES = ("pe", "act", "dve", "pool", "sp", "gp")


class _Op:
    __slots__ = ("eng", "fn", "dma", "dma_ord", "inc", "pos", "deps", "dma_waits",
                 "tag", "_bar_dma")

    def __init__(self, eng, fn, dma, pos, tag):
        self.eng = eng
        self.fn = fn
        self.dma = dma
        self.dma_ord = 0
        self.inc = False
        self.pos = pos
        self.deps = []
        self.dma_waits = {}
        self.tag = tag
        self._bar_dma = None


class Sched:

    def __init__(self):
        self.ops = {e: [] for e in ENGINES}
        self.reg_w = {}
        self.reg_r = {}
        self.dma_count = {}
        self.bar_pending = {e: None for e in ENGINES}

    def op(self, eng, fn, reads=(), writes=(), dma=None, tag=""):
        o = _Op(eng, fn, dma, len(self.ops[eng]), tag)
        bp = self.bar_pending.get(eng)
        if bp is not None:
            prods, dma_snap = bp
            for p in prods:
                if p.eng != eng:
                    o.deps.append(p)
            o._bar_dma = dma_snap
            self.bar_pending[eng] = None

        def add(p):
            if p is None or p is o:
                return
            if p.dma is not None:
                o.dma_waits[p.dma] = max(o.dma_waits.get(p.dma, 0),
                                         16 * self.dma_count.get(p.dma, 0))
            else:
                o.deps.append(p)
                p.inc = True

        for k in reads:
            add(self.reg_w.get(k))
        for k in writes:
            add(self.reg_w.get(k))
            rd = self.reg_r.get(k)
            if rd:
                for r in rd.values():
                    add(r)
        for k in writes:
            self.reg_w[k] = o
            self.reg_r[k] = {}
        for k in reads:
            self.reg_r.setdefault(k, {})[eng] = o
        if dma is not None:
            self.dma_count[dma] = self.dma_count.get(dma, 0) + 1
            o.dma_ord = self.dma_count[dma]
        self.ops[eng].append(o)
        return o

    def barrier(self):
        prods = []
        for e in ENGINES:
            for o in reversed(self.ops[e]):
                if o.dma is None:
                    o.inc = True
                    prods.append(o)
                    break
        dma_snap = dict(self.dma_count)
        for e in ENGINES:
            self.bar_pending[e] = (prods, dma_snap)

    def finalize(self, eng_sems, dma_sems, final_dma_waits):
        val = {}
        for e in ENGINES:
            cnt = 0
            for o in self.ops[e]:
                if o.inc:
                    cnt += 1
                val[o] = cnt

        def run_engine(engobj, e):
            marks = {}
            for o in self.ops[e]:
                waits = {}
                if o._bar_dma:
                    for dn, dc in o._bar_dma.items():
                        waits[("d", dn)] = max(waits.get(("d", dn), 0), 16 * dc)
                for dn, v in o.dma_waits.items():
                    waits[("d", dn)] = max(waits.get(("d", dn), 0), v)
                for p in o.deps:
                    waits[("e", p.eng)] = max(waits.get(("e", p.eng), 0), val[p])
                for key, v in waits.items():
                    if marks.get(key, 0) >= v:
                        continue
                    marks[key] = v
                    sem = dma_sems[key[1]] if key[0] == "d" else eng_sems[key[1]]
                    engobj.wait_ge(sem, v)
                inst = o.fn(engobj)
                if o.dma is not None:
                    inst.then_inc(dma_sems[o.dma], 16)
                elif o.inc:
                    if inst is None:
                        inst = engobj.nop()
                    inst.then_inc(eng_sems[e], 1)
            if e == "gp":
                for dn, v in final_dma_waits.items():
                    engobj.wait_ge(dma_sems[dn], v)
        return run_engine


def build_program():
    nc = bass.Bass("TRN2", target_bir_lowering=False)
    es = contextlib.ExitStack()

    def din(name, shape, dt):
        return nc.dram_tensor(name, list(shape), dt, kind="ExternalInput")

    xT = din("xT", (E, L), F32)
    wqkvT = din("wqkvT", (E, 3 * E), F32)
    woutT = din("woutT", (E, E), F32)
    w1T = din("w1T", (NEXP, E, FF), F32)
    w2T = din("w2T", (NEXP, FF, E), F32)
    b1T = din("b1T", (NEXP, 128, FF // 128), F32)
    b2T = din("b2T", (NEXP, 128, E // 128), F32)
    routerT = din("routerT", (E, NEXP), F32)
    costab = din("costab", (128, L), BF16)
    sintab = din("sintab", (128, L), BF16)
    wm64 = din("wm64", (64, 64), F32)
    hpT = din("hpT", (64, 64), F32)
    s1pT = din("s1pT", (64, 64), F32)
    s2T = din("s2T", (64, 64), F32)
    s3T = din("s3T", (64, 64), F32)
    gflat = din("gflat", (1, 64 * 64), F32)
    onescol = din("onescol", (128, 1), BF16)
    onesrow = din("onesrow", (1, 128), BF16)
    ind4b = din("ind4b", (128, 4), BF16)
    ind4q = din("ind4q", (4, 4 * 128), BF16)
    ind8b = din("ind8b", (8, 8 * 128), BF16)
    id8 = din("id8", (8, 8), F32)
    id128 = din("id128", (128, 128), F32)
    c4096 = din("c4096", (1, 64), F32)
    actconst = din("actconst", (128, 2), F32)

    outT = nc.dram_tensor("outT", [E, L], F32, kind="ExternalOutput")

    w1bf = nc.dram_tensor("w1bf", [NEXP, E, FF], BF16, kind="Internal")
    w2bf = nc.dram_tensor("w2bf", [NEXP, FF, E], BF16, kind="Internal")
    qphi_d = nc.dram_tensor("qphi_d", [4096, L], BF16, kind="Internal")

    eng_sems = {e: es.enter_context(nc.semaphore(f"s_{e}")) for e in ENGINES}
    dma_names = ["d_init", "d_wc", "d_mpl", "d_cs0", "d_cs1",
                 "d_xb0", "d_xb1", "d_x0", "d_x1", "d_qsp0", "d_qsp1",
                 "d_qh0", "d_qh1", "d_w1a", "d_w1b", "d_w2a", "d_w2b",
                 "d_out0", "d_out1"]
    dma_sems = {n: es.enter_context(nc.semaphore(n)) for n in dma_names}

    S = Sched()

    pb = [es.enter_context(nc.psum_tensor(f"pb{i}", [128, 512], F32)) for i in range(8)]
    PBK = [(f"pb{i}",) for i in range(8)]
    ones_c = es.enter_context(nc.sbuf_tensor("ones_c", [128, 1], BF16))
    ones_r = es.enter_context(nc.sbuf_tensor("ones_r", [1, 128], BF16))
    i4b = es.enter_context(nc.sbuf_tensor("i4b", [128, 4], BF16))
    i4q = es.enter_context(nc.sbuf_tensor("i4q", [4, 4 * 128], BF16))
    i8b = es.enter_context(nc.sbuf_tensor("i8b", [8, 8 * 128], BF16))
    id8_s = es.enter_context(nc.sbuf_tensor("id8_s", [8, 8], F32))
    id128_s = es.enter_context(nc.sbuf_tensor("id128_s", [128, 128], F32))
    Sbf = es.enter_context(nc.sbuf_tensor("Sbf", [128, NH, 130], BF16))
    acst = es.enter_context(nc.sbuf_tensor("acst", [128, 2], F32))

    def g_init(dst_ap, src_ap, tag, writes):
        S.op("gp", lambda G, d=dst_ap, s=src_ap: G.dma_start(d, s),
             writes=writes, dma="d_init", tag=tag)

    g_init(ones_c[:, :], onescol.ap(), "ld_onesc", [("ones_c",)])
    g_init(ones_r[:, :], onesrow.ap(), "ld_onesr", [("ones_r",)])
    g_init(i4b[:, :], ind4b.ap(), "ld_i4b", [("i4b",)])
    g_init(i4q[:, :], ind4q.ap(), "ld_i4q", [("i4q",)])
    g_init(i8b[:, :], ind8b.ap(), "ld_i8b", [("i8b",)])
    g_init(id8_s[:, :], id8.ap(), "ld_id8", [("id8",)])
    g_init(id128_s[:, :], id128.ap(), "ld_id128", [("id128",)])
    g_init(acst[:, :], actconst.ap(), "ld_acst", [("acst",)])

    es1 = contextlib.ExitStack()
    wqkv_s = es1.enter_context(nc.sbuf_tensor("wqkv_s", [128, 8, 3 * E], BF16))
    Mble = [es1.enter_context(nc.sbuf_tensor(f"Mble{g}", [128, 1024], BF16)) for g in range(4)]
    Mblo = [es1.enter_context(nc.sbuf_tensor(f"Mblo{g}", [128, 1024], BF16)) for g in range(4)]
    cos_t = [es1.enter_context(nc.sbuf_tensor(f"cos{p}", [128, T], BF16)) for p in range(2)]
    sin_t = [es1.enter_context(nc.sbuf_tensor(f"sin{p}", [128, T], BF16)) for p in range(2)]
    xbf = es1.enter_context(nc.sbuf_tensor("xbf", [128, 8, T], BF16))
    ln1 = es1.enter_context(nc.sbuf_tensor("ln1", [128, 8, T], BF16))
    qbf = es1.enter_context(nc.sbuf_tensor("qbf", [128, 8, T], BF16))
    kbf = es1.enter_context(nc.sbuf_tensor("kbf", [128, 8, T], BF16))
    ropeEk = es1.enter_context(nc.sbuf_tensor("ropeEk", [128, T], BF16))
    ropeOk = es1.enter_context(nc.sbuf_tensor("ropeOk", [128, T], BF16))
    ropeEq = es1.enter_context(nc.sbuf_tensor("ropeEq", [128, T], BF16))
    ropeOq = es1.enter_context(nc.sbuf_tensor("ropeOq", [128, T], BF16))
    sq_s = [es1.enter_context(nc.sbuf_tensor(f"sq{j}", [128, T], BF16)) for j in range(2)]
    xsq_s = [es1.enter_context(nc.sbuf_tensor(f"xsq{j}", [128, T], BF16)) for j in range(2)]
    tmp_bf = [es1.enter_context(nc.sbuf_tensor(f"tmpbf{j}", [128, T], BF16)) for j in range(2)]
    tmp_f = [es1.enter_context(nc.sbuf_tensor(f"tmpf{j}", [128, T], F32)) for j in range(2)]
    kphi_s = es1.enter_context(nc.sbuf_tensor("kphi_s", [128, TC, 1024], BF16))
    qstage = [es1.enter_context(nc.sbuf_tensor(f"qstage{j}", [128, 8, T], BF16)) for j in range(2)]
    vtok = es1.enter_context(nc.sbuf_tensor("vtok", [128, TC, NH, 65], BF16))
    xnk_s = es1.enter_context(nc.sbuf_tensor("xnk_s", [128, 4, 16], F32))
    xnq_s = es1.enter_context(nc.sbuf_tensor("xnq_s", [4, 4, T], BF16))
    row1 = es1.enter_context(nc.sbuf_tensor("row1", [1, 3 * 512], F32))
    rowb1 = es1.enter_context(nc.sbuf_tensor("rowb1", [1, 2 * 512], BF16))
    Sacc = es1.enter_context(nc.sbuf_tensor("Sacc", [128, NH, 130], F32))

    esM = contextlib.ExitStack()
    wm_s = esM.enter_context(nc.sbuf_tensor("wm_s", [64, 64], F32))
    hpT_s = esM.enter_context(nc.sbuf_tensor("hpT_s", [64, 64], F32))
    s1p_s = esM.enter_context(nc.sbuf_tensor("s1p_s", [64, 64], F32))
    s2_s = esM.enter_context(nc.sbuf_tensor("s2_s", [64, 64], F32))
    s3_s = esM.enter_context(nc.sbuf_tensor("s3_s", [64, 64], F32))
    gfl_s = esM.enter_context(nc.sbuf_tensor("gfl_s", [1, 1024], F32))
    gb_s = esM.enter_context(nc.sbuf_tensor("gb_s", [64, 1024], F32))
    mstage = esM.enter_context(nc.sbuf_tensor("mstage", [64, 256], F32))
    mtmp = [esM.enter_context(nc.sbuf_tensor(f"mtmp{j}", [64, 64], F32)) for j in range(3)]
    c4096_s = esM.enter_context(nc.sbuf_tensor("c4096_s", [1, 64], F32))

    g_init(wm_s[:, :], wm64.ap(), "ld_wm", [("wm",)])
    g_init(hpT_s[:, :], hpT.ap(), "ld_hpT", [("hpT",)])
    g_init(s1p_s[:, :], s1pT.ap(), "ld_s1p", [("s1p",)])
    g_init(s2_s[:, :], s2T.ap(), "ld_s2", [("s2",)])
    g_init(s3_s[:, :], s3T.ap(), "ld_s3", [("s3",)])
    g_init(c4096_s[:, :], c4096.ap(), "ld_c4096", [("c4096",)])
    if os.environ.get("KERNEL_WQKV", "1") == "1":
        g_init(wqkv_s[:, :, :], wqkvT.ap().rearrange("(c p) o -> p c o", p=128),
               "ld_wqkv", [("wqkv",)])

    S.op("dve", lambda V: V.memset(vtok[:, :, :, 64:65], 1.0), writes=[("vtok_ones",)], tag="ms_v")
    S.op("dve", lambda V: V.memset(Sacc[:, :, :], 0.0),
         writes=[("Sacc", h) for h in range(NH)], tag="ms_S")
    for g in range(4):
        S.op("dve", lambda V, g=g: V.memset(Mble[g][:, :], 0.0), writes=[("Mble0", g)], tag="ms_Me")
        S.op("dve", lambda V, g=g: V.memset(Mblo[g][:, :], 0.0), writes=[("Mblo0", g)], tag="ms_Mo")

    for b in range(4 if os.environ.get("KERNEL_MB", "1") == "1" else 0):
        S.op("gp", lambda G, b=b: G.dma_start(gfl_s[:, :], gflat.ap()[:, b * 1024:(b + 1) * 1024]),
             writes=[("gfl",)], dma="d_init", tag="ld_gfl")
        for j in range(2 if "g" in os.environ.get("KERNEL_MPARTS", "gmp") else 0):
            bank, bkey = pb[j], PBK[j]
            S.op("pe", lambda P, bk=bank, j=j: P.matmul(
                bk[0:64, :], c4096_s[0:1, 0:64], gfl_s[0:1, j * 512:(j + 1) * 512],
                start=True, stop=True),
                reads=[("c4096",), ("gfl",)], writes=[bkey], tag="gbmm")
            S.op("dve", lambda V, bk=bank, j=j: V.tensor_copy(
                gb_s[:, j * 512:(j + 1) * 512], bk[0:64, :]),
                reads=[bkey], writes=[("gb",)], tag="gbcp")
        MPARTS = os.environ.get("KERNEL_MPARTS", "gmp")
        for hh in range(4 if "m" in MPARTS else 0):
            h = b * 4 + hh
            g, hp = h // 4, h % 4
            for r in range(4):
                idx = 4 * h + r
                li = idx - b * 16
                pbank, pkey = pb[2 + (r % 2)], PBK[2 + (r % 2)]
                S.op("dve", lambda V, i=idx: V.tensor_scalar(
                    mtmp[0][:, :], wm_s[:, :], s3_s[:, i:i + 1], None, ALU.mult),
                    reads=[("wm",), ("s3",)], writes=[("mt0",)], tag="m_s3")
                S.op("dve", lambda V, li=li: V.tensor_tensor(
                    mtmp[1][:, :], mtmp[0][:, :], gb_s[:, li * 64:(li + 1) * 64], ALU.mult),
                    reads=[("mt0",), ("gb",)], writes=[("mt1",)], tag="m_g")
                S.op("pe", lambda P, bk=pbank: P.matmul(bk[0:64, 0:64], wm_s[:, :], mtmp[1][:, :],
                                                        start=True, stop=True),
                     reads=[("wm",), ("mt1",)], writes=[pkey], tag="m_mm1")
                S.op("dve", lambda V, bk=pbank, i=idx: V.tensor_scalar(
                    mtmp[2][:, :], bk[0:64, 0:64], s2_s[:, i:i + 1], None, ALU.mult),
                    reads=[pkey, ("s2",)], writes=[("mt2",)], tag="m_s2")
                S.op("pe", lambda P, bk=pbank: P.matmul(bk[0:64, 0:64], hpT_s[:, :], mtmp[2][:, :],
                                                        start=True, stop=True),
                     reads=[("hpT",), ("mt2",)], writes=[pkey], tag="m_mm2")
                S.op("dve", lambda V, bk=pbank, i=idx, r=r: V.tensor_scalar(
                    mstage[:, r * 64:(r + 1) * 64], bk[0:64, 0:64], s1p_s[:, i:i + 1],
                    None, ALU.mult),
                    reads=[pkey, ("s1p",)], writes=[("Mst", r)], tag="m_s1")
            if "p" not in MPARTS:
                continue
            rkeys = [("Mst", r) for r in range(4)]
            S.op("gp", lambda G, g=g, hp=hp: G.dma_start(
                Mble[g][32 * hp:32 * hp + 32, 256 * hp:256 * hp + 256],
                mstage[0:32, :]),
                reads=rkeys + [("Mble0", g)], writes=[("Mble", g, hp)], dma="d_mpl", tag="mple")
            S.op("gp", lambda G, g=g, hp=hp: G.dma_start(
                Mblo[g][32 * hp:32 * hp + 32, 256 * hp:256 * hp + 256],
                mstage[32:64, :]),
                reads=rkeys + [("Mblo0", g)], writes=[("Mblo", g, hp)], dma="d_mpl", tag="mplo")

    S.barrier()
    esM.close()

    MKE = [("Mble", g, hp) for g in range(4) for hp in range(4)]

    def MKO(g):
        return [("Mblo", g, hp) for hp in range(4)]

    def MKEg(g):
        return [("Mble", g, hp) for hp in range(4)]

    def emit_ln(xkey, xtile, lnkey, lntile, rowf, rowb, kp, tag):
        for c in range(8):
            S.op("act", lambda A, c=c: A.activation(xsq_s[c % 2][:, :], xtile[:, c, :], AF.Square),
                 reads=[xkey], writes=[(kp + "xsq", c % 2)], tag=f"{tag}_sq")
            S.op("pe", lambda P, c=c: P.matmul(
                pb[6][0:1, :], ones_c[:, :], xtile[:, c, :], start=(c == 0), stop=(c == 7)),
                reads=[xkey, ("ones_c",)], writes=[PBK[6][0]], tag=f"{tag}_sum")
            S.op("pe", lambda P, c=c: P.matmul(
                pb[7][0:1, :], ones_c[:, :], xsq_s[c % 2][:, :], start=(c == 0), stop=(c == 7)),
                reads=[(kp + "xsq", c % 2), ("ones_c",)], writes=[PBK[7][0]], tag=f"{tag}_ssq")
        S.op("dve", lambda V: V.tensor_scalar(rowb[0:1, 0:512], pb[6][0:1, :],
                                              1.0 / 1024.0, None, ALU.mult),
             reads=[PBK[6][0]], writes=[(kp + "meanb",)], tag=f"{tag}_mean")
        S.op("dve", lambda V: V.tensor_tensor(rowf[0:1, 0:512], rowb[0:1, 0:512],
                                              rowb[0:1, 0:512], ALU.mult),
             reads=[(kp + "meanb",)], writes=[(kp + "msq",)], tag=f"{tag}_msq")
        S.op("dve", lambda V: V.scalar_tensor_tensor(
            rowf[0:1, 512:1024], pb[7][0:1, :], 1.0 / 1024.0, rowf[0:1, 0:512],
            ALU.mult, ALU.subtract),
            reads=[PBK[7][0], (kp + "msq",)], writes=[(kp + "var",)], tag=f"{tag}_var")
        S.op("act", lambda A: A.activation(rowf[0:1, 1024:1536], rowf[0:1, 512:1024],
                                           AF.Ln, bias=acst[0:1, 0:1], scale=1.0),
             reads=[(kp + "var",), ("acst",)], writes=[(kp + "lnv",)], tag=f"{tag}_ln")
        S.op("act", lambda A: A.activation(rowb[0:1, 512:1024], rowf[0:1, 1024:1536],
                                           AF.Exp, bias=0.0, scale=acst[0:1, 1:2]),
             reads=[(kp + "lnv",), ("acst",)], writes=[(kp + "rstd",)], tag=f"{tag}_rstd")
        S.op("pe", lambda P: P.matmul(pb[6][:, :], ones_r[0:1, :], rowb[0:1, 0:512],
                                      start=True, stop=True),
             reads=[(kp + "meanb",), ("ones_r",)], writes=[PBK[6][0]], tag=f"{tag}_bm")
        S.op("pe", lambda P: P.matmul(pb[7][:, :], ones_r[0:1, :], rowb[0:1, 512:1024],
                                      start=True, stop=True),
             reads=[(kp + "rstd",), ("ones_r",)], writes=[PBK[7][0]], tag=f"{tag}_br")
        for c in range(8):
            S.op("dve", lambda V, c=c: V.tensor_tensor(
                tmp_f[c % 2][:, :], xtile[:, c, :], pb[6][:, :], ALU.subtract),
                reads=[xkey, PBK[6][0]], writes=[(kp + "tmp", c % 2)], tag=f"{tag}_sub")
            S.op("dve", lambda V, c=c: V.tensor_tensor(
                lntile[:, c, :], tmp_f[c % 2][:, :], pb[7][:, :], ALU.mult),
                reads=[(kp + "tmp", c % 2), PBK[7][0]], writes=[lnkey], tag=f"{tag}_mul")

    for i in range(NT_EMIT if P1_ON else 0):
        par = i % 2
        t0 = i * T
        S.op("gp", lambda G, t0=t0: G.dma_start(
            xbf[:, :, :],
            xT.ap().rearrange("(c p) t -> p c t", p=128)[:, :, t0:t0 + T]),
            writes=[("xbf",)], dma="d_xb0", tag="p1_x")
        S.op("sp", lambda Y, par=par, t0=t0: Y.dma_start(
            cos_t[par][:, :], costab.ap()[:, t0:t0 + T]),
            writes=[("cos", par)], dma=f"d_cs{par}", tag="p1_cos")
        S.op("sp", lambda Y, par=par, t0=t0: Y.dma_start(
            sin_t[par][:, :], sintab.ap()[:, t0:t0 + T]),
            writes=[("sin", par)], dma=f"d_cs{par}", tag="p1_sin")
        wc_list = [i] if NT_EMIT >= 16 else (list(range(16)) if i == 0 else [])
        for ci in wc_list:
            if ci < 8:
                S.op("gp", lambda G, e=ci: G.dma_start(w1bf.ap()[e], w1T.ap()[e]),
                     writes=[("w1bf", ci)], dma="d_wc", tag="wcast1")
            else:
                S.op("gp", lambda G, e=ci - 8: G.dma_start(w2bf.ap()[e], w2T.ap()[e]),
                     writes=[("w2bf", ci - 8)], dma="d_wc", tag="wcast2")

        emit_ln(("xbf",), xbf, ("ln1",), ln1, row1, rowb1, "l1", "ln1")

        for oc in range(16):
            bank, bkey = pb[oc % 2], PBK[oc % 2]
            dst = qbf if oc < 8 else kbf
            dkey = ("qbf", oc) if oc < 8 else ("kbf", oc - 8)
            for kc in range(8):
                S.op("pe", lambda P, b=bank, oc=oc, kc=kc: P.matmul(
                    b[:, :], wqkv_s[:, kc, oc * 128:(oc + 1) * 128], ln1[:, kc, :],
                    start=(kc == 0), stop=(kc == 7)),
                    reads=[("wqkv",), ("ln1",)], writes=[bkey], tag="qk_mm")
            S.op("act", lambda A, b=bank, d=dst, c=oc % 8: A.activation(
                d[:, c, :], b[:, :], AF.Identity),
                reads=[bkey], writes=[dkey], tag="qk_ev")
        for tc in range(TC):
            for vh in range(2):
                j = (tc * 2 + vh) % 2
                bank, bkey = pb[j], PBK[j]
                for kc in range(8):
                    S.op("pe", lambda P, b=bank, tc=tc, vh=vh, kc=kc: P.matmul(
                        b[:, :], ln1[:, kc, tc * 128:(tc + 1) * 128],
                        wqkv_s[:, kc, 2048 + vh * 512:2048 + (vh + 1) * 512],
                        start=(kc == 0), stop=(kc == 7)),
                        reads=[("wqkv",), ("ln1",)], writes=[bkey], tag="v_mm")
                S.op("act", lambda A, b=bank, tc=tc, vh=vh: A.activation(
                    vtok[:, tc, vh * 8:(vh + 1) * 8, 0:64],
                    b[:, :].rearrange("p (h d) -> p h d", d=64), AF.Identity),
                    reads=[bkey], writes=[("vtok", tc, vh)], tag="v_ev")

        for g in range(4):
            S.op("act", lambda A, g=g: A.activation(sq_s[0][:, :], kbf[:, 2 * g, :], AF.Square),
                 reads=[("kbf", 2 * g)], writes=[("sqE",)], tag="ksqE")
            S.op("act", lambda A, g=g: A.activation(sq_s[1][:, :], kbf[:, 2 * g + 1, :], AF.Square),
                 reads=[("kbf", 2 * g + 1)], writes=[("sqO",)], tag="ksqO")
            for tc in range(TC):
                S.op("pe", lambda P, tc=tc: P.matmul(
                    pb[2][:, tc * 4:tc * 4 + 4], sq_s[0][:, tc * 128:(tc + 1) * 128],
                    i4b[:, :], start=(tc == 0), stop=False),
                    reads=[("sqE",), ("i4b",)], writes=[PBK[2][0]], tag="xnk_mmE")
                S.op("pe", lambda P, tc=tc: P.matmul(
                    pb[2][:, tc * 4:tc * 4 + 4], sq_s[1][:, tc * 128:(tc + 1) * 128],
                    i4b[:, :], start=False, stop=(tc == 3)),
                    reads=[("sqO",), ("i4b",)], writes=[PBK[2][0]], tag="xnk_mmO")
            S.op("dve", lambda V, g=g: V.tensor_scalar(
                xnk_s[:, g, :], pb[2][:, 0:16], -0.5, -LN16, ALU.mult, ALU.add),
                reads=[PBK[2][0]], writes=[("xnk", g)], tag="xnk_ev")
            S.op("act", lambda A, g=g: A.activation(sq_s[0][:, :], qbf[:, 2 * g, :], AF.Square),
                 reads=[("qbf", 2 * g)], writes=[("sqE",)], tag="qsqE")
            S.op("act", lambda A, g=g: A.activation(sq_s[1][:, :], qbf[:, 2 * g + 1, :], AF.Square),
                 reads=[("qbf", 2 * g + 1)], writes=[("sqO",)], tag="qsqO")
            S.op("pe", lambda P: P.matmul(pb[2][0:4, :], i4b[:, :], sq_s[0][:, :],
                                          start=True, stop=False),
                 reads=[("sqE",), ("i4b",)], writes=[PBK[2][0]], tag="xnq_mmE")
            S.op("pe", lambda P: P.matmul(pb[2][0:4, :], i4b[:, :], sq_s[1][:, :],
                                          start=False, stop=True),
                 reads=[("sqO",), ("i4b",)], writes=[PBK[2][0]], tag="xnq_mmO")
            S.op("dve", lambda V, g=g: V.tensor_scalar(
                xnq_s[:, g, :], pb[2][0:4, :], -0.5, -LN16, ALU.mult, ALU.add),
                reads=[PBK[2][0]], writes=[("xnq", g)], tag="xnq_ev")

            for (src, k0, k1, dE, dO, dkE, dkO) in (
                (kbf, ("kbf", 2 * g), ("kbf", 2 * g + 1), ropeEk, ropeOk, ("rEk",), ("rOk",)),
                (qbf, ("qbf", 2 * g), ("qbf", 2 * g + 1), ropeEq, ropeOq, ("rEq",), ("rOq",)),
            ):
                S.op("dve", lambda V, s=src, g=g, par=par: V.tensor_tensor(
                    tmp_bf[0][:, :], s[:, 2 * g, :], cos_t[par][:, :], ALU.mult),
                    reads=[k0, ("cos", par)], writes=[("rt0",)], tag="rp1")
                S.op("dve", lambda V, s=src, g=g, par=par: V.tensor_tensor(
                    tmp_bf[1][:, :], s[:, 2 * g + 1, :], sin_t[par][:, :], ALU.mult),
                    reads=[k1, ("sin", par)], writes=[("rt1",)], tag="rp2")
                S.op("dve", lambda V, d=dE: V.tensor_tensor(
                    d[:, :], tmp_bf[0][:, :], tmp_bf[1][:, :], ALU.subtract),
                    reads=[("rt0",), ("rt1",)], writes=[dkE], tag="rp3")
                S.op("dve", lambda V, s=src, g=g, par=par: V.tensor_tensor(
                    tmp_bf[0][:, :], s[:, 2 * g, :], sin_t[par][:, :], ALU.mult),
                    reads=[k0, ("sin", par)], writes=[("rt0",)], tag="rp4")
                S.op("dve", lambda V, s=src, g=g, par=par: V.tensor_tensor(
                    tmp_bf[1][:, :], s[:, 2 * g + 1, :], cos_t[par][:, :], ALU.mult),
                    reads=[k1, ("cos", par)], writes=[("rt1",)], tag="rp5")
                S.op("dve", lambda V, d=dO: V.tensor_tensor(
                    d[:, :], tmp_bf[0][:, :], tmp_bf[1][:, :], ALU.add),
                    reads=[("rt0",), ("rt1",)], writes=[dkO], tag="rp6")

            for tc in range(TC):
                bA, bB = (pb[3], pb[4]) if tc % 2 == 0 else (pb[5], pb[6])
                kA, kB = (PBK[3][0], PBK[4][0]) if tc % 2 == 0 else (PBK[5][0], PBK[6][0])
                for half, (bank, bkey) in enumerate(((bA, kA), (bB, kB))):
                    S.op("pe", lambda P, b=bank, g=g, tc=tc, hf=half: P.matmul(
                        b[:, :], ropeEk[:, tc * 128:(tc + 1) * 128],
                        Mble[g][:, hf * 512:(hf + 1) * 512], start=True, stop=False),
                        reads=[("rEk",)] + MKEg(g), writes=[bkey], tag="phk_mmE")
                    S.op("pe", lambda P, b=bank, g=g, tc=tc, hf=half: P.matmul(
                        b[:, :], ropeOk[:, tc * 128:(tc + 1) * 128],
                        Mblo[g][:, hf * 512:(hf + 1) * 512], start=False, stop=True),
                        reads=[("rOk",)] + MKO(g), writes=[bkey], tag="phk_mmO")
                for hp in range(4):
                    bank, bkey = (bA, kA) if hp < 2 else (bB, kB)
                    off = (hp % 2) * 256
                    S.op("act", lambda A, b=bank, g=g, tc=tc, hp=hp, off=off: A.activation(
                        kphi_s[:, tc, hp * 256:(hp + 1) * 256], b[:, off:off + 256],
                        AF.Exp, bias=xnk_s[:, g, tc * 4 + hp:tc * 4 + hp + 1], scale=1.0),
                        reads=[bkey, ("xnk", g)], writes=[("kphi", tc, hp)], tag="phk_exp")
            for hp in range(4):
                h = 4 * g + hp
                n = 0
                for mc in range(2):
                    for tc in range(TC):
                        S.op("pe", lambda P, tc=tc, hp=hp, mc=mc, h=h,
                             first=(n == 0), last=(n == 7): P.matmul(
                            pb[7][:, mc * 65:mc * 65 + 65],
                            kphi_s[:, tc, hp * 256 + mc * 128:hp * 256 + (mc + 1) * 128],
                            vtok[:, tc, h, 0:65], start=first, stop=last),
                            reads=[("kphi", tc, hp), ("vtok", tc, h // 8), ("vtok_ones",)],
                            writes=[PBK[7][0]], tag="S_mm")
                        n += 1
                S.op("dve", lambda V, h=h: V.tensor_tensor(
                    Sacc[:, h, :], Sacc[:, h, :], pb[7][:, 0:130], ALU.add),
                    reads=[PBK[7][0], ("Sacc", h)], writes=[("Sacc", h)], tag="S_fl")

            qsp = g % 2
            for mc in range(8):
                bank, bkey = pb[mc % 2], PBK[mc % 2]
                hp = mc // 2
                S.op("pe", lambda P, b=bank, g=g, mc=mc: P.matmul(
                    b[:, :], Mble[g][:, mc * 128:(mc + 1) * 128], ropeEq[:, :],
                    start=True, stop=False),
                    reads=[("rEq",)] + MKEg(g), writes=[bkey], tag="phq_mmE")
                S.op("pe", lambda P, b=bank, g=g, mc=mc: P.matmul(
                    b[:, :], Mblo[g][:, mc * 128:(mc + 1) * 128], ropeOq[:, :],
                    start=False, stop=False),
                    reads=[("rOq",)] + MKO(g), writes=[bkey], tag="phq_mmO")
                S.op("pe", lambda P, b=bank, g=g, hp=hp: P.matmul(
                    b[:, :], i4q[:, hp * 128:(hp + 1) * 128], xnq_s[:, g, :],
                    start=False, stop=True),
                    reads=[("xnq", g), ("i4q",)], writes=[bkey], tag="phq_r1")
                S.op("act", lambda A, b=bank, qsp=qsp, mc=mc: A.activation(
                    qstage[qsp][:, mc, :], b[:, :], AF.Exp),
                    reads=[bkey], writes=[("qst", qsp, mc)], tag="phq_exp")
            S.op("gp", lambda G, qsp=qsp, g=g, t0=t0: G.dma_start(
                qphi_d.ap().rearrange("(r p) t -> p r t", p=128)[:, 8 * g:8 * g + 8, t0:t0 + T],
                qstage[qsp][:, :, :]),
                reads=[("qst", qsp, mc) for mc in range(8)], dma=f"d_qsp{qsp}", tag="q_spill")

    S.op("dve", lambda V: V.tensor_copy(Sbf[:, :, :], Sacc[:, :, :]),
         reads=[("Sacc", h) for h in range(NH)], writes=[("Sbf",)], tag="S_cast")

    S.barrier()
    es1.close()

    es2 = contextlib.ExitStack()
    wout_s = es2.enter_context(nc.sbuf_tensor("wout_s", [128, 8, E], BF16))
    b1s = es2.enter_context(nc.sbuf_tensor("b1s", [128, NEXP, 32], F32))
    b2s = es2.enter_context(nc.sbuf_tensor("b2s", [128, NEXP, 8], F32))
    rtr_s = es2.enter_context(nc.sbuf_tensor("rtr_s", [128, 8, 8], BF16))
    x_s = es2.enter_context(nc.sbuf_tensor("x_s", [128, 8, T], F32))
    qph = [es2.enter_context(nc.sbuf_tensor(f"qph{p}", [128, 8, T], BF16)) for p in range(2)]
    ln2 = es2.enter_context(nc.sbuf_tensor("ln2", [128, 8, T], BF16))
    att = es2.enter_context(nc.sbuf_tensor("att", [128, 8, T], BF16))
    acc = es2.enter_context(nc.sbuf_tensor("acc", [128, 8, T], F32))
    gsb = es2.enter_context(nc.sbuf_tensor("gsb", [128, 32, T], BF16))
    w1s = [es2.enter_context(nc.sbuf_tensor(f"w1s{p}", [128, 8, T], BF16)) for p in range(2)]
    w2s = [es2.enter_context(nc.sbuf_tensor(f"w2s{p}", [128, 32, 128], BF16)) for p in range(2)]
    gateb = es2.enter_context(nc.sbuf_tensor("gateb", [128, 8, T], BF16))
    gfm = es2.enter_context(nc.sbuf_tensor("gfm", [8, T], BF16))
    rfm = es2.enter_context(nc.sbuf_tensor("rfm", [8, T], F32))
    rtk = es2.enter_context(nc.sbuf_tensor("rtk", [128, 8], F32))
    etk = es2.enter_context(nc.sbuf_tensor("etk", [128, 8], F32))
    gtk = es2.enter_context(nc.sbuf_tensor("gtk", [128, 8], F32))
    tscr = es2.enter_context(nc.sbuf_tensor("tscr", [128, 8], F32))
    nm_s = [es2.enter_context(nc.sbuf_tensor(f"nm{j}", [64, T], BF16)) for j in range(2)]
    dn_s = es2.enter_context(nc.sbuf_tensor("dn_s", [1, 2, 512], F32))
    rc_s = es2.enter_context(nc.sbuf_tensor("rc_s", [1, 2, 512], BF16))
    tmp2f = [es2.enter_context(nc.sbuf_tensor(f"tmp2f{j}", [128, T], F32)) for j in range(2)]
    row2 = es2.enter_context(nc.sbuf_tensor("row2", [1, 3 * 512], F32))
    rowb2 = es2.enter_context(nc.sbuf_tensor("rowb2", [1, 2 * 512], BF16))
    ysq2 = [es2.enter_context(nc.sbuf_tensor(f"ysq2{j}", [128, T], BF16)) for j in range(2)]
    ybf2 = [es2.enter_context(nc.sbuf_tensor(f"ybf2{j}", [128, T], BF16)) for j in range(2)]

    S.op("gp", lambda G: G.dma_start(wout_s[:, :, :],
                                     woutT.ap().rearrange("(c p) o -> p c o", p=128)),
         writes=[("wout",)], dma="d_init", tag="ld_wout")
    S.op("gp", lambda G: G.dma_start(b1s[:, :, :], b1T.ap().rearrange("e p c -> p e c")),
         writes=[("b1",)], dma="d_init", tag="ld_b1")
    S.op("gp", lambda G: G.dma_start(b2s[:, :, :], b2T.ap().rearrange("e p c -> p e c")),
         writes=[("b2",)], dma="d_init", tag="ld_b2")
    S.op("gp", lambda G: G.dma_start(rtr_s[:, :, :],
                                     routerT.ap().rearrange("(c p) e -> p c e", p=128)),
         writes=[("rtr",)], dma="d_init", tag="ld_rtr")

    def emit_ln2():
        kp = "l2"
        for c in range(8):
            S.op("act", lambda A, c=c: A.activation(
                ybf2[c % 2][:, :], x_s[:, c, :], AF.Identity),
                reads=[("y",)], writes=[(kp + "ybf", c % 2)], tag="l2_cp")
            S.op("act", lambda A, c=c: A.activation(
                ysq2[c % 2][:, :], ybf2[c % 2][:, :], AF.Square),
                reads=[(kp + "ybf", c % 2)], writes=[(kp + "ysq", c % 2)], tag="l2_sq")
            S.op("pe", lambda P, c=c: P.matmul(
                pb[6][0:1, :], ones_c[:, :], ybf2[c % 2][:, :], start=(c == 0), stop=(c == 7)),
                reads=[(kp + "ybf", c % 2), ("ones_c",)], writes=[PBK[6][0]], tag="l2_sum")
            S.op("pe", lambda P, c=c: P.matmul(
                pb[7][0:1, :], ones_c[:, :], ysq2[c % 2][:, :], start=(c == 0), stop=(c == 7)),
                reads=[(kp + "ysq", c % 2), ("ones_c",)], writes=[PBK[7][0]], tag="l2_ssq")
        S.op("dve", lambda V: V.tensor_scalar(rowb2[0:1, 0:512], pb[6][0:1, :],
                                              1.0 / 1024.0, None, ALU.mult),
             reads=[PBK[6][0]], writes=[(kp + "meanb",)], tag="l2_mean")
        S.op("dve", lambda V: V.tensor_tensor(row2[0:1, 0:512], rowb2[0:1, 0:512],
                                              rowb2[0:1, 0:512], ALU.mult),
             reads=[(kp + "meanb",)], writes=[(kp + "msq",)], tag="l2_msq")
        S.op("dve", lambda V: V.scalar_tensor_tensor(
            row2[0:1, 512:1024], pb[7][0:1, :], 1.0 / 1024.0, row2[0:1, 0:512],
            ALU.mult, ALU.subtract),
            reads=[PBK[7][0], (kp + "msq",)], writes=[(kp + "var",)], tag="l2_var")
        S.op("act", lambda A: A.activation(row2[0:1, 1024:1536], row2[0:1, 512:1024],
                                           AF.Ln, bias=acst[0:1, 0:1], scale=1.0),
             reads=[(kp + "var",), ("acst",)], writes=[(kp + "lnv",)], tag="l2_ln")
        S.op("act", lambda A: A.activation(rowb2[0:1, 512:1024], row2[0:1, 1024:1536],
                                           AF.Exp, bias=0.0, scale=acst[0:1, 1:2]),
             reads=[(kp + "lnv",), ("acst",)], writes=[(kp + "rstd",)], tag="l2_rstd")
        S.op("pe", lambda P: P.matmul(pb[6][:, :], ones_r[0:1, :], rowb2[0:1, 0:512],
                                      start=True, stop=True),
             reads=[(kp + "meanb",), ("ones_r",)], writes=[PBK[6][0]], tag="l2_bm")
        S.op("pe", lambda P: P.matmul(pb[7][:, :], ones_r[0:1, :], rowb2[0:1, 512:1024],
                                      start=True, stop=True),
             reads=[(kp + "rstd",), ("ones_r",)], writes=[PBK[7][0]], tag="l2_br")
        for c in range(8):
            S.op("dve", lambda V, c=c: V.tensor_tensor(
                tmp2f[c % 2][:, :], x_s[:, c, :], pb[6][:, :], ALU.subtract),
                reads=[("y",), PBK[6][0]], writes=[("o2t", c % 2)], tag="l2_sub")
            S.op("dve", lambda V, c=c: V.tensor_tensor(
                ln2[:, c, :], tmp2f[c % 2][:, :], pb[7][:, :], ALU.mult),
                reads=[("o2t", c % 2), PBK[7][0]], writes=[("ln2",)], tag="l2_mul")

    for i in range(NT_EMIT if P2_ON else 0):
        t0 = i * T
        S.op("gp", lambda G, t0=t0: G.dma_start(
            x_s[:, :, :],
            xT.ap().rearrange("(c p) t -> p c t", p=128)[:, :, t0:t0 + T]),
            writes=[("x",), ("y",)], dma="d_x0", tag="p2_x")
        S.op("sp", lambda Y, t0=t0: Y.dma_start(
            qph[0][:, :, :],
            qphi_d.ap().rearrange("(r p) t -> p r t", p=128)[:, 0:8, t0:t0 + T]),
            writes=[("qph", 0)], dma="d_qh0", tag="p2_qphi")

        for h in range(NH):
            qtr, j = h // 4, h % 2
            qb = qtr % 2
            if h % 4 == 0 and qtr + 1 < 4:
                qb2 = (qtr + 1) % 2
                S.op("sp", lambda Y, q=qtr + 1, qb2=qb2, t0=t0: Y.dma_start(
                    qph[qb2][:, :, :],
                    qphi_d.ap().rearrange("(r p) t -> p r t", p=128)[:, 8 * q:8 * q + 8, t0:t0 + T]),
                    writes=[("qph", qb2)], dma=f"d_qh{qb2}", tag="p2_qphi")
            q0 = (h % 4) * 2
            bA, kA = pb[j], PBK[j][0]
            bB, kB = pb[2 + j], PBK[2 + j][0]
            for mc in range(2):
                S.op("pe", lambda P, b=bA, h=h, qb=qb, q0=q0, mc=mc: P.matmul(
                    b[0:65, :], Sbf[:, h, mc * 65:(mc + 1) * 65], qph[qb][:, q0 + mc, :],
                    start=(mc == 0), stop=(mc == 1)),
                    reads=[("Sbf",), ("qph", qb)], writes=[kA], tag="nm_mm")
            S.op("dve", lambda V, b=bA, j=j: V.tensor_scalar(
                dn_s[0:1, j, :], b[64:65, :], EPS, None, ALU.max),
                reads=[kA], writes=[("dn", j)], tag="nm_max")
            S.op("dve", lambda V, j=j: V.reciprocal(rc_s[0:1, j, :], dn_s[0:1, j, :]),
                 reads=[("dn", j)], writes=[("rc", j)], tag="nm_rc")
            S.op("pe", lambda P, b=bB, j=j: P.matmul(
                b[0:64, :], ones_r[0:1, 0:64], rc_s[0:1, j, :], start=True, stop=True),
                reads=[("rc", j), ("ones_r",)], writes=[kB], tag="nm_bc")
            S.op("act", lambda A, b=bA, j=j: A.activation(
                nm_s[j][:, :], b[0:64, :], AF.Identity),
                reads=[kA], writes=[("nm", j)], tag="nm_ev")
            S.op("dve", lambda V, b=bB, h=h, j=j: V.tensor_tensor(
                att[j * 64:j * 64 + 64, h // 2, :], nm_s[j][:, :], b[0:64, :], ALU.mult),
                reads=[("nm", j), kB], writes=[("att", h)], tag="nm_mul")

        for oc in range(8):
            bank, bkey = pb[4 + oc % 2], PBK[4 + oc % 2][0]
            for kc in range(8):
                S.op("pe", lambda P, b=bank, oc=oc, kc=kc: P.matmul(
                    b[:, :], wout_s[:, kc, oc * 128:(oc + 1) * 128], att[:, kc, :],
                    start=(kc == 0), stop=(kc == 7)),
                    reads=[("wout",), ("att", kc * 2), ("att", kc * 2 + 1)],
                    writes=[bkey], tag="op_mm")
            S.op("dve", lambda V, b=bank, oc=oc: V.tensor_tensor(
                x_s[:, oc, :], x_s[:, oc, :], b[:, :], ALU.add),
                reads=[("x",), bkey], writes=[("y",)], tag="op_res")

        emit_ln2()

        for kc in range(8):
            S.op("pe", lambda P, kc=kc: P.matmul(
                pb[0][0:8, :], rtr_s[:, kc, :], ln2[:, kc, :], start=(kc == 0), stop=(kc == 7)),
                reads=[("rtr",), ("ln2",)], writes=[PBK[0][0]], tag="rt_mm")
        S.op("dve", lambda V: V.tensor_copy(rfm[:, :], pb[0][0:8, :]),
             reads=[PBK[0][0]], writes=[("rfm",)], tag="rt_ev")
        for tc in range(TC):
            S.op("pe", lambda P, tc=tc: P.transpose(
                pb[1][0:128, 0:8], rfm[:, tc * 128:(tc + 1) * 128], id8_s[:, :]),
                reads=[("rfm",), ("id8",)], writes=[PBK[1][0]], tag="rt_tr")
            S.op("dve", lambda V: V.tensor_copy(rtk[:, :], pb[1][0:128, 0:8]),
                 reads=[PBK[1][0]], writes=[("rtk",)], tag="tk_cp")
            S.op("dve", lambda V: V.reduce_max(tscr[:, 0:1], rtk[:, :], axis=AX),
                 reads=[("rtk",)], writes=[("tscr",)], tag="tk_m1")
            S.op("dve", lambda V: V.tensor_scalar(
                gtk[:, :], rtk[:, :], tscr[:, 0:1], None, ALU.is_equal),
                reads=[("rtk",), ("tscr",)], writes=[("gtk",)], tag="tk_eq1")
            S.op("dve", lambda V: V.scalar_tensor_tensor(
                rtk[:, :], gtk[:, :], -1e30, rtk[:, :], ALU.mult, ALU.add),
                reads=[("gtk",), ("rtk",)], writes=[("rtk",)], tag="tk_rm")
            S.op("dve", lambda V: V.reduce_max(tscr[:, 1:2], rtk[:, :], axis=AX),
                 reads=[("rtk",)], writes=[("tscr",)], tag="tk_m2")
            S.op("dve", lambda V: V.tensor_tensor(
                tscr[:, 2:3], tscr[:, 1:2], tscr[:, 0:1], ALU.subtract),
                reads=[("tscr",)], writes=[("tscr",)], tag="tk_d21")
            S.op("act", lambda A: A.activation(tscr[:, 3:4], tscr[:, 2:3], AF.Exp),
                 reads=[("tscr",)], writes=[("tscr",)], tag="tk_e21")
            S.op("dve", lambda V: V.tensor_scalar(
                tscr[:, 4:5], tscr[:, 3:4], 1.0, None, ALU.add),
                reads=[("tscr",)], writes=[("tscr",)], tag="tk_e21p")
            S.op("dve", lambda V: V.reciprocal(tscr[:, 5:6], tscr[:, 4:5]),
                 reads=[("tscr",)], writes=[("tscr",)], tag="tk_w1")
            S.op("dve", lambda V: V.tensor_tensor(
                tscr[:, 6:7], tscr[:, 3:4], tscr[:, 5:6], ALU.mult),
                reads=[("tscr",)], writes=[("tscr",)], tag="tk_w2")
            S.op("dve", lambda V: V.tensor_scalar(
                etk[:, :], rtk[:, :], tscr[:, 1:2], None, ALU.is_equal),
                reads=[("rtk",), ("tscr",)], writes=[("etk",)], tag="tk_eq2")
            S.op("dve", lambda V: V.tensor_scalar(
                gtk[:, :], gtk[:, :], tscr[:, 5:6], None, ALU.mult),
                reads=[("gtk",), ("tscr",)], writes=[("gtk",)], tag="tk_gw1")
            S.op("dve", lambda V: V.scalar_tensor_tensor(
                gtk[:, :], etk[:, :], tscr[:, 6:7], gtk[:, :], ALU.mult, ALU.add),
                reads=[("etk",), ("tscr",), ("gtk",)], writes=[("gtk",)], tag="tk_gate")
            S.op("pe", lambda P, tc=tc: P.transpose(
                pb[1][0:8, 0:128], gtk[:, :], id128_s[:, :]),
                reads=[("gtk",), ("id128",)], writes=[PBK[1][0]], tag="tk_tr")
            S.op("dve", lambda V, tc=tc: V.tensor_copy(gfm[:, tc * 128:(tc + 1) * 128],
                                                       pb[1][0:8, 0:128]),
                 reads=[PBK[1][0]], writes=[("gfm",)], tag="tk_gfm")
        for e in range(NEXP):
            bank, bkey = pb[4 + e % 2], PBK[4 + e % 2][0]
            S.op("pe", lambda P, b=bank, e=e: P.matmul(
                b[:, :], i8b[:, e * 128:(e + 1) * 128], gfm[:, :], start=True, stop=True),
                reads=[("gfm",), ("i8b",)], writes=[bkey], tag="gb_mm")
            S.op("act", lambda A, b=bank, e=e: A.activation(
                gateb[:, e, :], b[:, :], AF.Identity),
                reads=[bkey], writes=[("gateb", e)], tag="gb_ev")

        for e in range(NEXP):
            for ft in range(8):
                wp = (e * 8 + ft) % 2
                S.op("sp", lambda Y, e=e, ft=ft, wp=wp: Y.dma_start(
                    w1s[wp][:, :, :],
                    w1bf.ap()[e].rearrange("(c p) f -> p c f", p=128)[:, :, ft * 512:(ft + 1) * 512]),
                    reads=[("w1bf", e)], writes=[("w1s", wp)], dma=f"d_w1{'ab'[wp]}", tag="w1_ld")
                for fcl in range(4):
                    bank, bkey = pb[fcl % 2], PBK[fcl % 2][0]
                    fc = ft * 4 + fcl
                    for kc in range(8):
                        S.op("pe", lambda P, b=bank, wp=wp, fcl=fcl, kc=kc: P.matmul(
                            b[:, :], w1s[wp][:, kc, fcl * 128:(fcl + 1) * 128], ln2[:, kc, :],
                            start=(kc == 0), stop=(kc == 7)),
                            reads=[("w1s", wp), ("ln2",)], writes=[bkey], tag="h1_mm")
                    S.op("act", lambda A, b=bank, e=e, fc=fc: A.activation(
                        gsb[:, fc, :], b[:, :], AF.Gelu, bias=b1s[:, e, fc:fc + 1], scale=1.0),
                        reads=[bkey, ("b1",)], writes=[("gsb", fc)], tag="h1_gelu")
            for hc in range(8):
                wp = (e * 8 + hc) % 2
                S.op("sp", lambda Y, e=e, hc=hc, wp=wp: Y.dma_start(
                    w2s[wp][:, :, :],
                    w2bf.ap()[e].rearrange("(c p) o -> p c o", p=128)[:, :, hc * 128:(hc + 1) * 128]),
                    reads=[("w2bf", e)], writes=[("w2s", wp)], dma=f"d_w2{'ab'[wp]}", tag="w2_ld")
                bank, bkey = pb[2 + hc % 2], PBK[2 + hc % 2][0]
                for fc in range(32):
                    S.op("pe", lambda P, b=bank, wp=wp, fc=fc: P.matmul(
                        b[:, :], w2s[wp][:, fc, :], gsb[:, fc, :],
                        start=(fc == 0), stop=(fc == 31)),
                        reads=[("w2s", wp), ("gsb", fc)], writes=[bkey], tag="o2_mm")
                if e == 0:
                    S.op("dve", lambda V, b=bank, e=e, hc=hc: V.scalar_tensor_tensor(
                        acc[:, hc, :], b[:, :], b2s[:, e, hc:hc + 1], gateb[:, e, :],
                        ALU.add, ALU.mult),
                        reads=[bkey, ("b2",), ("gateb", e)], writes=[("acc", hc)], tag="o2_ev0")
                else:
                    j = hc % 2
                    S.op("dve", lambda V, b=bank, e=e, hc=hc, j=j: V.scalar_tensor_tensor(
                        tmp2f[j][:, :], b[:, :], b2s[:, e, hc:hc + 1], gateb[:, e, :],
                        ALU.add, ALU.mult),
                        reads=[bkey, ("b2",), ("gateb", e)], writes=[("o2t", j)], tag="o2_ev")
                    S.op("dve", lambda V, hc=hc, j=j: V.tensor_tensor(
                        acc[:, hc, :], acc[:, hc, :], tmp2f[j][:, :], ALU.add),
                        reads=[("o2t", j), ("acc", hc)], writes=[("acc", hc)], tag="o2_acc")

        for oc in range(8):
            S.op("dve", lambda V, oc=oc: V.tensor_tensor(
                x_s[:, oc, :], x_s[:, oc, :], acc[:, oc, :], ALU.add),
                reads=[("y",), ("acc", oc)], writes=[("y",)], tag="fin_add")
        S.op("gp", lambda G, t0=t0: G.dma_start(
            outT.ap().rearrange("(c p) t -> p c t", p=128)[:, :, t0:t0 + T],
            x_s[:, :, :]),
            reads=[("y",)], dma="d_out0", tag="p2_out")

    final_waits = {"d_out0": 16 * S.dma_count.get("d_out0", 0)}
    final_waits = {k: v for k, v in final_waits.items() if v}
    run_engine = S.finalize(eng_sems, dma_sems, final_waits)

    with nc.Block() as block, nc.allow_low_precision(
            reason="bf16 matmul operand staging; fp32 accumulation everywhere"):
        block.tensor(lambda eng: run_engine(eng, "pe"))
        block.scalar(lambda eng: run_engine(eng, "act"))
        block.vector(lambda eng: run_engine(eng, "dve"))
        block.sync(lambda eng: run_engine(eng, "sp"))
        block.gpsimd(lambda eng: run_engine(eng, "gp"))

    es2.close()
    es.close()
    return nc


def _host_constants():
    def fwht(v):
        shape = v.shape
        Pn = shape[-1]
        h = 1
        while h < Pn:
            v = v.reshape(shape[:-1] + (-1, 2 * h))
            a, b = v[..., :h], v[..., h:]
            v = np.concatenate([a + b, a - b], axis=-1).reshape(shape)
            h *= 2
        return v

    W64 = fwht(np.eye(64, dtype=np.float64))
    rowperm = np.concatenate([np.arange(0, 64, 2), np.arange(1, 64, 2)])
    perm = np.zeros(1024, dtype=np.int64)
    n = 0
    for g in range(4):
        for parity in range(2):
            for hp in range(4):
                for j in range(32):
                    perm[n] = (4 * g + hp) * 64 + 2 * j + parity
                    n += 1
    freqs = 10000.0 ** (-np.arange(0, 64, 2) / 64.0)
    tpos = np.arange(L)
    ang = np.where(tpos[None, :] >= 1,
                   (tpos[None, :] - 1) * freqs[np.arange(128) % 32, None], 0.0)
    cos_t = np.cos(ang)
    sin_t = np.sin(ang)
    cos_t[:, 0] = 1.0
    sin_t[:, 0] = 0.0

    i4b = np.zeros((128, 4), dtype=np.float32)
    for d in range(128):
        i4b[d, d // 32] = 1.0
    i4q = np.zeros((4, 4 * 128), dtype=np.float32)
    for hp in range(4):
        i4q[hp, hp * 128:(hp + 1) * 128] = 1.0
    i8b = np.zeros((8, 8 * 128), dtype=np.float32)
    for e in range(8):
        i8b[e, e * 128:(e + 1) * 128] = 1.0

    return {
        "W64": W64, "rowperm": rowperm, "perm": perm,
        "costab": cos_t.astype(BF), "sintab": sin_t.astype(BF),
        "wm64": W64.astype(np.float32),
        "hpT": np.ascontiguousarray(W64[rowperm, :].T).astype(np.float32),
        "onescol": np.ones((128, 1), dtype=BF),
        "onesrow": np.ones((1, 128), dtype=BF),
        "ind4b": i4b.astype(BF), "ind4q": i4q.astype(BF), "ind8b": i8b.astype(BF),
        "id8": np.eye(8, dtype=np.float32), "id128": np.eye(128, dtype=np.float32),
        "c4096": np.full((1, 64), 1.0 / 4096.0, dtype=np.float32),
        "actconst": np.tile(np.array([[LNEPS, -0.5]], dtype=np.float32), (128, 1)),
    }


_CACHE = {}


def kernel(L0_attn_x, L0_attn_w_qkv, L0_attn_w_out, L0_attn_sorf_signs,
           L0_attn_sorf_scales, L0_ffn_router_w, L0_ffn_w1, L0_ffn_b1,
           L0_ffn_w2, L0_ffn_b2):
    x = np.ascontiguousarray(np.asarray(L0_attn_x, dtype=np.float32))
    w_qkv = np.asarray(L0_attn_w_qkv, dtype=np.float32)
    w_out = np.asarray(L0_attn_w_out, dtype=np.float32)
    signs = np.asarray(L0_attn_sorf_signs, dtype=np.float32)
    scales = np.asarray(L0_attn_sorf_scales, dtype=np.float32)
    router_w = np.asarray(L0_ffn_router_w, dtype=np.float32)
    w1 = np.asarray(L0_ffn_w1, dtype=np.float32)
    b1 = np.asarray(L0_ffn_b1, dtype=np.float32)
    w2 = np.asarray(L0_ffn_w2, dtype=np.float32)
    b2 = np.asarray(L0_ffn_b2, dtype=np.float32)

    cst = _host_constants()
    perm = cst["perm"]

    wqkvT = np.ascontiguousarray(w_qkv.T)
    wq = wqkvT[:, 0:1024][:, perm]
    wk = wqkvT[:, 1024:2048][:, perm]
    wv = wqkvT[:, 2048:3072]
    WQKVT = np.ascontiguousarray(np.concatenate([wq, wk, wv], axis=1))

    shared = {
        "wqkvT": WQKVT,
        "woutT": np.ascontiguousarray(w_out.T),
        "w1T": np.ascontiguousarray(w1.transpose(0, 2, 1)),
        "w2T": np.ascontiguousarray(w2.transpose(0, 2, 1)),
        "b1T": np.ascontiguousarray(b1.reshape(8, 32, 128).transpose(0, 2, 1)),
        "b2T": np.ascontiguousarray(b2.reshape(8, 8, 128).transpose(0, 2, 1)),
        "routerT": np.ascontiguousarray(router_w.T),
        "costab": cst["costab"], "sintab": cst["sintab"],
        "wm64": cst["wm64"], "hpT": cst["hpT"],
        "s1pT": np.ascontiguousarray(
            signs[:, :, 0, :][..., cst["rowperm"]].reshape(64, 64).T).astype(np.float32),
        "s2T": np.ascontiguousarray(signs[:, :, 1, :].reshape(64, 64).T).astype(np.float32),
        "s3T": np.ascontiguousarray(signs[:, :, 2, :].reshape(64, 64).T).astype(np.float32),
        "gflat": np.ascontiguousarray(scales.reshape(1, 64 * 64)).astype(np.float32),
        "onescol": cst["onescol"], "onesrow": cst["onesrow"],
        "ind4b": cst["ind4b"], "ind4q": cst["ind4q"], "ind8b": cst["ind8b"],
        "id8": cst["id8"], "id128": cst["id128"], "c4096": cst["c4096"],
        "actconst": cst["actconst"],
    }

    if "nc" not in _CACHE:
        _CACHE["nc"] = build_program()
    nc = _CACHE["nc"]

    in_maps = []
    for c in range(C):
        m = dict(shared)
        m["xT"] = np.ascontiguousarray(x[c].T)
        in_maps.append(m)

    res = run_bass_kernel_spmd(nc, in_maps, core_ids=list(range(C)))
    out = np.empty((C, L, E), dtype=np.float32)
    for c in range(C):
        out[c] = res.results[c]["outT"].T
    return out


# version 03
# speedup vs baseline: 2.4676x; 2.4676x over previous
import contextlib
import numpy as np
import ml_dtypes

import concourse.bass as bass
import concourse.mybir as mybir
from concourse.bass_utils import run_bass_kernel_spmd
from concourse.alu_op_type import AluOpType as ALU

F32 = mybir.dt.float32
BF16 = mybir.dt.bfloat16
AF = mybir.ActivationFunctionType
AX = mybir.AxisListType.X
BF = ml_dtypes.bfloat16

import os
C, L, E = 8, 8192, 1024
NH, HD = 16, 64
T = 512
NT = L // T
NT_EMIT = int(os.environ.get("KERNEL_NT", str(NT)))
P1_ON = os.environ.get("KERNEL_P1", "1") == "1"
P2_ON = os.environ.get("KERNEL_P2", "1") == "1"
FFN_ONLY = os.environ.get("KERNEL_FFN_ONLY", "0") == "1"
FFN_NODMA = os.environ.get("KERNEL_FFN_NODMA", "0") == "1"
TC = T // 128
NEXP, FF = 8, 4096
LN16 = float(np.log(16.0))
EPS = 1e-6
LNEPS = 1e-5

ENGINES = ("pe", "act", "dve", "pool", "sp", "gp")


class _Op:
    __slots__ = ("eng", "fn", "dma", "dma_ord", "inc", "pos", "deps", "dma_waits",
                 "tag", "_bar_dma")

    def __init__(self, eng, fn, dma, pos, tag):
        self.eng = eng
        self.fn = fn
        self.dma = dma
        self.dma_ord = 0
        self.inc = False
        self.pos = pos
        self.deps = []
        self.dma_waits = {}
        self.tag = tag
        self._bar_dma = None


class Sched:

    def __init__(self):
        self.ops = {e: [] for e in ENGINES}
        self.reg_w = {}
        self.reg_r = {}
        self.dma_count = {}
        self.bar_pending = {e: None for e in ENGINES}

    def op(self, eng, fn, reads=(), writes=(), dma=None, tag=""):
        o = _Op(eng, fn, dma, len(self.ops[eng]), tag)
        bp = self.bar_pending.get(eng)
        if bp is not None:
            prods, dma_snap = bp
            for p in prods:
                if p.eng != eng:
                    o.deps.append(p)
            o._bar_dma = dma_snap
            self.bar_pending[eng] = None

        def add(p):
            if p is None or p is o:
                return
            if p.eng == "pe" and eng == "pe" and p.dma is None and dma is None:
                return
            if p.dma is not None:
                o.dma_waits[p.dma] = max(o.dma_waits.get(p.dma, 0),
                                         16 * self.dma_count.get(p.dma, 0))
            else:
                o.deps.append(p)
                p.inc = True

        for k in reads:
            add(self.reg_w.get(k))
        for k in writes:
            add(self.reg_w.get(k))
            rd = self.reg_r.get(k)
            if rd:
                for r in rd.values():
                    add(r)
        for k in writes:
            self.reg_w[k] = o
            self.reg_r[k] = {}
        for k in reads:
            self.reg_r.setdefault(k, {})[eng] = o
        if dma is not None:
            self.dma_count[dma] = self.dma_count.get(dma, 0) + 1
            o.dma_ord = self.dma_count[dma]
        self.ops[eng].append(o)
        return o

    def barrier(self):
        prods = []
        for e in ENGINES:
            for o in reversed(self.ops[e]):
                if o.dma is None:
                    o.inc = True
                    prods.append(o)
                    break
        dma_snap = dict(self.dma_count)
        for e in ENGINES:
            self.bar_pending[e] = (prods, dma_snap)

    def finalize(self, eng_sems, dma_sems, final_dma_waits):
        val = {}
        for e in ENGINES:
            cnt = 0
            for o in self.ops[e]:
                if o.inc:
                    cnt += 1
                val[o] = cnt

        def run_engine(engobj, e):
            marks = {}
            for o in self.ops[e]:
                waits = {}
                if o._bar_dma:
                    for dn, dc in o._bar_dma.items():
                        waits[("d", dn)] = max(waits.get(("d", dn), 0), 16 * dc)
                for dn, v in o.dma_waits.items():
                    waits[("d", dn)] = max(waits.get(("d", dn), 0), v)
                for p in o.deps:
                    waits[("e", p.eng)] = max(waits.get(("e", p.eng), 0), val[p])
                for key, v in waits.items():
                    if marks.get(key, 0) >= v:
                        continue
                    marks[key] = v
                    sem = dma_sems[key[1]] if key[0] == "d" else eng_sems[key[1]]
                    engobj.wait_ge(sem, v)
                inst = o.fn(engobj)
                if o.dma is not None:
                    inst.then_inc(dma_sems[o.dma], 16)
                elif o.inc:
                    if inst is None:
                        inst = engobj.nop()
                    inst.then_inc(eng_sems[e], 1)
            if e == "gp":
                for dn, v in final_dma_waits.items():
                    engobj.wait_ge(dma_sems[dn], v)
        return run_engine


def build_program():
    nc = bass.Bass("TRN2", target_bir_lowering=False)
    es = contextlib.ExitStack()

    def din(name, shape, dt):
        return nc.dram_tensor(name, list(shape), dt, kind="ExternalInput")

    xT = din("xT", (E, L), F32)
    wqkvT = din("wqkvT", (E, 3 * E), F32)
    woutT = din("woutT", (E, E), F32)
    w1T = din("w1T", (NEXP, E, FF), F32)
    w2T = din("w2T", (NEXP, FF, E), F32)
    b1T = din("b1T", (NEXP, 128, FF // 128), F32)
    b2T = din("b2T", (NEXP, 128, E // 128), F32)
    routerT = din("routerT", (E, NEXP), F32)
    costab = din("costab", (128, L), BF16)
    sintab = din("sintab", (128, L), BF16)
    wm64 = din("wm64", (64, 64), F32)
    hpT = din("hpT", (64, 64), F32)
    s1pT = din("s1pT", (64, 64), F32)
    s2T = din("s2T", (64, 64), F32)
    s3T = din("s3T", (64, 64), F32)
    gflat = din("gflat", (1, 64 * 64), F32)
    onescol = din("onescol", (128, 1), BF16)
    onesrow = din("onesrow", (1, 128), BF16)
    ind4b = din("ind4b", (128, 4), BF16)
    ind4q = din("ind4q", (4, 4 * 128), BF16)
    ind8b = din("ind8b", (8, 8 * 128), BF16)
    id8 = din("id8", (8, 8), F32)
    id128 = din("id128", (128, 128), F32)
    c4096 = din("c4096", (1, 64), F32)
    actconst = din("actconst", (128, 2), F32)

    outT = nc.dram_tensor("outT", [E, L], F32, kind="ExternalOutput")

    w1bf = nc.dram_tensor("w1bf", [NEXP, E, FF], BF16, kind="Internal")
    w2bf = nc.dram_tensor("w2bf", [NEXP, FF, E], BF16, kind="Internal")
    qphi_d = nc.dram_tensor("qphi_d", [4096, L], BF16, kind="Internal")

    eng_sems = {e: es.enter_context(nc.semaphore(f"s_{e}")) for e in ENGINES}
    dma_names = ["d_init", "d_wc", "d_mpl", "d_cs0", "d_cs1",
                 "d_xb0", "d_xb1", "d_x0", "d_x1", "d_qsp0", "d_qsp1",
                 "d_qh0", "d_qh1", "d_w1a", "d_w1b", "d_w2a", "d_w2b",
                 "d_out0", "d_out1"]
    dma_sems = {n: es.enter_context(nc.semaphore(n)) for n in dma_names}

    S = Sched()

    pb = [es.enter_context(nc.psum_tensor(f"pb{i}", [128, 512], F32)) for i in range(8)]
    PBK = [(f"pb{i}",) for i in range(8)]
    ones_c = es.enter_context(nc.sbuf_tensor("ones_c", [128, 1], BF16))
    ones_r = es.enter_context(nc.sbuf_tensor("ones_r", [1, 128], BF16))
    i4b = es.enter_context(nc.sbuf_tensor("i4b", [128, 4], BF16))
    i4q = es.enter_context(nc.sbuf_tensor("i4q", [4, 4 * 128], BF16))
    i8b = es.enter_context(nc.sbuf_tensor("i8b", [8, 8 * 128], BF16))
    id8_s = es.enter_context(nc.sbuf_tensor("id8_s", [8, 8], F32))
    id128_s = es.enter_context(nc.sbuf_tensor("id128_s", [128, 128], F32))
    Sbf = es.enter_context(nc.sbuf_tensor("Sbf", [128, NH, 130], BF16))
    acst = es.enter_context(nc.sbuf_tensor("acst", [128, 2], F32))

    def g_init(dst_ap, src_ap, tag, writes):
        S.op("gp", lambda G, d=dst_ap, s=src_ap: G.dma_start(d, s),
             writes=writes, dma="d_init", tag=tag)

    g_init(ones_c[:, :], onescol.ap(), "ld_onesc", [("ones_c",)])
    g_init(ones_r[:, :], onesrow.ap(), "ld_onesr", [("ones_r",)])
    g_init(i4b[:, :], ind4b.ap(), "ld_i4b", [("i4b",)])
    g_init(i4q[:, :], ind4q.ap(), "ld_i4q", [("i4q",)])
    g_init(i8b[:, :], ind8b.ap(), "ld_i8b", [("i8b",)])
    g_init(id8_s[:, :], id8.ap(), "ld_id8", [("id8",)])
    g_init(id128_s[:, :], id128.ap(), "ld_id128", [("id128",)])
    g_init(acst[:, :], actconst.ap(), "ld_acst", [("acst",)])

    es1 = contextlib.ExitStack()
    wqkv_s = es1.enter_context(nc.sbuf_tensor("wqkv_s", [128, 8, 3 * E], BF16))
    Mble = [es1.enter_context(nc.sbuf_tensor(f"Mble{g}", [128, 1024], BF16)) for g in range(4)]
    Mblo = [es1.enter_context(nc.sbuf_tensor(f"Mblo{g}", [128, 1024], BF16)) for g in range(4)]
    cos_t = [es1.enter_context(nc.sbuf_tensor(f"cos{p}", [128, T], BF16)) for p in range(2)]
    sin_t = [es1.enter_context(nc.sbuf_tensor(f"sin{p}", [128, T], BF16)) for p in range(2)]
    xbf = es1.enter_context(nc.sbuf_tensor("xbf", [128, 8, T], BF16))
    ln1 = es1.enter_context(nc.sbuf_tensor("ln1", [128, 8, T], BF16))
    qbf = es1.enter_context(nc.sbuf_tensor("qbf", [128, 8, T], BF16))
    kbf = es1.enter_context(nc.sbuf_tensor("kbf", [128, 8, T], BF16))
    ropeEk = es1.enter_context(nc.sbuf_tensor("ropeEk", [128, T], BF16))
    ropeOk = es1.enter_context(nc.sbuf_tensor("ropeOk", [128, T], BF16))
    ropeEq = es1.enter_context(nc.sbuf_tensor("ropeEq", [128, T], BF16))
    ropeOq = es1.enter_context(nc.sbuf_tensor("ropeOq", [128, T], BF16))
    sq_s = [es1.enter_context(nc.sbuf_tensor(f"sq{j}", [128, T], BF16)) for j in range(2)]
    xsq_s = [es1.enter_context(nc.sbuf_tensor(f"xsq{j}", [128, T], BF16)) for j in range(2)]
    tmp_bf = [es1.enter_context(nc.sbuf_tensor(f"tmpbf{j}", [128, T], BF16)) for j in range(2)]
    tmp_f = [es1.enter_context(nc.sbuf_tensor(f"tmpf{j}", [128, T], F32)) for j in range(2)]
    kphi_s = es1.enter_context(nc.sbuf_tensor("kphi_s", [128, TC, 1024], BF16))
    qstage = [es1.enter_context(nc.sbuf_tensor(f"qstage{j}", [128, 8, T], BF16)) for j in range(2)]
    vtok = es1.enter_context(nc.sbuf_tensor("vtok", [128, TC, NH, 65], BF16))
    xnk_s = es1.enter_context(nc.sbuf_tensor("xnk_s", [128, 4, 16], F32))
    xnq_s = es1.enter_context(nc.sbuf_tensor("xnq_s", [4, 4, T], BF16))
    row1 = es1.enter_context(nc.sbuf_tensor("row1", [1, 3 * 512], F32))
    rowb1 = es1.enter_context(nc.sbuf_tensor("rowb1", [1, 2 * 512], BF16))
    Sacc = es1.enter_context(nc.sbuf_tensor("Sacc", [128, NH, 130], F32))

    esM = contextlib.ExitStack()
    wm_s = esM.enter_context(nc.sbuf_tensor("wm_s", [64, 64], F32))
    hpT_s = esM.enter_context(nc.sbuf_tensor("hpT_s", [64, 64], F32))
    s1p_s = esM.enter_context(nc.sbuf_tensor("s1p_s", [64, 64], F32))
    s2_s = esM.enter_context(nc.sbuf_tensor("s2_s", [64, 64], F32))
    s3_s = esM.enter_context(nc.sbuf_tensor("s3_s", [64, 64], F32))
    gfl_s = esM.enter_context(nc.sbuf_tensor("gfl_s", [1, 1024], F32))
    gb_s = esM.enter_context(nc.sbuf_tensor("gb_s", [64, 1024], F32))
    mstage = esM.enter_context(nc.sbuf_tensor("mstage", [64, 256], F32))
    mtmp = [esM.enter_context(nc.sbuf_tensor(f"mtmp{j}", [64, 64], F32)) for j in range(3)]
    c4096_s = esM.enter_context(nc.sbuf_tensor("c4096_s", [1, 64], F32))

    g_init(wm_s[:, :], wm64.ap(), "ld_wm", [("wm",)])
    g_init(hpT_s[:, :], hpT.ap(), "ld_hpT", [("hpT",)])
    g_init(s1p_s[:, :], s1pT.ap(), "ld_s1p", [("s1p",)])
    g_init(s2_s[:, :], s2T.ap(), "ld_s2", [("s2",)])
    g_init(s3_s[:, :], s3T.ap(), "ld_s3", [("s3",)])
    g_init(c4096_s[:, :], c4096.ap(), "ld_c4096", [("c4096",)])
    if os.environ.get("KERNEL_WQKV", "1") == "1":
        g_init(wqkv_s[:, :, :], wqkvT.ap().rearrange("(c p) o -> p c o", p=128),
               "ld_wqkv", [("wqkv",)])

    S.op("dve", lambda V: V.memset(vtok[:, :, :, 64:65], 1.0), writes=[("vtok_ones",)], tag="ms_v")
    S.op("dve", lambda V: V.memset(Sacc[:, :, :], 0.0),
         writes=[("Sacc", h) for h in range(NH)], tag="ms_S")
    for g in range(4):
        S.op("dve", lambda V, g=g: V.memset(Mble[g][:, :], 0.0), writes=[("Mble0", g)], tag="ms_Me")
        S.op("dve", lambda V, g=g: V.memset(Mblo[g][:, :], 0.0), writes=[("Mblo0", g)], tag="ms_Mo")

    for b in range(4 if os.environ.get("KERNEL_MB", "1") == "1" else 0):
        S.op("gp", lambda G, b=b: G.dma_start(gfl_s[:, :], gflat.ap()[:, b * 1024:(b + 1) * 1024]),
             writes=[("gfl",)], dma="d_init", tag="ld_gfl")
        for j in range(2 if "g" in os.environ.get("KERNEL_MPARTS", "gmp") else 0):
            bank, bkey = pb[j], PBK[j]
            S.op("pe", lambda P, bk=bank, j=j: P.matmul(
                bk[0:64, :], c4096_s[0:1, 0:64], gfl_s[0:1, j * 512:(j + 1) * 512],
                start=True, stop=True),
                reads=[("c4096",), ("gfl",)], writes=[bkey], tag="gbmm")
            S.op("dve", lambda V, bk=bank, j=j: V.tensor_copy(
                gb_s[:, j * 512:(j + 1) * 512], bk[0:64, :]),
                reads=[bkey], writes=[("gb",)], tag="gbcp")
        MPARTS = os.environ.get("KERNEL_MPARTS", "gmp")
        for hh in range(4 if "m" in MPARTS else 0):
            h = b * 4 + hh
            g, hp = h // 4, h % 4
            for r in range(4):
                idx = 4 * h + r
                li = idx - b * 16
                pbank, pkey = pb[2 + (r % 2)], PBK[2 + (r % 2)]
                S.op("dve", lambda V, i=idx: V.tensor_scalar(
                    mtmp[0][:, :], wm_s[:, :], s3_s[:, i:i + 1], None, ALU.mult),
                    reads=[("wm",), ("s3",)], writes=[("mt0",)], tag="m_s3")
                S.op("dve", lambda V, li=li: V.tensor_tensor(
                    mtmp[1][:, :], mtmp[0][:, :], gb_s[:, li * 64:(li + 1) * 64], ALU.mult),
                    reads=[("mt0",), ("gb",)], writes=[("mt1",)], tag="m_g")
                S.op("pe", lambda P, bk=pbank: P.matmul(bk[0:64, 0:64], wm_s[:, :], mtmp[1][:, :],
                                                        start=True, stop=True),
                     reads=[("wm",), ("mt1",)], writes=[pkey], tag="m_mm1")
                S.op("dve", lambda V, bk=pbank, i=idx: V.tensor_scalar(
                    mtmp[2][:, :], bk[0:64, 0:64], s2_s[:, i:i + 1], None, ALU.mult),
                    reads=[pkey, ("s2",)], writes=[("mt2",)], tag="m_s2")
                S.op("pe", lambda P, bk=pbank: P.matmul(bk[0:64, 0:64], hpT_s[:, :], mtmp[2][:, :],
                                                        start=True, stop=True),
                     reads=[("hpT",), ("mt2",)], writes=[pkey], tag="m_mm2")
                S.op("dve", lambda V, bk=pbank, i=idx, r=r: V.tensor_scalar(
                    mstage[:, r * 64:(r + 1) * 64], bk[0:64, 0:64], s1p_s[:, i:i + 1],
                    None, ALU.mult),
                    reads=[pkey, ("s1p",)], writes=[("Mst", r)], tag="m_s1")
            if "p" not in MPARTS:
                continue
            rkeys = [("Mst", r) for r in range(4)]
            S.op("gp", lambda G, g=g, hp=hp: G.dma_start(
                Mble[g][32 * hp:32 * hp + 32, 256 * hp:256 * hp + 256],
                mstage[0:32, :]),
                reads=rkeys + [("Mble0", g)], writes=[("Mble", g, hp)], dma="d_mpl", tag="mple")
            S.op("gp", lambda G, g=g, hp=hp: G.dma_start(
                Mblo[g][32 * hp:32 * hp + 32, 256 * hp:256 * hp + 256],
                mstage[32:64, :]),
                reads=rkeys + [("Mblo0", g)], writes=[("Mblo", g, hp)], dma="d_mpl", tag="mplo")

    S.barrier()
    esM.close()

    MKE = [("Mble", g, hp) for g in range(4) for hp in range(4)]

    def MKO(g):
        return [("Mblo", g, hp) for hp in range(4)]

    def MKEg(g):
        return [("Mble", g, hp) for hp in range(4)]

    def emit_ln(xkey, xtile, lnkey, lntile, rowf, rowb, kp, tag):
        for c in range(8):
            S.op("act", lambda A, c=c: A.activation(xsq_s[c % 2][:, :], xtile[:, c, :], AF.Square),
                 reads=[xkey], writes=[(kp + "xsq", c % 2)], tag=f"{tag}_sq")
            S.op("pe", lambda P, c=c: P.matmul(
                pb[6][0:1, :], ones_c[:, :], xtile[:, c, :], start=(c == 0), stop=(c == 7)),
                reads=[xkey, ("ones_c",)], writes=[PBK[6][0]], tag=f"{tag}_sum")
            S.op("pe", lambda P, c=c: P.matmul(
                pb[7][0:1, :], ones_c[:, :], xsq_s[c % 2][:, :], start=(c == 0), stop=(c == 7)),
                reads=[(kp + "xsq", c % 2), ("ones_c",)], writes=[PBK[7][0]], tag=f"{tag}_ssq")
        S.op("dve", lambda V: V.tensor_scalar(rowb[0:1, 0:512], pb[6][0:1, :],
                                              1.0 / 1024.0, None, ALU.mult),
             reads=[PBK[6][0]], writes=[(kp + "meanb",)], tag=f"{tag}_mean")
        S.op("dve", lambda V: V.tensor_tensor(rowf[0:1, 0:512], rowb[0:1, 0:512],
                                              rowb[0:1, 0:512], ALU.mult),
             reads=[(kp + "meanb",)], writes=[(kp + "msq",)], tag=f"{tag}_msq")
        S.op("dve", lambda V: V.scalar_tensor_tensor(
            rowf[0:1, 512:1024], pb[7][0:1, :], 1.0 / 1024.0, rowf[0:1, 0:512],
            ALU.mult, ALU.subtract),
            reads=[PBK[7][0], (kp + "msq",)], writes=[(kp + "var",)], tag=f"{tag}_var")
        S.op("act", lambda A: A.activation(rowf[0:1, 1024:1536], rowf[0:1, 512:1024],
                                           AF.Ln, bias=acst[0:1, 0:1], scale=1.0),
             reads=[(kp + "var",), ("acst",)], writes=[(kp + "lnv",)], tag=f"{tag}_ln")
        S.op("act", lambda A: A.activation(rowb[0:1, 512:1024], rowf[0:1, 1024:1536],
                                           AF.Exp, bias=0.0, scale=acst[0:1, 1:2]),
             reads=[(kp + "lnv",), ("acst",)], writes=[(kp + "rstd",)], tag=f"{tag}_rstd")
        S.op("pe", lambda P: P.matmul(pb[6][:, :], ones_r[0:1, :], rowb[0:1, 0:512],
                                      start=True, stop=True),
             reads=[(kp + "meanb",), ("ones_r",)], writes=[PBK[6][0]], tag=f"{tag}_bm")
        S.op("pe", lambda P: P.matmul(pb[7][:, :], ones_r[0:1, :], rowb[0:1, 512:1024],
                                      start=True, stop=True),
             reads=[(kp + "rstd",), ("ones_r",)], writes=[PBK[7][0]], tag=f"{tag}_br")
        for c in range(8):
            S.op("dve", lambda V, c=c: V.tensor_tensor(
                tmp_f[c % 2][:, :], xtile[:, c, :], pb[6][:, :], ALU.subtract),
                reads=[xkey, PBK[6][0]], writes=[(kp + "tmp", c % 2)], tag=f"{tag}_sub")
            S.op("dve", lambda V, c=c: V.tensor_tensor(
                lntile[:, c, :], tmp_f[c % 2][:, :], pb[7][:, :], ALU.mult),
                reads=[(kp + "tmp", c % 2), PBK[7][0]], writes=[lnkey], tag=f"{tag}_mul")

    for i in range(NT_EMIT if P1_ON else 0):
        par = i % 2
        t0 = i * T
        S.op("gp", lambda G, t0=t0: G.dma_start(
            xbf[:, :, :],
            xT.ap().rearrange("(c p) t -> p c t", p=128)[:, :, t0:t0 + T]),
            writes=[("xbf",)], dma="d_xb0", tag="p1_x")
        S.op("sp", lambda Y, par=par, t0=t0: Y.dma_start(
            cos_t[par][:, :], costab.ap()[:, t0:t0 + T]),
            writes=[("cos", par)], dma=f"d_cs{par}", tag="p1_cos")
        S.op("sp", lambda Y, par=par, t0=t0: Y.dma_start(
            sin_t[par][:, :], sintab.ap()[:, t0:t0 + T]),
            writes=[("sin", par)], dma=f"d_cs{par}", tag="p1_sin")
        wc_list = [i] if NT_EMIT >= 16 else (list(range(16)) if i == 0 else [])
        for ci in wc_list:
            if ci < 8:
                S.op("gp", lambda G, e=ci: G.dma_start(w1bf.ap()[e], w1T.ap()[e]),
                     writes=[("w1bf", ci)], dma="d_wc", tag="wcast1")
            else:
                S.op("gp", lambda G, e=ci - 8: G.dma_start(w2bf.ap()[e], w2T.ap()[e]),
                     writes=[("w2bf", ci - 8)], dma="d_wc", tag="wcast2")

        emit_ln(("xbf",), xbf, ("ln1",), ln1, row1, rowb1, "l1", "ln1")

        for oc in range(16):
            bank, bkey = pb[oc % 2], PBK[oc % 2]
            dst = qbf if oc < 8 else kbf
            dkey = ("qbf", oc) if oc < 8 else ("kbf", oc - 8)
            for kc in range(8):
                S.op("pe", lambda P, b=bank, oc=oc, kc=kc: P.matmul(
                    b[:, :], wqkv_s[:, kc, oc * 128:(oc + 1) * 128], ln1[:, kc, :],
                    start=(kc == 0), stop=(kc == 7)),
                    reads=[("wqkv",), ("ln1",)], writes=[bkey], tag="qk_mm")
            S.op("act", lambda A, b=bank, d=dst, c=oc % 8: A.activation(
                d[:, c, :], b[:, :], AF.Identity),
                reads=[bkey], writes=[dkey], tag="qk_ev")
        for tc in range(TC):
            for vh in range(2):
                j = (tc * 2 + vh) % 2
                bank, bkey = pb[j], PBK[j]
                for kc in range(8):
                    S.op("pe", lambda P, b=bank, tc=tc, vh=vh, kc=kc: P.matmul(
                        b[:, :], ln1[:, kc, tc * 128:(tc + 1) * 128],
                        wqkv_s[:, kc, 2048 + vh * 512:2048 + (vh + 1) * 512],
                        start=(kc == 0), stop=(kc == 7)),
                        reads=[("wqkv",), ("ln1",)], writes=[bkey], tag="v_mm")
                S.op("act", lambda A, b=bank, tc=tc, vh=vh: A.activation(
                    vtok[:, tc, vh * 8:(vh + 1) * 8, 0:64],
                    b[:, :].rearrange("p (h d) -> p h d", d=64), AF.Identity),
                    reads=[bkey], writes=[("vtok", tc, vh)], tag="v_ev")

        for g in range(4):
            S.op("act", lambda A, g=g: A.activation(sq_s[0][:, :], kbf[:, 2 * g, :], AF.Square),
                 reads=[("kbf", 2 * g)], writes=[("sqE",)], tag="ksqE")
            S.op("act", lambda A, g=g: A.activation(sq_s[1][:, :], kbf[:, 2 * g + 1, :], AF.Square),
                 reads=[("kbf", 2 * g + 1)], writes=[("sqO",)], tag="ksqO")
            for tc in range(TC):
                S.op("pe", lambda P, tc=tc: P.matmul(
                    pb[2][:, tc * 4:tc * 4 + 4], sq_s[0][:, tc * 128:(tc + 1) * 128],
                    i4b[:, :], start=(tc == 0), stop=False),
                    reads=[("sqE",), ("i4b",)], writes=[PBK[2][0]], tag="xnk_mmE")
                S.op("pe", lambda P, tc=tc: P.matmul(
                    pb[2][:, tc * 4:tc * 4 + 4], sq_s[1][:, tc * 128:(tc + 1) * 128],
                    i4b[:, :], start=False, stop=(tc == 3)),
                    reads=[("sqO",), ("i4b",)], writes=[PBK[2][0]], tag="xnk_mmO")
            S.op("dve", lambda V, g=g: V.tensor_scalar(
                xnk_s[:, g, :], pb[2][:, 0:16], -0.5, -LN16, ALU.mult, ALU.add),
                reads=[PBK[2][0]], writes=[("xnk", g)], tag="xnk_ev")
            S.op("act", lambda A, g=g: A.activation(sq_s[0][:, :], qbf[:, 2 * g, :], AF.Square),
                 reads=[("qbf", 2 * g)], writes=[("sqE",)], tag="qsqE")
            S.op("act", lambda A, g=g: A.activation(sq_s[1][:, :], qbf[:, 2 * g + 1, :], AF.Square),
                 reads=[("qbf", 2 * g + 1)], writes=[("sqO",)], tag="qsqO")
            S.op("pe", lambda P: P.matmul(pb[2][0:4, :], i4b[:, :], sq_s[0][:, :],
                                          start=True, stop=False),
                 reads=[("sqE",), ("i4b",)], writes=[PBK[2][0]], tag="xnq_mmE")
            S.op("pe", lambda P: P.matmul(pb[2][0:4, :], i4b[:, :], sq_s[1][:, :],
                                          start=False, stop=True),
                 reads=[("sqO",), ("i4b",)], writes=[PBK[2][0]], tag="xnq_mmO")
            S.op("dve", lambda V, g=g: V.tensor_scalar(
                xnq_s[:, g, :], pb[2][0:4, :], -0.5, -LN16, ALU.mult, ALU.add),
                reads=[PBK[2][0]], writes=[("xnq", g)], tag="xnq_ev")

            for (src, k0, k1, dE, dO, dkE, dkO) in (
                (kbf, ("kbf", 2 * g), ("kbf", 2 * g + 1), ropeEk, ropeOk, ("rEk",), ("rOk",)),
                (qbf, ("qbf", 2 * g), ("qbf", 2 * g + 1), ropeEq, ropeOq, ("rEq",), ("rOq",)),
            ):
                S.op("dve", lambda V, s=src, g=g, par=par: V.tensor_tensor(
                    tmp_bf[0][:, :], s[:, 2 * g, :], cos_t[par][:, :], ALU.mult),
                    reads=[k0, ("cos", par)], writes=[("rt0",)], tag="rp1")
                S.op("dve", lambda V, s=src, g=g, par=par: V.tensor_tensor(
                    tmp_bf[1][:, :], s[:, 2 * g + 1, :], sin_t[par][:, :], ALU.mult),
                    reads=[k1, ("sin", par)], writes=[("rt1",)], tag="rp2")
                S.op("dve", lambda V, d=dE: V.tensor_tensor(
                    d[:, :], tmp_bf[0][:, :], tmp_bf[1][:, :], ALU.subtract),
                    reads=[("rt0",), ("rt1",)], writes=[dkE], tag="rp3")
                S.op("dve", lambda V, s=src, g=g, par=par: V.tensor_tensor(
                    tmp_bf[0][:, :], s[:, 2 * g, :], sin_t[par][:, :], ALU.mult),
                    reads=[k0, ("sin", par)], writes=[("rt0",)], tag="rp4")
                S.op("dve", lambda V, s=src, g=g, par=par: V.tensor_tensor(
                    tmp_bf[1][:, :], s[:, 2 * g + 1, :], cos_t[par][:, :], ALU.mult),
                    reads=[k1, ("cos", par)], writes=[("rt1",)], tag="rp5")
                S.op("dve", lambda V, d=dO: V.tensor_tensor(
                    d[:, :], tmp_bf[0][:, :], tmp_bf[1][:, :], ALU.add),
                    reads=[("rt0",), ("rt1",)], writes=[dkO], tag="rp6")

            for tc in range(TC):
                bA, bB = (pb[3], pb[4]) if tc % 2 == 0 else (pb[5], pb[6])
                kA, kB = (PBK[3][0], PBK[4][0]) if tc % 2 == 0 else (PBK[5][0], PBK[6][0])
                for half, (bank, bkey) in enumerate(((bA, kA), (bB, kB))):
                    S.op("pe", lambda P, b=bank, g=g, tc=tc, hf=half: P.matmul(
                        b[:, :], ropeEk[:, tc * 128:(tc + 1) * 128],
                        Mble[g][:, hf * 512:(hf + 1) * 512], start=True, stop=False),
                        reads=[("rEk",)] + MKEg(g), writes=[bkey], tag="phk_mmE")
                    S.op("pe", lambda P, b=bank, g=g, tc=tc, hf=half: P.matmul(
                        b[:, :], ropeOk[:, tc * 128:(tc + 1) * 128],
                        Mblo[g][:, hf * 512:(hf + 1) * 512], start=False, stop=True),
                        reads=[("rOk",)] + MKO(g), writes=[bkey], tag="phk_mmO")
                for hp in range(4):
                    bank, bkey = (bA, kA) if hp < 2 else (bB, kB)
                    off = (hp % 2) * 256
                    S.op("act", lambda A, b=bank, g=g, tc=tc, hp=hp, off=off: A.activation(
                        kphi_s[:, tc, hp * 256:(hp + 1) * 256], b[:, off:off + 256],
                        AF.Exp, bias=xnk_s[:, g, tc * 4 + hp:tc * 4 + hp + 1], scale=1.0),
                        reads=[bkey, ("xnk", g)], writes=[("kphi", tc, hp)], tag="phk_exp")
            for hp in range(4):
                h = 4 * g + hp
                n = 0
                for mc in range(2):
                    for tc in range(TC):
                        S.op("pe", lambda P, tc=tc, hp=hp, mc=mc, h=h,
                             first=(n == 0), last=(n == 7): P.matmul(
                            pb[7][:, mc * 65:mc * 65 + 65],
                            kphi_s[:, tc, hp * 256 + mc * 128:hp * 256 + (mc + 1) * 128],
                            vtok[:, tc, h, 0:65], start=first, stop=last),
                            reads=[("kphi", tc, hp), ("vtok", tc, h // 8), ("vtok_ones",)],
                            writes=[PBK[7][0]], tag="S_mm")
                        n += 1
                S.op("dve", lambda V, h=h: V.tensor_tensor(
                    Sacc[:, h, :], Sacc[:, h, :], pb[7][:, 0:130], ALU.add),
                    reads=[PBK[7][0], ("Sacc", h)], writes=[("Sacc", h)], tag="S_fl")

            qsp = g % 2
            for mc in range(8):
                bank, bkey = pb[mc % 2], PBK[mc % 2]
                hp = mc // 2
                S.op("pe", lambda P, b=bank, g=g, mc=mc: P.matmul(
                    b[:, :], Mble[g][:, mc * 128:(mc + 1) * 128], ropeEq[:, :],
                    start=True, stop=False),
                    reads=[("rEq",)] + MKEg(g), writes=[bkey], tag="phq_mmE")
                S.op("pe", lambda P, b=bank, g=g, mc=mc: P.matmul(
                    b[:, :], Mblo[g][:, mc * 128:(mc + 1) * 128], ropeOq[:, :],
                    start=False, stop=False),
                    reads=[("rOq",)] + MKO(g), writes=[bkey], tag="phq_mmO")
                S.op("pe", lambda P, b=bank, g=g, hp=hp: P.matmul(
                    b[:, :], i4q[:, hp * 128:(hp + 1) * 128], xnq_s[:, g, :],
                    start=False, stop=True),
                    reads=[("xnq", g), ("i4q",)], writes=[bkey], tag="phq_r1")
                S.op("act", lambda A, b=bank, qsp=qsp, mc=mc: A.activation(
                    qstage[qsp][:, mc, :], b[:, :], AF.Exp),
                    reads=[bkey], writes=[("qst", qsp, mc)], tag="phq_exp")
            S.op("gp", lambda G, qsp=qsp, g=g, t0=t0: G.dma_start(
                qphi_d.ap().rearrange("(r p) t -> p r t", p=128)[:, 8 * g:8 * g + 8, t0:t0 + T],
                qstage[qsp][:, :, :]),
                reads=[("qst", qsp, mc) for mc in range(8)], dma=f"d_qsp{qsp}", tag="q_spill")

    S.op("dve", lambda V: V.tensor_copy(Sbf[:, :, :], Sacc[:, :, :]),
         reads=[("Sacc", h) for h in range(NH)], writes=[("Sbf",)], tag="S_cast")

    S.barrier()
    es1.close()

    es2 = contextlib.ExitStack()
    wout_s = es2.enter_context(nc.sbuf_tensor("wout_s", [128, 8, E], BF16))
    b1s = es2.enter_context(nc.sbuf_tensor("b1s", [128, NEXP, 32], F32))
    b2s = es2.enter_context(nc.sbuf_tensor("b2s", [128, NEXP, 8], F32))
    rtr_s = es2.enter_context(nc.sbuf_tensor("rtr_s", [128, 8, 8], BF16))
    x_s = es2.enter_context(nc.sbuf_tensor("x_s", [128, 8, T], F32))
    qph = [es2.enter_context(nc.sbuf_tensor(f"qph{p}", [128, 8, T], BF16)) for p in range(2)]
    ln2 = es2.enter_context(nc.sbuf_tensor("ln2", [128, 8, T], BF16))
    att = es2.enter_context(nc.sbuf_tensor("att", [128, 8, T], BF16))
    acc = es2.enter_context(nc.sbuf_tensor("acc", [128, 8, T], F32))
    gsb = es2.enter_context(nc.sbuf_tensor("gsb", [128, 32, T], BF16))
    w1s = [es2.enter_context(nc.sbuf_tensor(f"w1s{p}", [128, 8, T], BF16)) for p in range(2)]
    w2s = [es2.enter_context(nc.sbuf_tensor(f"w2s{p}", [128, 32, 128], BF16)) for p in range(2)]
    gateb = es2.enter_context(nc.sbuf_tensor("gateb", [128, 8, T], BF16))
    gfm = es2.enter_context(nc.sbuf_tensor("gfm", [8, T], BF16))
    rfm = es2.enter_context(nc.sbuf_tensor("rfm", [8, T], F32))
    rtk = es2.enter_context(nc.sbuf_tensor("rtk", [128, 8], F32))
    etk = es2.enter_context(nc.sbuf_tensor("etk", [128, 8], F32))
    gtk = es2.enter_context(nc.sbuf_tensor("gtk", [128, 8], F32))
    tscr = es2.enter_context(nc.sbuf_tensor("tscr", [128, 8], F32))
    nm_s = [es2.enter_context(nc.sbuf_tensor(f"nm{j}", [64, T], BF16)) for j in range(2)]
    dn_s = es2.enter_context(nc.sbuf_tensor("dn_s", [1, 2, 512], F32))
    rc_s = es2.enter_context(nc.sbuf_tensor("rc_s", [1, 2, 512], BF16))
    tmp2f = [es2.enter_context(nc.sbuf_tensor(f"tmp2f{j}", [128, T], F32)) for j in range(2)]
    row2 = es2.enter_context(nc.sbuf_tensor("row2", [1, 3 * 512], F32))
    rowb2 = es2.enter_context(nc.sbuf_tensor("rowb2", [1, 2 * 512], BF16))
    ysq2 = [es2.enter_context(nc.sbuf_tensor(f"ysq2{j}", [128, T], BF16)) for j in range(2)]
    ybf2 = [es2.enter_context(nc.sbuf_tensor(f"ybf2{j}", [128, T], BF16)) for j in range(2)]

    S.op("gp", lambda G: G.dma_start(wout_s[:, :, :],
                                     woutT.ap().rearrange("(c p) o -> p c o", p=128)),
         writes=[("wout",)], dma="d_init", tag="ld_wout")
    S.op("gp", lambda G: G.dma_start(b1s[:, :, :], b1T.ap().rearrange("e p c -> p e c")),
         writes=[("b1",)], dma="d_init", tag="ld_b1")
    S.op("gp", lambda G: G.dma_start(b2s[:, :, :], b2T.ap().rearrange("e p c -> p e c")),
         writes=[("b2",)], dma="d_init", tag="ld_b2")
    S.op("gp", lambda G: G.dma_start(rtr_s[:, :, :],
                                     routerT.ap().rearrange("(c p) e -> p c e", p=128)),
         writes=[("rtr",)], dma="d_init", tag="ld_rtr")

    def emit_ln2():
        kp = "l2"
        for c in range(8):
            S.op("act", lambda A, c=c: A.activation(
                ybf2[c % 2][:, :], x_s[:, c, :], AF.Identity),
                reads=[("y",)], writes=[(kp + "ybf", c % 2)], tag="l2_cp")
            S.op("act", lambda A, c=c: A.activation(
                ysq2[c % 2][:, :], ybf2[c % 2][:, :], AF.Square),
                reads=[(kp + "ybf", c % 2)], writes=[(kp + "ysq", c % 2)], tag="l2_sq")
            S.op("pe", lambda P, c=c: P.matmul(
                pb[6][0:1, :], ones_c[:, :], ybf2[c % 2][:, :], start=(c == 0), stop=(c == 7)),
                reads=[(kp + "ybf", c % 2), ("ones_c",)], writes=[PBK[6][0]], tag="l2_sum")
            S.op("pe", lambda P, c=c: P.matmul(
                pb[7][0:1, :], ones_c[:, :], ysq2[c % 2][:, :], start=(c == 0), stop=(c == 7)),
                reads=[(kp + "ysq", c % 2), ("ones_c",)], writes=[PBK[7][0]], tag="l2_ssq")
        S.op("dve", lambda V: V.tensor_scalar(rowb2[0:1, 0:512], pb[6][0:1, :],
                                              1.0 / 1024.0, None, ALU.mult),
             reads=[PBK[6][0]], writes=[(kp + "meanb",)], tag="l2_mean")
        S.op("dve", lambda V: V.tensor_tensor(row2[0:1, 0:512], rowb2[0:1, 0:512],
                                              rowb2[0:1, 0:512], ALU.mult),
             reads=[(kp + "meanb",)], writes=[(kp + "msq",)], tag="l2_msq")
        S.op("dve", lambda V: V.scalar_tensor_tensor(
            row2[0:1, 512:1024], pb[7][0:1, :], 1.0 / 1024.0, row2[0:1, 0:512],
            ALU.mult, ALU.subtract),
            reads=[PBK[7][0], (kp + "msq",)], writes=[(kp + "var",)], tag="l2_var")
        S.op("act", lambda A: A.activation(row2[0:1, 1024:1536], row2[0:1, 512:1024],
                                           AF.Ln, bias=acst[0:1, 0:1], scale=1.0),
             reads=[(kp + "var",), ("acst",)], writes=[(kp + "lnv",)], tag="l2_ln")
        S.op("act", lambda A: A.activation(rowb2[0:1, 512:1024], row2[0:1, 1024:1536],
                                           AF.Exp, bias=0.0, scale=acst[0:1, 1:2]),
             reads=[(kp + "lnv",), ("acst",)], writes=[(kp + "rstd",)], tag="l2_rstd")
        S.op("pe", lambda P: P.matmul(pb[6][:, :], ones_r[0:1, :], rowb2[0:1, 0:512],
                                      start=True, stop=True),
             reads=[(kp + "meanb",), ("ones_r",)], writes=[PBK[6][0]], tag="l2_bm")
        S.op("pe", lambda P: P.matmul(pb[7][:, :], ones_r[0:1, :], rowb2[0:1, 512:1024],
                                      start=True, stop=True),
             reads=[(kp + "rstd",), ("ones_r",)], writes=[PBK[7][0]], tag="l2_br")
        for c in range(8):
            S.op("dve", lambda V, c=c: V.tensor_tensor(
                tmp2f[c % 2][:, :], x_s[:, c, :], pb[6][:, :], ALU.subtract),
                reads=[("y",), PBK[6][0]], writes=[("o2t", c % 2)], tag="l2_sub")
            S.op("dve", lambda V, c=c: V.tensor_tensor(
                ln2[:, c, :], tmp2f[c % 2][:, :], pb[7][:, :], ALU.mult),
                reads=[("o2t", c % 2), PBK[7][0]], writes=[("ln2",)], tag="l2_mul")

    for i in range(NT_EMIT if P2_ON else 0):
        t0 = i * T
        S.op("gp", lambda G, t0=t0: G.dma_start(
            x_s[:, :, :],
            xT.ap().rearrange("(c p) t -> p c t", p=128)[:, :, t0:t0 + T]),
            writes=[("x",), ("y",)], dma="d_x0", tag="p2_x")
        if not FFN_ONLY:
            S.op("sp", lambda Y, t0=t0: Y.dma_start(
                qph[0][:, :, :],
                qphi_d.ap().rearrange("(r p) t -> p r t", p=128)[:, 0:8, t0:t0 + T]),
                writes=[("qph", 0)], dma="d_qh0", tag="p2_qphi")

        if FFN_ONLY:
            S.op("dve", lambda V: V.memset(ln2[:, :, :], 0.001), writes=[("ln2",)], tag="ms_ln2")
            S.op("dve", lambda V: V.memset(gateb[:, :, :], 0.25),
                 writes=[("gateb", e) for e in range(NEXP)], tag="ms_gb")
            S.op("dve", lambda V: V.memset(x_s[:, :, :], 0.0), reads=[("x",)],
                 writes=[("y",)], tag="ms_y")
        for h in range(NH if not FFN_ONLY else 0):
            qtr, j = h // 4, h % 2
            qb = qtr % 2
            if h % 4 == 0 and qtr + 1 < 4:
                qb2 = (qtr + 1) % 2
                S.op("sp", lambda Y, q=qtr + 1, qb2=qb2, t0=t0: Y.dma_start(
                    qph[qb2][:, :, :],
                    qphi_d.ap().rearrange("(r p) t -> p r t", p=128)[:, 8 * q:8 * q + 8, t0:t0 + T]),
                    writes=[("qph", qb2)], dma=f"d_qh{qb2}", tag="p2_qphi")
            q0 = (h % 4) * 2
            bA, kA = pb[j], PBK[j][0]
            bB, kB = pb[2 + j], PBK[2 + j][0]
            for mc in range(2):
                S.op("pe", lambda P, b=bA, h=h, qb=qb, q0=q0, mc=mc: P.matmul(
                    b[0:65, :], Sbf[:, h, mc * 65:(mc + 1) * 65], qph[qb][:, q0 + mc, :],
                    start=(mc == 0), stop=(mc == 1)),
                    reads=[("Sbf",), ("qph", qb)], writes=[kA], tag="nm_mm")
            S.op("dve", lambda V, b=bA, j=j: V.tensor_scalar(
                dn_s[0:1, j, :], b[64:65, :], EPS, None, ALU.max),
                reads=[kA], writes=[("dn", j)], tag="nm_max")
            S.op("dve", lambda V, j=j: V.reciprocal(rc_s[0:1, j, :], dn_s[0:1, j, :]),
                 reads=[("dn", j)], writes=[("rc", j)], tag="nm_rc")
            S.op("pe", lambda P, b=bB, j=j: P.matmul(
                b[0:64, :], ones_r[0:1, 0:64], rc_s[0:1, j, :], start=True, stop=True),
                reads=[("rc", j), ("ones_r",)], writes=[kB], tag="nm_bc")
            S.op("act", lambda A, b=bA, j=j: A.activation(
                nm_s[j][:, :], b[0:64, :], AF.Identity),
                reads=[kA], writes=[("nm", j)], tag="nm_ev")
            S.op("dve", lambda V, b=bB, h=h, j=j: V.tensor_tensor(
                att[j * 64:j * 64 + 64, h // 2, :], nm_s[j][:, :], b[0:64, :], ALU.mult),
                reads=[("nm", j), kB], writes=[("att", h)], tag="nm_mul")

        for oc in range(8 if not FFN_ONLY else 0):
            bank, bkey = pb[4 + oc % 2], PBK[4 + oc % 2][0]
            for kc in range(8):
                S.op("pe", lambda P, b=bank, oc=oc, kc=kc: P.matmul(
                    b[:, :], wout_s[:, kc, oc * 128:(oc + 1) * 128], att[:, kc, :],
                    start=(kc == 0), stop=(kc == 7)),
                    reads=[("wout",), ("att", kc * 2), ("att", kc * 2 + 1)],
                    writes=[bkey], tag="op_mm")
            S.op("dve", lambda V, b=bank, oc=oc: V.tensor_tensor(
                x_s[:, oc, :], x_s[:, oc, :], b[:, :], ALU.add),
                reads=[("x",), bkey], writes=[("y",)], tag="op_res")

        if not FFN_ONLY:
            emit_ln2()

        for kc in range(8 if not FFN_ONLY else 0):
            S.op("pe", lambda P, kc=kc: P.matmul(
                pb[0][0:8, :], rtr_s[:, kc, :], ln2[:, kc, :], start=(kc == 0), stop=(kc == 7)),
                reads=[("rtr",), ("ln2",)], writes=[PBK[0][0]], tag="rt_mm")
        if not FFN_ONLY:
            S.op("dve", lambda V: V.tensor_copy(rfm[:, :], pb[0][0:8, :]),
                 reads=[PBK[0][0]], writes=[("rfm",)], tag="rt_ev")
        for tc in range(TC if not FFN_ONLY else 0):
            S.op("pe", lambda P, tc=tc: P.transpose(
                pb[1][0:128, 0:8], rfm[:, tc * 128:(tc + 1) * 128], id8_s[:, :]),
                reads=[("rfm",), ("id8",)], writes=[PBK[1][0]], tag="rt_tr")
            S.op("dve", lambda V: V.tensor_copy(rtk[:, :], pb[1][0:128, 0:8]),
                 reads=[PBK[1][0]], writes=[("rtk",)], tag="tk_cp")
            S.op("dve", lambda V: V.reduce_max(tscr[:, 0:1], rtk[:, :], axis=AX),
                 reads=[("rtk",)], writes=[("tscr",)], tag="tk_m1")
            S.op("dve", lambda V: V.tensor_scalar(
                gtk[:, :], rtk[:, :], tscr[:, 0:1], None, ALU.is_equal),
                reads=[("rtk",), ("tscr",)], writes=[("gtk",)], tag="tk_eq1")
            S.op("dve", lambda V: V.scalar_tensor_tensor(
                rtk[:, :], gtk[:, :], -1e30, rtk[:, :], ALU.mult, ALU.add),
                reads=[("gtk",), ("rtk",)], writes=[("rtk",)], tag="tk_rm")
            S.op("dve", lambda V: V.reduce_max(tscr[:, 1:2], rtk[:, :], axis=AX),
                 reads=[("rtk",)], writes=[("tscr",)], tag="tk_m2")
            S.op("dve", lambda V: V.tensor_tensor(
                tscr[:, 2:3], tscr[:, 1:2], tscr[:, 0:1], ALU.subtract),
                reads=[("tscr",)], writes=[("tscr",)], tag="tk_d21")
            S.op("act", lambda A: A.activation(tscr[:, 3:4], tscr[:, 2:3], AF.Exp),
                 reads=[("tscr",)], writes=[("tscr",)], tag="tk_e21")
            S.op("dve", lambda V: V.tensor_scalar(
                tscr[:, 4:5], tscr[:, 3:4], 1.0, None, ALU.add),
                reads=[("tscr",)], writes=[("tscr",)], tag="tk_e21p")
            S.op("dve", lambda V: V.reciprocal(tscr[:, 5:6], tscr[:, 4:5]),
                 reads=[("tscr",)], writes=[("tscr",)], tag="tk_w1")
            S.op("dve", lambda V: V.tensor_tensor(
                tscr[:, 6:7], tscr[:, 3:4], tscr[:, 5:6], ALU.mult),
                reads=[("tscr",)], writes=[("tscr",)], tag="tk_w2")
            S.op("dve", lambda V: V.tensor_scalar(
                etk[:, :], rtk[:, :], tscr[:, 1:2], None, ALU.is_equal),
                reads=[("rtk",), ("tscr",)], writes=[("etk",)], tag="tk_eq2")
            S.op("dve", lambda V: V.tensor_scalar(
                gtk[:, :], gtk[:, :], tscr[:, 5:6], None, ALU.mult),
                reads=[("gtk",), ("tscr",)], writes=[("gtk",)], tag="tk_gw1")
            S.op("dve", lambda V: V.scalar_tensor_tensor(
                gtk[:, :], etk[:, :], tscr[:, 6:7], gtk[:, :], ALU.mult, ALU.add),
                reads=[("etk",), ("tscr",), ("gtk",)], writes=[("gtk",)], tag="tk_gate")
            S.op("pe", lambda P, tc=tc: P.transpose(
                pb[1][0:8, 0:128], gtk[:, :], id128_s[:, :]),
                reads=[("gtk",), ("id128",)], writes=[PBK[1][0]], tag="tk_tr")
            S.op("dve", lambda V, tc=tc: V.tensor_copy(gfm[:, tc * 128:(tc + 1) * 128],
                                                       pb[1][0:8, 0:128]),
                 reads=[PBK[1][0]], writes=[("gfm",)], tag="tk_gfm")
        for e in range(NEXP if not FFN_ONLY else 0):
            bank, bkey = pb[4 + e % 2], PBK[4 + e % 2][0]
            S.op("pe", lambda P, b=bank, e=e: P.matmul(
                b[:, :], i8b[:, e * 128:(e + 1) * 128], gfm[:, :], start=True, stop=True),
                reads=[("gfm",), ("i8b",)], writes=[bkey], tag="gb_mm")
            S.op("act", lambda A, b=bank, e=e: A.activation(
                gateb[:, e, :], b[:, :], AF.Identity),
                reads=[bkey], writes=[("gateb", e)], tag="gb_ev")

        for e in range(NEXP):
            for ft in range(8):
                wp = (e * 8 + ft) % 2
                if FFN_NODMA:
                    pass
                else:
                    S.op("sp", lambda Y, e=e, ft=ft, wp=wp: Y.dma_start(
                    w1s[wp][:, :, :],
                    w1bf.ap()[e].rearrange("(c p) f -> p c f", p=128)[:, :, ft * 512:(ft + 1) * 512]),
                    reads=[("w1bf", e)], writes=[("w1s", wp)], dma=f"d_w1{'ab'[wp]}", tag="w1_ld")
                for fcl in range(4):
                    bank, bkey = pb[fcl % 2], PBK[fcl % 2][0]
                    fc = ft * 4 + fcl
                    for kc in range(8):
                        S.op("pe", lambda P, b=bank, wp=wp, fcl=fcl, kc=kc: P.matmul(
                            b[:, :], w1s[wp][:, kc, fcl * 128:(fcl + 1) * 128], ln2[:, kc, :],
                            start=(kc == 0), stop=(kc == 7)),
                            reads=[("w1s", wp), ("ln2",)], writes=[bkey], tag="h1_mm")
                    S.op("act", lambda A, b=bank, e=e, fc=fc: A.activation(
                        gsb[:, fc, :], b[:, :], AF.Gelu, bias=b1s[:, e, fc:fc + 1], scale=1.0),
                        reads=[bkey, ("b1",)], writes=[("gsb", fc)], tag="h1_gelu")
            for hc in range(8):
                wp = (e * 8 + hc) % 2
                if FFN_NODMA:
                    pass
                else:
                    S.op("sp", lambda Y, e=e, hc=hc, wp=wp: Y.dma_start(
                    w2s[wp][:, :, :],
                    w2bf.ap()[e].rearrange("(c p) o -> p c o", p=128)[:, :, hc * 128:(hc + 1) * 128]),
                    reads=[("w2bf", e)], writes=[("w2s", wp)], dma=f"d_w2{'ab'[wp]}", tag="w2_ld")
                bank, bkey = pb[2 + hc % 2], PBK[2 + hc % 2][0]
                for fc in range(32):
                    S.op("pe", lambda P, b=bank, wp=wp, fc=fc: P.matmul(
                        b[:, :], w2s[wp][:, fc, :], gsb[:, fc, :],
                        start=(fc == 0), stop=(fc == 31)),
                        reads=[("w2s", wp), ("gsb", fc)], writes=[bkey], tag="o2_mm")
                if e == 0:
                    S.op("dve", lambda V, b=bank, e=e, hc=hc: V.scalar_tensor_tensor(
                        acc[:, hc, :], b[:, :], b2s[:, e, hc:hc + 1], gateb[:, e, :],
                        ALU.add, ALU.mult),
                        reads=[bkey, ("b2",), ("gateb", e)], writes=[("acc", hc)], tag="o2_ev0")
                else:
                    j = hc % 2
                    S.op("dve", lambda V, b=bank, e=e, hc=hc, j=j: V.scalar_tensor_tensor(
                        tmp2f[j][:, :], b[:, :], b2s[:, e, hc:hc + 1], gateb[:, e, :],
                        ALU.add, ALU.mult),
                        reads=[bkey, ("b2",), ("gateb", e)], writes=[("o2t", j)], tag="o2_ev")
                    S.op("dve", lambda V, hc=hc, j=j: V.tensor_tensor(
                        acc[:, hc, :], acc[:, hc, :], tmp2f[j][:, :], ALU.add),
                        reads=[("o2t", j), ("acc", hc)], writes=[("acc", hc)], tag="o2_acc")

        for oc in range(8):
            S.op("dve", lambda V, oc=oc: V.tensor_tensor(
                x_s[:, oc, :], x_s[:, oc, :], acc[:, oc, :], ALU.add),
                reads=[("y",), ("acc", oc)], writes=[("y",)], tag="fin_add")
        S.op("gp", lambda G, t0=t0: G.dma_start(
            outT.ap().rearrange("(c p) t -> p c t", p=128)[:, :, t0:t0 + T],
            x_s[:, :, :]),
            reads=[("y",)], dma="d_out0", tag="p2_out")

    final_waits = {"d_out0": 16 * S.dma_count.get("d_out0", 0)}
    final_waits = {k: v for k, v in final_waits.items() if v}
    run_engine = S.finalize(eng_sems, dma_sems, final_waits)

    with nc.Block() as block, nc.allow_low_precision(
            reason="bf16 matmul operand staging; fp32 accumulation everywhere"):
        block.tensor(lambda eng: run_engine(eng, "pe"))
        block.scalar(lambda eng: run_engine(eng, "act"))
        block.vector(lambda eng: run_engine(eng, "dve"))
        block.sync(lambda eng: run_engine(eng, "sp"))
        block.gpsimd(lambda eng: run_engine(eng, "gp"))

    es2.close()
    es.close()
    return nc


def _host_constants():
    def fwht(v):
        shape = v.shape
        Pn = shape[-1]
        h = 1
        while h < Pn:
            v = v.reshape(shape[:-1] + (-1, 2 * h))
            a, b = v[..., :h], v[..., h:]
            v = np.concatenate([a + b, a - b], axis=-1).reshape(shape)
            h *= 2
        return v

    W64 = fwht(np.eye(64, dtype=np.float64))
    rowperm = np.concatenate([np.arange(0, 64, 2), np.arange(1, 64, 2)])
    perm = np.zeros(1024, dtype=np.int64)
    n = 0
    for g in range(4):
        for parity in range(2):
            for hp in range(4):
                for j in range(32):
                    perm[n] = (4 * g + hp) * 64 + 2 * j + parity
                    n += 1
    freqs = 10000.0 ** (-np.arange(0, 64, 2) / 64.0)
    tpos = np.arange(L)
    ang = np.where(tpos[None, :] >= 1,
                   (tpos[None, :] - 1) * freqs[np.arange(128) % 32, None], 0.0)
    cos_t = np.cos(ang)
    sin_t = np.sin(ang)
    cos_t[:, 0] = 1.0
    sin_t[:, 0] = 0.0

    i4b = np.zeros((128, 4), dtype=np.float32)
    for d in range(128):
        i4b[d, d // 32] = 1.0
    i4q = np.zeros((4, 4 * 128), dtype=np.float32)
    for hp in range(4):
        i4q[hp, hp * 128:(hp + 1) * 128] = 1.0
    i8b = np.zeros((8, 8 * 128), dtype=np.float32)
    for e in range(8):
        i8b[e, e * 128:(e + 1) * 128] = 1.0

    return {
        "W64": W64, "rowperm": rowperm, "perm": perm,
        "costab": cos_t.astype(BF), "sintab": sin_t.astype(BF),
        "wm64": W64.astype(np.float32),
        "hpT": np.ascontiguousarray(W64[rowperm, :].T).astype(np.float32),
        "onescol": np.ones((128, 1), dtype=BF),
        "onesrow": np.ones((1, 128), dtype=BF),
        "ind4b": i4b.astype(BF), "ind4q": i4q.astype(BF), "ind8b": i8b.astype(BF),
        "id8": np.eye(8, dtype=np.float32), "id128": np.eye(128, dtype=np.float32),
        "c4096": np.full((1, 64), 1.0 / 4096.0, dtype=np.float32),
        "actconst": np.tile(np.array([[LNEPS, -0.5]], dtype=np.float32), (128, 1)),
    }


_CACHE = {}


def kernel(L0_attn_x, L0_attn_w_qkv, L0_attn_w_out, L0_attn_sorf_signs,
           L0_attn_sorf_scales, L0_ffn_router_w, L0_ffn_w1, L0_ffn_b1,
           L0_ffn_w2, L0_ffn_b2):
    x = np.ascontiguousarray(np.asarray(L0_attn_x, dtype=np.float32))
    w_qkv = np.asarray(L0_attn_w_qkv, dtype=np.float32)
    w_out = np.asarray(L0_attn_w_out, dtype=np.float32)
    signs = np.asarray(L0_attn_sorf_signs, dtype=np.float32)
    scales = np.asarray(L0_attn_sorf_scales, dtype=np.float32)
    router_w = np.asarray(L0_ffn_router_w, dtype=np.float32)
    w1 = np.asarray(L0_ffn_w1, dtype=np.float32)
    b1 = np.asarray(L0_ffn_b1, dtype=np.float32)
    w2 = np.asarray(L0_ffn_w2, dtype=np.float32)
    b2 = np.asarray(L0_ffn_b2, dtype=np.float32)

    cst = _host_constants()
    perm = cst["perm"]

    wqkvT = np.ascontiguousarray(w_qkv.T)
    wq = wqkvT[:, 0:1024][:, perm]
    wk = wqkvT[:, 1024:2048][:, perm]
    wv = wqkvT[:, 2048:3072]
    WQKVT = np.ascontiguousarray(np.concatenate([wq, wk, wv], axis=1))

    shared = {
        "wqkvT": WQKVT,
        "woutT": np.ascontiguousarray(w_out.T),
        "w1T": np.ascontiguousarray(w1.transpose(0, 2, 1)),
        "w2T": np.ascontiguousarray(w2.transpose(0, 2, 1)),
        "b1T": np.ascontiguousarray(b1.reshape(8, 32, 128).transpose(0, 2, 1)),
        "b2T": np.ascontiguousarray(b2.reshape(8, 8, 128).transpose(0, 2, 1)),
        "routerT": np.ascontiguousarray(router_w.T),
        "costab": cst["costab"], "sintab": cst["sintab"],
        "wm64": cst["wm64"], "hpT": cst["hpT"],
        "s1pT": np.ascontiguousarray(
            signs[:, :, 0, :][..., cst["rowperm"]].reshape(64, 64).T).astype(np.float32),
        "s2T": np.ascontiguousarray(signs[:, :, 1, :].reshape(64, 64).T).astype(np.float32),
        "s3T": np.ascontiguousarray(signs[:, :, 2, :].reshape(64, 64).T).astype(np.float32),
        "gflat": np.ascontiguousarray(scales.reshape(1, 64 * 64)).astype(np.float32),
        "onescol": cst["onescol"], "onesrow": cst["onesrow"],
        "ind4b": cst["ind4b"], "ind4q": cst["ind4q"], "ind8b": cst["ind8b"],
        "id8": cst["id8"], "id128": cst["id128"], "c4096": cst["c4096"],
        "actconst": cst["actconst"],
    }

    if "nc" not in _CACHE:
        _CACHE["nc"] = build_program()
    nc = _CACHE["nc"]

    in_maps = []
    for c in range(C):
        m = dict(shared)
        m["xT"] = np.ascontiguousarray(x[c].T)
        in_maps.append(m)

    res = run_bass_kernel_spmd(nc, in_maps, core_ids=list(range(C)))
    out = np.empty((C, L, E), dtype=np.float32)
    for c in range(C):
        out[c] = res.results[c]["outT"].T
    return out


# version 04 (final)
# speedup vs baseline: 2.5199x; 1.0212x over previous
import contextlib
import numpy as np
import ml_dtypes

import concourse.bass as bass
import concourse.mybir as mybir
from concourse.bass_utils import run_bass_kernel_spmd
from concourse.alu_op_type import AluOpType as ALU

F32 = mybir.dt.float32
BF16 = mybir.dt.bfloat16
AF = mybir.ActivationFunctionType
AX = mybir.AxisListType.X
BF = ml_dtypes.bfloat16

import os
C, L, E = 8, 8192, 1024
NH, HD = 16, 64
T = 512
NT = L // T
NT_EMIT = int(os.environ.get("KERNEL_NT", str(NT)))
P1_ON = os.environ.get("KERNEL_P1", "1") == "1"
P2_ON = os.environ.get("KERNEL_P2", "1") == "1"
FFN_ONLY = os.environ.get("KERNEL_FFN_ONLY", "0") == "1"
FFN_NODMA = os.environ.get("KERNEL_FFN_NODMA", "0") == "1"
TC = T // 128
NEXP, FF = 8, 4096
LN16 = float(np.log(16.0))
EPS = 1e-6
LNEPS = 1e-5

ENGINES = ("pe", "act", "dve", "pool", "sp", "gp")


class _Op:
    __slots__ = ("eng", "fn", "dma", "dma_ord", "inc", "pos", "deps", "dma_waits",
                 "tag", "_bar_dma")

    def __init__(self, eng, fn, dma, pos, tag):
        self.eng = eng
        self.fn = fn
        self.dma = dma
        self.dma_ord = 0
        self.inc = False
        self.pos = pos
        self.deps = []
        self.dma_waits = {}
        self.tag = tag
        self._bar_dma = None


class Sched:

    def __init__(self):
        self.ops = {e: [] for e in ENGINES}
        self.reg_w = {}
        self.reg_r = {}
        self.dma_count = {}
        self.bar_pending = {e: None for e in ENGINES}

    def op(self, eng, fn, reads=(), writes=(), dma=None, tag=""):
        o = _Op(eng, fn, dma, len(self.ops[eng]), tag)
        bp = self.bar_pending.get(eng)
        if bp is not None:
            prods, dma_snap = bp
            for p in prods:
                if p.eng != eng:
                    o.deps.append(p)
            o._bar_dma = dma_snap
            self.bar_pending[eng] = None

        def add(p):
            if p is None or p is o:
                return
            if p.eng == "pe" and eng == "pe" and p.dma is None and dma is None:
                return
            if p.dma is not None:
                o.dma_waits[p.dma] = max(o.dma_waits.get(p.dma, 0),
                                         16 * self.dma_count.get(p.dma, 0))
            else:
                o.deps.append(p)
                p.inc = True

        for k in reads:
            add(self.reg_w.get(k))
        for k in writes:
            add(self.reg_w.get(k))
            rd = self.reg_r.get(k)
            if rd:
                for r in rd.values():
                    add(r)
        for k in writes:
            self.reg_w[k] = o
            self.reg_r[k] = {}
        for k in reads:
            self.reg_r.setdefault(k, {})[eng] = o
        if dma is not None:
            self.dma_count[dma] = self.dma_count.get(dma, 0) + 1
            o.dma_ord = self.dma_count[dma]
        self.ops[eng].append(o)
        return o

    def barrier(self):
        prods = []
        for e in ENGINES:
            for o in reversed(self.ops[e]):
                if o.dma is None:
                    o.inc = True
                    prods.append(o)
                    break
        dma_snap = dict(self.dma_count)
        for e in ENGINES:
            self.bar_pending[e] = (prods, dma_snap)

    def finalize(self, eng_sems, dma_sems, final_dma_waits):
        val = {}
        for e in ENGINES:
            cnt = 0
            for o in self.ops[e]:
                if o.inc:
                    cnt += 1
                val[o] = cnt

        def run_engine(engobj, e):
            marks = {}
            for o in self.ops[e]:
                waits = {}
                if o._bar_dma:
                    for dn, dc in o._bar_dma.items():
                        waits[("d", dn)] = max(waits.get(("d", dn), 0), 16 * dc)
                for dn, v in o.dma_waits.items():
                    waits[("d", dn)] = max(waits.get(("d", dn), 0), v)
                for p in o.deps:
                    waits[("e", p.eng)] = max(waits.get(("e", p.eng), 0), val[p])
                for key, v in waits.items():
                    if marks.get(key, 0) >= v:
                        continue
                    marks[key] = v
                    sem = dma_sems[key[1]] if key[0] == "d" else eng_sems[key[1]]
                    engobj.wait_ge(sem, v)
                inst = o.fn(engobj)
                if o.dma is not None:
                    inst.then_inc(dma_sems[o.dma], 16)
                elif o.inc:
                    if inst is None:
                        inst = engobj.nop()
                    inst.then_inc(eng_sems[e], 1)
            if e == "gp":
                for dn, v in final_dma_waits.items():
                    engobj.wait_ge(dma_sems[dn], v)
        return run_engine


def build_program():
    nc = bass.Bass("TRN2", target_bir_lowering=False)
    es = contextlib.ExitStack()

    def din(name, shape, dt):
        return nc.dram_tensor(name, list(shape), dt, kind="ExternalInput")

    xT = din("xT", (E, L), F32)
    wqkvT = din("wqkvT", (E, 3 * E), F32)
    woutT = din("woutT", (E, E), F32)
    w1T = din("w1T", (NEXP, E, FF), F32)
    w2T = din("w2T", (NEXP, FF, E), F32)
    b1T = din("b1T", (NEXP, 128, FF // 128), F32)
    b2T = din("b2T", (NEXP, 128, E // 128), F32)
    routerT = din("routerT", (E, NEXP), F32)
    costab = din("costab", (128, L), BF16)
    sintab = din("sintab", (128, L), BF16)
    wm64 = din("wm64", (64, 64), F32)
    hpT = din("hpT", (64, 64), F32)
    s1pT = din("s1pT", (64, 64), F32)
    s2T = din("s2T", (64, 64), F32)
    s3T = din("s3T", (64, 64), F32)
    gflat = din("gflat", (1, 64 * 64), F32)
    onescol = din("onescol", (128, 1), BF16)
    onesrow = din("onesrow", (1, 128), BF16)
    ind4b = din("ind4b", (128, 4), BF16)
    ind4q = din("ind4q", (4, 4 * 128), BF16)
    ind8b = din("ind8b", (8, 8 * 128), BF16)
    id8 = din("id8", (8, 8), F32)
    id128 = din("id128", (128, 128), F32)
    c4096 = din("c4096", (1, 64), F32)
    actconst = din("actconst", (128, 3), F32)

    outT = nc.dram_tensor("outT", [E, L], F32, kind="ExternalOutput")

    w1bf = nc.dram_tensor("w1bf", [NEXP, E, FF], BF16, kind="Internal")
    w2bf = nc.dram_tensor("w2bf", [NEXP, FF, E], BF16, kind="Internal")
    qphi_d = nc.dram_tensor("qphi_d", [4096, L], BF16, kind="Internal")

    eng_sems = {e: es.enter_context(nc.semaphore(f"s_{e}")) for e in ENGINES}
    dma_names = ["d_init", "d_wc", "d_mpl", "d_cs0", "d_cs1",
                 "d_xb0", "d_xb1", "d_x0", "d_x1", "d_qsp0", "d_qsp1",
                 "d_qh0", "d_qh1", "d_w1a", "d_w1b", "d_w2a", "d_w2b",
                 "d_out0", "d_out1"]
    dma_sems = {n: es.enter_context(nc.semaphore(n)) for n in dma_names}

    S = Sched()

    pb = [es.enter_context(nc.psum_tensor(f"pb{i}", [128, 512], F32)) for i in range(8)]
    PBK = [(f"pb{i}",) for i in range(8)]
    ones_c = es.enter_context(nc.sbuf_tensor("ones_c", [128, 1], BF16))
    ones_r = es.enter_context(nc.sbuf_tensor("ones_r", [1, 128], BF16))
    i4b = es.enter_context(nc.sbuf_tensor("i4b", [128, 4], BF16))
    i4q = es.enter_context(nc.sbuf_tensor("i4q", [4, 4 * 128], BF16))
    i8b = es.enter_context(nc.sbuf_tensor("i8b", [8, 8 * 128], BF16))
    id8_s = es.enter_context(nc.sbuf_tensor("id8_s", [8, 8], F32))
    id128_s = es.enter_context(nc.sbuf_tensor("id128_s", [128, 128], F32))
    Sbf = es.enter_context(nc.sbuf_tensor("Sbf", [128, NH, 130], BF16))
    acst = es.enter_context(nc.sbuf_tensor("acst", [128, 3], F32))

    def g_init(dst_ap, src_ap, tag, writes):
        S.op("gp", lambda G, d=dst_ap, s=src_ap: G.dma_start(d, s),
             writes=writes, dma="d_init", tag=tag)

    g_init(ones_c[:, :], onescol.ap(), "ld_onesc", [("ones_c",)])
    g_init(ones_r[:, :], onesrow.ap(), "ld_onesr", [("ones_r",)])
    g_init(i4b[:, :], ind4b.ap(), "ld_i4b", [("i4b",)])
    g_init(i4q[:, :], ind4q.ap(), "ld_i4q", [("i4q",)])
    g_init(i8b[:, :], ind8b.ap(), "ld_i8b", [("i8b",)])
    g_init(id8_s[:, :], id8.ap(), "ld_id8", [("id8",)])
    g_init(id128_s[:, :], id128.ap(), "ld_id128", [("id128",)])
    g_init(acst[:, :], actconst.ap(), "ld_acst", [("acst",)])

    es1 = contextlib.ExitStack()
    wqkv_s = es1.enter_context(nc.sbuf_tensor("wqkv_s", [128, 8, 3 * E], BF16))
    Mble = [es1.enter_context(nc.sbuf_tensor(f"Mble{g}", [128, 1024], BF16)) for g in range(4)]
    Mblo = [es1.enter_context(nc.sbuf_tensor(f"Mblo{g}", [128, 1024], BF16)) for g in range(4)]
    cos_t = [es1.enter_context(nc.sbuf_tensor(f"cos{p}", [128, T], BF16)) for p in range(2)]
    sin_t = [es1.enter_context(nc.sbuf_tensor(f"sin{p}", [128, T], BF16)) for p in range(2)]
    xbf = es1.enter_context(nc.sbuf_tensor("xbf", [128, 8, T], BF16))
    ln1 = es1.enter_context(nc.sbuf_tensor("ln1", [128, 8, T], BF16))
    qbf = es1.enter_context(nc.sbuf_tensor("qbf", [128, 8, T], BF16))
    kbf = es1.enter_context(nc.sbuf_tensor("kbf", [128, 8, T], BF16))
    ropeEk = es1.enter_context(nc.sbuf_tensor("ropeEk", [128, T], BF16))
    ropeOk = es1.enter_context(nc.sbuf_tensor("ropeOk", [128, T], BF16))
    ropeEq = es1.enter_context(nc.sbuf_tensor("ropeEq", [128, T], BF16))
    ropeOq = es1.enter_context(nc.sbuf_tensor("ropeOq", [128, T], BF16))
    sq_s = [es1.enter_context(nc.sbuf_tensor(f"sq{j}", [128, T], BF16)) for j in range(2)]
    xsq_s = [es1.enter_context(nc.sbuf_tensor(f"xsq{j}", [128, T], BF16)) for j in range(2)]
    tmp_bf = [es1.enter_context(nc.sbuf_tensor(f"tmpbf{j}", [128, T], BF16)) for j in range(2)]
    tmp_f = [es1.enter_context(nc.sbuf_tensor(f"tmpf{j}", [128, T], F32)) for j in range(2)]
    kphi_s = es1.enter_context(nc.sbuf_tensor("kphi_s", [128, TC, 1024], BF16))
    qstage = [es1.enter_context(nc.sbuf_tensor(f"qstage{j}", [128, 8, T], BF16)) for j in range(2)]
    vtok = es1.enter_context(nc.sbuf_tensor("vtok", [128, TC, NH, 65], BF16))
    xnk_s = es1.enter_context(nc.sbuf_tensor("xnk_s", [128, 4, 16], F32))
    xnq_s = es1.enter_context(nc.sbuf_tensor("xnq_s", [4, 4, T], BF16))
    row1 = es1.enter_context(nc.sbuf_tensor("row1", [1, 3 * 512], F32))
    rowb1 = es1.enter_context(nc.sbuf_tensor("rowb1", [1, 2 * 512], BF16))
    Sacc = es1.enter_context(nc.sbuf_tensor("Sacc", [128, NH, 130], F32))

    esM = contextlib.ExitStack()
    wm_s = esM.enter_context(nc.sbuf_tensor("wm_s", [64, 64], F32))
    hpT_s = esM.enter_context(nc.sbuf_tensor("hpT_s", [64, 64], F32))
    s1p_s = esM.enter_context(nc.sbuf_tensor("s1p_s", [64, 64], F32))
    s2_s = esM.enter_context(nc.sbuf_tensor("s2_s", [64, 64], F32))
    s3_s = esM.enter_context(nc.sbuf_tensor("s3_s", [64, 64], F32))
    gfl_s = esM.enter_context(nc.sbuf_tensor("gfl_s", [1, 1024], F32))
    gb_s = esM.enter_context(nc.sbuf_tensor("gb_s", [64, 1024], F32))
    mstage = esM.enter_context(nc.sbuf_tensor("mstage", [64, 256], F32))
    mtmp = [esM.enter_context(nc.sbuf_tensor(f"mtmp{j}", [64, 64], F32)) for j in range(3)]
    c4096_s = esM.enter_context(nc.sbuf_tensor("c4096_s", [1, 64], F32))

    g_init(wm_s[:, :], wm64.ap(), "ld_wm", [("wm",)])
    g_init(hpT_s[:, :], hpT.ap(), "ld_hpT", [("hpT",)])
    g_init(s1p_s[:, :], s1pT.ap(), "ld_s1p", [("s1p",)])
    g_init(s2_s[:, :], s2T.ap(), "ld_s2", [("s2",)])
    g_init(s3_s[:, :], s3T.ap(), "ld_s3", [("s3",)])
    g_init(c4096_s[:, :], c4096.ap(), "ld_c4096", [("c4096",)])
    if os.environ.get("KERNEL_WQKV", "1") == "1":
        g_init(wqkv_s[:, :, :], wqkvT.ap().rearrange("(c p) o -> p c o", p=128),
               "ld_wqkv", [("wqkv",)])

    S.op("dve", lambda V: V.memset(vtok[:, :, :, 64:65], 1.0), writes=[("vtok_ones",)], tag="ms_v")
    S.op("dve", lambda V: V.memset(Sacc[:, :, :], 0.0),
         writes=[("Sacc", h) for h in range(NH)], tag="ms_S")
    for g in range(4):
        S.op("dve", lambda V, g=g: V.memset(Mble[g][:, :], 0.0), writes=[("Mble0", g)], tag="ms_Me")
        S.op("dve", lambda V, g=g: V.memset(Mblo[g][:, :], 0.0), writes=[("Mblo0", g)], tag="ms_Mo")

    for b in range(4 if os.environ.get("KERNEL_MB", "1") == "1" else 0):
        S.op("gp", lambda G, b=b: G.dma_start(gfl_s[:, :], gflat.ap()[:, b * 1024:(b + 1) * 1024]),
             writes=[("gfl",)], dma="d_init", tag="ld_gfl")
        for j in range(2 if "g" in os.environ.get("KERNEL_MPARTS", "gmp") else 0):
            bank, bkey = pb[j], PBK[j]
            S.op("pe", lambda P, bk=bank, j=j: P.matmul(
                bk[0:64, :], c4096_s[0:1, 0:64], gfl_s[0:1, j * 512:(j + 1) * 512],
                start=True, stop=True),
                reads=[("c4096",), ("gfl",)], writes=[bkey], tag="gbmm")
            S.op("dve", lambda V, bk=bank, j=j: V.tensor_copy(
                gb_s[:, j * 512:(j + 1) * 512], bk[0:64, :]),
                reads=[bkey], writes=[("gb",)], tag="gbcp")
        MPARTS = os.environ.get("KERNEL_MPARTS", "gmp")
        for hh in range(4 if "m" in MPARTS else 0):
            h = b * 4 + hh
            g, hp = h // 4, h % 4
            for r in range(4):
                idx = 4 * h + r
                li = idx - b * 16
                pbank, pkey = pb[2 + (r % 2)], PBK[2 + (r % 2)]
                S.op("dve", lambda V, i=idx: V.tensor_scalar(
                    mtmp[0][:, :], wm_s[:, :], s3_s[:, i:i + 1], None, ALU.mult),
                    reads=[("wm",), ("s3",)], writes=[("mt0",)], tag="m_s3")
                S.op("dve", lambda V, li=li: V.tensor_tensor(
                    mtmp[1][:, :], mtmp[0][:, :], gb_s[:, li * 64:(li + 1) * 64], ALU.mult),
                    reads=[("mt0",), ("gb",)], writes=[("mt1",)], tag="m_g")
                S.op("pe", lambda P, bk=pbank: P.matmul(bk[0:64, 0:64], wm_s[:, :], mtmp[1][:, :],
                                                        start=True, stop=True),
                     reads=[("wm",), ("mt1",)], writes=[pkey], tag="m_mm1")
                S.op("dve", lambda V, bk=pbank, i=idx: V.tensor_scalar(
                    mtmp[2][:, :], bk[0:64, 0:64], s2_s[:, i:i + 1], None, ALU.mult),
                    reads=[pkey, ("s2",)], writes=[("mt2",)], tag="m_s2")
                S.op("pe", lambda P, bk=pbank: P.matmul(bk[0:64, 0:64], hpT_s[:, :], mtmp[2][:, :],
                                                        start=True, stop=True),
                     reads=[("hpT",), ("mt2",)], writes=[pkey], tag="m_mm2")
                S.op("dve", lambda V, bk=pbank, i=idx, r=r: V.tensor_scalar(
                    mstage[:, r * 64:(r + 1) * 64], bk[0:64, 0:64], s1p_s[:, i:i + 1],
                    None, ALU.mult),
                    reads=[pkey, ("s1p",)], writes=[("Mst", r)], tag="m_s1")
            if "p" not in MPARTS:
                continue
            rkeys = [("Mst", r) for r in range(4)]
            S.op("gp", lambda G, g=g, hp=hp: G.dma_start(
                Mble[g][32 * hp:32 * hp + 32, 256 * hp:256 * hp + 256],
                mstage[0:32, :]),
                reads=rkeys + [("Mble0", g)], writes=[("Mble", g, hp)], dma="d_mpl", tag="mple")
            S.op("gp", lambda G, g=g, hp=hp: G.dma_start(
                Mblo[g][32 * hp:32 * hp + 32, 256 * hp:256 * hp + 256],
                mstage[32:64, :]),
                reads=rkeys + [("Mblo0", g)], writes=[("Mblo", g, hp)], dma="d_mpl", tag="mplo")

    S.barrier()
    esM.close()

    MKE = [("Mble", g, hp) for g in range(4) for hp in range(4)]

    def MKO(g):
        return [("Mblo", g, hp) for hp in range(4)]

    def MKEg(g):
        return [("Mble", g, hp) for hp in range(4)]

    def emit_ln(xkey, xtile, lnkey, lntile, rowf, rowb, kp, tag):
        for c in range(8):
            S.op("act", lambda A, c=c: A.activation(xsq_s[c % 2][:, :], xtile[:, c, :], AF.Square),
                 reads=[xkey], writes=[(kp + "xsq", c % 2)], tag=f"{tag}_sq")
            S.op("pe", lambda P, c=c: P.matmul(
                pb[6][0:1, :], ones_c[:, :], xtile[:, c, :], start=(c == 0), stop=(c == 7)),
                reads=[xkey, ("ones_c",)], writes=[PBK[6][0]], tag=f"{tag}_sum")
            S.op("pe", lambda P, c=c: P.matmul(
                pb[7][0:1, :], ones_c[:, :], xsq_s[c % 2][:, :], start=(c == 0), stop=(c == 7)),
                reads=[(kp + "xsq", c % 2), ("ones_c",)], writes=[PBK[7][0]], tag=f"{tag}_ssq")
        S.op("dve", lambda V: V.tensor_scalar(rowb[0:1, 0:512], pb[6][0:1, :],
                                              1.0 / 1024.0, None, ALU.mult),
             reads=[PBK[6][0]], writes=[(kp + "meanb",)], tag=f"{tag}_mean")
        S.op("dve", lambda V: V.tensor_tensor(rowf[0:1, 0:512], rowb[0:1, 0:512],
                                              rowb[0:1, 0:512], ALU.mult),
             reads=[(kp + "meanb",)], writes=[(kp + "msq",)], tag=f"{tag}_msq")
        S.op("dve", lambda V: V.scalar_tensor_tensor(
            rowf[0:1, 512:1024], pb[7][0:1, :], 1.0 / 1024.0, rowf[0:1, 0:512],
            ALU.mult, ALU.subtract),
            reads=[PBK[7][0], (kp + "msq",)], writes=[(kp + "var",)], tag=f"{tag}_var")
        S.op("act", lambda A: A.activation(rowf[0:1, 1024:1536], rowf[0:1, 512:1024],
                                           AF.Ln, bias=acst[0:1, 0:1], scale=1.0),
             reads=[(kp + "var",), ("acst",)], writes=[(kp + "lnv",)], tag=f"{tag}_ln")
        S.op("act", lambda A: A.activation(rowb[0:1, 512:1024], rowf[0:1, 1024:1536],
                                           AF.Exp, bias=0.0, scale=acst[0:1, 1:2]),
             reads=[(kp + "lnv",), ("acst",)], writes=[(kp + "rstd",)], tag=f"{tag}_rstd")
        S.op("pe", lambda P: P.matmul(pb[6][:, :], ones_r[0:1, :], rowb[0:1, 0:512],
                                      start=True, stop=True),
             reads=[(kp + "meanb",), ("ones_r",)], writes=[PBK[6][0]], tag=f"{tag}_bm")
        S.op("pe", lambda P: P.matmul(pb[7][:, :], ones_r[0:1, :], rowb[0:1, 512:1024],
                                      start=True, stop=True),
             reads=[(kp + "rstd",), ("ones_r",)], writes=[PBK[7][0]], tag=f"{tag}_br")
        for c in range(8):
            S.op("dve", lambda V, c=c: V.tensor_tensor(
                tmp_f[c % 2][:, :], xtile[:, c, :], pb[6][:, :], ALU.subtract),
                reads=[xkey, PBK[6][0]], writes=[(kp + "tmp", c % 2)], tag=f"{tag}_sub")
            S.op("dve", lambda V, c=c: V.tensor_tensor(
                lntile[:, c, :], tmp_f[c % 2][:, :], pb[7][:, :], ALU.mult),
                reads=[(kp + "tmp", c % 2), PBK[7][0]], writes=[lnkey], tag=f"{tag}_mul")

    for i in range(NT_EMIT if P1_ON else 0):
        par = i % 2
        t0 = i * T
        S.op("gp", lambda G, t0=t0: G.dma_start(
            xbf[:, :, :],
            xT.ap().rearrange("(c p) t -> p c t", p=128)[:, :, t0:t0 + T]),
            writes=[("xbf",)], dma="d_xb0", tag="p1_x")
        S.op("sp", lambda Y, par=par, t0=t0: Y.dma_start(
            cos_t[par][:, :], costab.ap()[:, t0:t0 + T]),
            writes=[("cos", par)], dma=f"d_cs{par}", tag="p1_cos")
        S.op("sp", lambda Y, par=par, t0=t0: Y.dma_start(
            sin_t[par][:, :], sintab.ap()[:, t0:t0 + T]),
            writes=[("sin", par)], dma=f"d_cs{par}", tag="p1_sin")
        wc_list = [i] if NT_EMIT >= 16 else (list(range(16)) if i == 0 else [])
        for ci in wc_list:
            if ci < 8:
                S.op("gp", lambda G, e=ci: G.dma_start(w1bf.ap()[e], w1T.ap()[e]),
                     writes=[("w1bf", ci)], dma="d_wc", tag="wcast1")
            else:
                S.op("gp", lambda G, e=ci - 8: G.dma_start(w2bf.ap()[e], w2T.ap()[e]),
                     writes=[("w2bf", ci - 8)], dma="d_wc", tag="wcast2")

        emit_ln(("xbf",), xbf, ("ln1",), ln1, row1, rowb1, "l1", "ln1")

        for oc in range(16):
            bank, bkey = pb[oc % 2], PBK[oc % 2]
            dst = qbf if oc < 8 else kbf
            dkey = ("qbf", oc) if oc < 8 else ("kbf", oc - 8)
            for kc in range(8):
                S.op("pe", lambda P, b=bank, oc=oc, kc=kc: P.matmul(
                    b[:, :], wqkv_s[:, kc, oc * 128:(oc + 1) * 128], ln1[:, kc, :],
                    start=(kc == 0), stop=(kc == 7)),
                    reads=[("wqkv",), ("ln1",)], writes=[bkey], tag="qk_mm")
            S.op("act", lambda A, b=bank, d=dst, c=oc % 8: A.activation(
                d[:, c, :], b[:, :], AF.Identity),
                reads=[bkey], writes=[dkey], tag="qk_ev")
        for tc in range(TC):
            for vh in range(2):
                j = (tc * 2 + vh) % 2
                bank, bkey = pb[j], PBK[j]
                for kc in range(8):
                    S.op("pe", lambda P, b=bank, tc=tc, vh=vh, kc=kc: P.matmul(
                        b[:, :], ln1[:, kc, tc * 128:(tc + 1) * 128],
                        wqkv_s[:, kc, 2048 + vh * 512:2048 + (vh + 1) * 512],
                        start=(kc == 0), stop=(kc == 7)),
                        reads=[("wqkv",), ("ln1",)], writes=[bkey], tag="v_mm")
                S.op("act", lambda A, b=bank, tc=tc, vh=vh: A.activation(
                    vtok[:, tc, vh * 8:(vh + 1) * 8, 0:64],
                    b[:, :].rearrange("p (h d) -> p h d", d=64), AF.Identity),
                    reads=[bkey], writes=[("vtok", tc, vh)], tag="v_ev")

        for g in range(4):
            S.op("act", lambda A, g=g: A.activation(sq_s[0][:, :], kbf[:, 2 * g, :], AF.Square),
                 reads=[("kbf", 2 * g)], writes=[("sqE",)], tag="ksqE")
            S.op("act", lambda A, g=g: A.activation(sq_s[1][:, :], kbf[:, 2 * g + 1, :], AF.Square),
                 reads=[("kbf", 2 * g + 1)], writes=[("sqO",)], tag="ksqO")
            for tc in range(TC):
                S.op("pe", lambda P, tc=tc: P.matmul(
                    pb[2][:, tc * 4:tc * 4 + 4], sq_s[0][:, tc * 128:(tc + 1) * 128],
                    i4b[:, :], start=(tc == 0), stop=False),
                    reads=[("sqE",), ("i4b",)], writes=[PBK[2][0]], tag="xnk_mmE")
                S.op("pe", lambda P, tc=tc: P.matmul(
                    pb[2][:, tc * 4:tc * 4 + 4], sq_s[1][:, tc * 128:(tc + 1) * 128],
                    i4b[:, :], start=False, stop=(tc == 3)),
                    reads=[("sqO",), ("i4b",)], writes=[PBK[2][0]], tag="xnk_mmO")
            S.op("dve", lambda V, g=g: V.tensor_scalar(
                xnk_s[:, g, :], pb[2][:, 0:16], -0.5, -LN16, ALU.mult, ALU.add),
                reads=[PBK[2][0]], writes=[("xnk", g)], tag="xnk_ev")
            S.op("act", lambda A, g=g: A.activation(sq_s[0][:, :], qbf[:, 2 * g, :], AF.Square),
                 reads=[("qbf", 2 * g)], writes=[("sqE",)], tag="qsqE")
            S.op("act", lambda A, g=g: A.activation(sq_s[1][:, :], qbf[:, 2 * g + 1, :], AF.Square),
                 reads=[("qbf", 2 * g + 1)], writes=[("sqO",)], tag="qsqO")
            S.op("pe", lambda P: P.matmul(pb[2][0:4, :], i4b[:, :], sq_s[0][:, :],
                                          start=True, stop=False),
                 reads=[("sqE",), ("i4b",)], writes=[PBK[2][0]], tag="xnq_mmE")
            S.op("pe", lambda P: P.matmul(pb[2][0:4, :], i4b[:, :], sq_s[1][:, :],
                                          start=False, stop=True),
                 reads=[("sqO",), ("i4b",)], writes=[PBK[2][0]], tag="xnq_mmO")
            S.op("dve", lambda V, g=g: V.tensor_scalar(
                xnq_s[:, g, :], pb[2][0:4, :], -0.5, -LN16, ALU.mult, ALU.add),
                reads=[PBK[2][0]], writes=[("xnq", g)], tag="xnq_ev")

            for (src, k0, k1, dE, dO, dkE, dkO) in (
                (kbf, ("kbf", 2 * g), ("kbf", 2 * g + 1), ropeEk, ropeOk, ("rEk",), ("rOk",)),
                (qbf, ("qbf", 2 * g), ("qbf", 2 * g + 1), ropeEq, ropeOq, ("rEq",), ("rOq",)),
            ):
                S.op("dve", lambda V, s=src, g=g, par=par: V.tensor_tensor(
                    tmp_bf[0][:, :], s[:, 2 * g, :], cos_t[par][:, :], ALU.mult),
                    reads=[k0, ("cos", par)], writes=[("rt0",)], tag="rp1")
                S.op("dve", lambda V, s=src, g=g, par=par: V.tensor_tensor(
                    tmp_bf[1][:, :], s[:, 2 * g + 1, :], sin_t[par][:, :], ALU.mult),
                    reads=[k1, ("sin", par)], writes=[("rt1",)], tag="rp2")
                S.op("dve", lambda V, d=dE: V.tensor_tensor(
                    d[:, :], tmp_bf[0][:, :], tmp_bf[1][:, :], ALU.subtract),
                    reads=[("rt0",), ("rt1",)], writes=[dkE], tag="rp3")
                S.op("dve", lambda V, s=src, g=g, par=par: V.tensor_tensor(
                    tmp_bf[0][:, :], s[:, 2 * g, :], sin_t[par][:, :], ALU.mult),
                    reads=[k0, ("sin", par)], writes=[("rt0",)], tag="rp4")
                S.op("dve", lambda V, s=src, g=g, par=par: V.tensor_tensor(
                    tmp_bf[1][:, :], s[:, 2 * g + 1, :], cos_t[par][:, :], ALU.mult),
                    reads=[k1, ("cos", par)], writes=[("rt1",)], tag="rp5")
                S.op("dve", lambda V, d=dO: V.tensor_tensor(
                    d[:, :], tmp_bf[0][:, :], tmp_bf[1][:, :], ALU.add),
                    reads=[("rt0",), ("rt1",)], writes=[dkO], tag="rp6")

            for tc in range(TC):
                bA, bB = (pb[3], pb[4]) if tc % 2 == 0 else (pb[5], pb[6])
                kA, kB = (PBK[3][0], PBK[4][0]) if tc % 2 == 0 else (PBK[5][0], PBK[6][0])
                for half, (bank, bkey) in enumerate(((bA, kA), (bB, kB))):
                    S.op("pe", lambda P, b=bank, g=g, tc=tc, hf=half: P.matmul(
                        b[:, :], ropeEk[:, tc * 128:(tc + 1) * 128],
                        Mble[g][:, hf * 512:(hf + 1) * 512], start=True, stop=False),
                        reads=[("rEk",)] + MKEg(g), writes=[bkey], tag="phk_mmE")
                    S.op("pe", lambda P, b=bank, g=g, tc=tc, hf=half: P.matmul(
                        b[:, :], ropeOk[:, tc * 128:(tc + 1) * 128],
                        Mblo[g][:, hf * 512:(hf + 1) * 512], start=False, stop=True),
                        reads=[("rOk",)] + MKO(g), writes=[bkey], tag="phk_mmO")
                for hp in range(4):
                    bank, bkey = (bA, kA) if hp < 2 else (bB, kB)
                    off = (hp % 2) * 256
                    S.op("act", lambda A, b=bank, g=g, tc=tc, hp=hp, off=off: A.activation(
                        kphi_s[:, tc, hp * 256:(hp + 1) * 256], b[:, off:off + 256],
                        AF.Exp, bias=xnk_s[:, g, tc * 4 + hp:tc * 4 + hp + 1], scale=1.0),
                        reads=[bkey, ("xnk", g)], writes=[("kphi", tc, hp)], tag="phk_exp")
            for hp in range(4):
                h = 4 * g + hp
                n = 0
                for mc in range(2):
                    for tc in range(TC):
                        S.op("pe", lambda P, tc=tc, hp=hp, mc=mc, h=h,
                             first=(n == 0), last=(n == 7): P.matmul(
                            pb[7][:, mc * 65:mc * 65 + 65],
                            kphi_s[:, tc, hp * 256 + mc * 128:hp * 256 + (mc + 1) * 128],
                            vtok[:, tc, h, 0:65], start=first, stop=last),
                            reads=[("kphi", tc, hp), ("vtok", tc, h // 8), ("vtok_ones",)],
                            writes=[PBK[7][0]], tag="S_mm")
                        n += 1
                S.op("dve", lambda V, h=h: V.tensor_tensor(
                    Sacc[:, h, :], Sacc[:, h, :], pb[7][:, 0:130], ALU.add),
                    reads=[PBK[7][0], ("Sacc", h)], writes=[("Sacc", h)], tag="S_fl")

            qsp = g % 2
            for mc in range(8):
                bank, bkey = pb[mc % 2], PBK[mc % 2]
                hp = mc // 2
                S.op("pe", lambda P, b=bank, g=g, mc=mc: P.matmul(
                    b[:, :], Mble[g][:, mc * 128:(mc + 1) * 128], ropeEq[:, :],
                    start=True, stop=False),
                    reads=[("rEq",)] + MKEg(g), writes=[bkey], tag="phq_mmE")
                S.op("pe", lambda P, b=bank, g=g, mc=mc: P.matmul(
                    b[:, :], Mblo[g][:, mc * 128:(mc + 1) * 128], ropeOq[:, :],
                    start=False, stop=False),
                    reads=[("rOq",)] + MKO(g), writes=[bkey], tag="phq_mmO")
                S.op("pe", lambda P, b=bank, g=g, hp=hp: P.matmul(
                    b[:, :], i4q[:, hp * 128:(hp + 1) * 128], xnq_s[:, g, :],
                    start=False, stop=True),
                    reads=[("xnq", g), ("i4q",)], writes=[bkey], tag="phq_r1")
                S.op("act", lambda A, b=bank, qsp=qsp, mc=mc: A.activation(
                    qstage[qsp][:, mc, :], b[:, :], AF.Exp),
                    reads=[bkey], writes=[("qst", qsp, mc)], tag="phq_exp")
            S.op("gp", lambda G, qsp=qsp, g=g, t0=t0: G.dma_start(
                qphi_d.ap().rearrange("(r p) t -> p r t", p=128)[:, 8 * g:8 * g + 8, t0:t0 + T],
                qstage[qsp][:, :, :]),
                reads=[("qst", qsp, mc) for mc in range(8)], dma=f"d_qsp{qsp}", tag="q_spill")

    S.op("dve", lambda V: V.tensor_copy(Sbf[:, :, :], Sacc[:, :, :]),
         reads=[("Sacc", h) for h in range(NH)], writes=[("Sbf",)], tag="S_cast")

    S.barrier()
    es1.close()

    es2 = contextlib.ExitStack()
    wout_s = es2.enter_context(nc.sbuf_tensor("wout_s", [128, 8, E], BF16))
    b1s = es2.enter_context(nc.sbuf_tensor("b1s", [128, NEXP, 32], F32))
    b2s = es2.enter_context(nc.sbuf_tensor("b2s", [128, NEXP, 8], F32))
    rtr_s = es2.enter_context(nc.sbuf_tensor("rtr_s", [128, 8, 8], BF16))
    x_s = es2.enter_context(nc.sbuf_tensor("x_s", [128, 8, T], F32))
    qph = [es2.enter_context(nc.sbuf_tensor(f"qph{p}", [128, 8, T], BF16)) for p in range(2)]
    ln2 = es2.enter_context(nc.sbuf_tensor("ln2", [128, 8, T], BF16))
    att = es2.enter_context(nc.sbuf_tensor("att", [128, 8, T], BF16))
    acc = es2.enter_context(nc.sbuf_tensor("acc", [128, 8, T], F32))
    gsb = es2.enter_context(nc.sbuf_tensor("gsb", [128, 32, T], BF16))
    w1s = [es2.enter_context(nc.sbuf_tensor(f"w1s{p}", [128, 8, T], BF16)) for p in range(2)]
    w2s = [es2.enter_context(nc.sbuf_tensor(f"w2s{p}", [128, 32, 128], BF16)) for p in range(2)]
    gateb = es2.enter_context(nc.sbuf_tensor("gateb", [128, 8, T], BF16))
    gfm = es2.enter_context(nc.sbuf_tensor("gfm", [8, T], BF16))
    rfm = es2.enter_context(nc.sbuf_tensor("rfm", [8, T], F32))
    rtk = es2.enter_context(nc.sbuf_tensor("rtk", [128, 8], F32))
    etk = es2.enter_context(nc.sbuf_tensor("etk", [128, 8], F32))
    gtk = es2.enter_context(nc.sbuf_tensor("gtk", [128, 8], F32))
    tscr = es2.enter_context(nc.sbuf_tensor("tscr", [128, 8], F32))
    nm_s = [es2.enter_context(nc.sbuf_tensor(f"nm{j}", [64, T], BF16)) for j in range(4)]
    dn_s = es2.enter_context(nc.sbuf_tensor("dn_s", [1, 4, 512], F32))
    rc_s = es2.enter_context(nc.sbuf_tensor("rc_s", [1, 4, 512], BF16))
    lnr_s = es2.enter_context(nc.sbuf_tensor("lnr_s", [1, 4, 512], F32))
    tmp2f = [es2.enter_context(nc.sbuf_tensor(f"tmp2f{j}", [128, T], F32)) for j in range(2)]
    row2 = es2.enter_context(nc.sbuf_tensor("row2", [1, 3 * 512], F32))
    rowb2 = es2.enter_context(nc.sbuf_tensor("rowb2", [1, 2 * 512], BF16))
    ysq2 = [es2.enter_context(nc.sbuf_tensor(f"ysq2{j}", [128, T], BF16)) for j in range(2)]
    ybf2 = [es2.enter_context(nc.sbuf_tensor(f"ybf2{j}", [128, T], BF16)) for j in range(2)]

    S.op("gp", lambda G: G.dma_start(wout_s[:, :, :],
                                     woutT.ap().rearrange("(c p) o -> p c o", p=128)),
         writes=[("wout",)], dma="d_init", tag="ld_wout")
    S.op("gp", lambda G: G.dma_start(b1s[:, :, :], b1T.ap().rearrange("e p c -> p e c")),
         writes=[("b1",)], dma="d_init", tag="ld_b1")
    S.op("gp", lambda G: G.dma_start(b2s[:, :, :], b2T.ap().rearrange("e p c -> p e c")),
         writes=[("b2",)], dma="d_init", tag="ld_b2")
    S.op("gp", lambda G: G.dma_start(rtr_s[:, :, :],
                                     routerT.ap().rearrange("(c p) e -> p c e", p=128)),
         writes=[("rtr",)], dma="d_init", tag="ld_rtr")

    def emit_ln2_chunk(c):
        kp = "l2"
        S.op("dve", lambda V, c=c: V.tensor_copy(ybf2[c % 2][:, :], x_s[:, c, :]),
             reads=[("y",)], writes=[(kp + "ybf", c % 2)], tag="l2_cp")
        S.op("act", lambda A, c=c: A.activation(
            ysq2[c % 2][:, :], ybf2[c % 2][:, :], AF.Square),
            reads=[(kp + "ybf", c % 2)], writes=[(kp + "ysq", c % 2)], tag="l2_sq")
        S.op("pe", lambda P, c=c: P.matmul(
            pb[6][0:1, :], ones_c[:, :], ybf2[c % 2][:, :], start=(c == 0), stop=(c == 7)),
            reads=[(kp + "ybf", c % 2), ("ones_c",)], writes=[PBK[6][0]], tag="l2_sum")
        S.op("pe", lambda P, c=c: P.matmul(
            pb[7][0:1, :], ones_c[:, :], ysq2[c % 2][:, :], start=(c == 0), stop=(c == 7)),
            reads=[(kp + "ysq", c % 2), ("ones_c",)], writes=[PBK[7][0]], tag="l2_ssq")

    def emit_ln2_tail():
        kp = "l2"
        S.op("dve", lambda V: V.tensor_scalar(rowb2[0:1, 0:512], pb[6][0:1, :],
                                              1.0 / 1024.0, None, ALU.mult),
             reads=[PBK[6][0]], writes=[(kp + "meanb",)], tag="l2_mean")
        S.op("dve", lambda V: V.tensor_tensor(row2[0:1, 0:512], rowb2[0:1, 0:512],
                                              rowb2[0:1, 0:512], ALU.mult),
             reads=[(kp + "meanb",)], writes=[(kp + "msq",)], tag="l2_msq")
        S.op("dve", lambda V: V.scalar_tensor_tensor(
            row2[0:1, 512:1024], pb[7][0:1, :], 1.0 / 1024.0, row2[0:1, 0:512],
            ALU.mult, ALU.subtract),
            reads=[PBK[7][0], (kp + "msq",)], writes=[(kp + "var",)], tag="l2_var")
        S.op("act", lambda A: A.activation(row2[0:1, 1024:1536], row2[0:1, 512:1024],
                                           AF.Ln, bias=acst[0:1, 0:1], scale=1.0),
             reads=[(kp + "var",), ("acst",)], writes=[(kp + "lnv",)], tag="l2_ln")
        S.op("act", lambda A: A.activation(rowb2[0:1, 512:1024], row2[0:1, 1024:1536],
                                           AF.Exp, bias=0.0, scale=acst[0:1, 1:2]),
             reads=[(kp + "lnv",), ("acst",)], writes=[(kp + "rstd",)], tag="l2_rstd")
        S.op("pe", lambda P: P.matmul(pb[6][:, :], ones_r[0:1, :], rowb2[0:1, 0:512],
                                      start=True, stop=True),
             reads=[(kp + "meanb",), ("ones_r",)], writes=[PBK[6][0]], tag="l2_bm")
        S.op("pe", lambda P: P.matmul(pb[7][:, :], ones_r[0:1, :], rowb2[0:1, 512:1024],
                                      start=True, stop=True),
             reads=[(kp + "rstd",), ("ones_r",)], writes=[PBK[7][0]], tag="l2_br")
        for c in range(8):
            S.op("dve", lambda V, c=c: V.tensor_tensor(
                tmp2f[c % 2][:, :], x_s[:, c, :], pb[6][:, :], ALU.subtract),
                reads=[("y",), PBK[6][0]], writes=[("o2t", c % 2)], tag="l2_sub")
            S.op("dve", lambda V, c=c: V.tensor_tensor(
                ln2[:, c, :], tmp2f[c % 2][:, :], pb[7][:, :], ALU.mult),
                reads=[("o2t", c % 2), PBK[7][0]], writes=[("ln2",)], tag="l2_mul")

    for i in range(NT_EMIT if P2_ON else 0):
        t0 = i * T
        S.op("gp", lambda G, t0=t0: G.dma_start(
            x_s[:, :, :],
            xT.ap().rearrange("(c p) t -> p c t", p=128)[:, :, t0:t0 + T]),
            writes=[("x",), ("y",)], dma="d_x0", tag="p2_x")
        if not FFN_ONLY:
            S.op("sp", lambda Y, t0=t0: Y.dma_start(
                qph[0][:, :, :],
                qphi_d.ap().rearrange("(r p) t -> p r t", p=128)[:, 0:8, t0:t0 + T]),
                writes=[("qph", 0)], dma="d_qh0", tag="p2_qphi")

        if FFN_ONLY:
            S.op("dve", lambda V: V.memset(ln2[:, :, :], 0.001), writes=[("ln2",)], tag="ms_ln2")
            S.op("dve", lambda V: V.memset(gateb[:, :, :], 0.25),
                 writes=[("gateb", e) for e in range(NEXP)], tag="ms_gb")
            S.op("dve", lambda V: V.memset(x_s[:, :, :], 0.0), reads=[("x",)],
                 writes=[("y",)], tag="ms_y")
        def emit_numer(h):
            qtr = h // 4
            qb = qtr % 2
            if h % 4 == 0 and qtr + 1 < 4:
                qb2 = (qtr + 1) % 2
                S.op("sp", lambda Y, q=qtr + 1, qb2=qb2, t0=t0: Y.dma_start(
                    qph[qb2][:, :, :],
                    qphi_d.ap().rearrange("(r p) t -> p r t", p=128)[:, 8 * q:8 * q + 8, t0:t0 + T]),
                    writes=[("qph", qb2)], dma=f"d_qh{qb2}", tag="p2_qphi")
            q0 = (h % 4) * 2
            bA, kA = pb[h % 4], PBK[h % 4][0]
            for mc in range(2):
                S.op("pe", lambda P, b=bA, h=h, qb=qb, q0=q0, mc=mc: P.matmul(
                    b[0:65, :], Sbf[:, h, mc * 65:(mc + 1) * 65], qph[qb][:, q0 + mc, :],
                    start=(mc == 0), stop=(mc == 1)),
                    reads=[("Sbf",), ("qph", qb)], writes=[kA], tag="nm_mm")

        def emit_finish(h):
            j, j2 = h % 4, h % 2
            bA, kA = pb[j], PBK[j][0]
            bB, kB = pb[4 + j2], PBK[4 + j2][0]
            S.op("dve", lambda V, b=bA, j=j: V.tensor_scalar(
                dn_s[0:1, j, :], b[64:65, :], EPS, None, ALU.max),
                reads=[kA], writes=[("dn", j)], tag="nm_max")
            S.op("act", lambda A, j=j: A.activation(lnr_s[0:1, j, :], dn_s[0:1, j, :], AF.Ln),
                 reads=[("dn", j)], writes=[("lnr", j)], tag="nm_ln")
            S.op("act", lambda A, j=j: A.activation(rc_s[0:1, j, :], lnr_s[0:1, j, :],
                                                    AF.Exp, bias=0.0, scale=acst[0:1, 2:3]),
                 reads=[("lnr", j), ("acst",)], writes=[("rc", j)], tag="nm_rc")
            S.op("pe", lambda P, b=bB, j=j: P.matmul(
                b[0:64, :], ones_r[0:1, 0:64], rc_s[0:1, j, :], start=True, stop=True),
                reads=[("rc", j), ("ones_r",)], writes=[kB], tag="nm_bc")
            S.op("act", lambda A, b=bA, j=j: A.activation(
                nm_s[j][:, :], b[0:64, :], AF.Identity),
                reads=[kA], writes=[("nm", j)], tag="nm_ev")
            S.op("dve", lambda V, b=bB, h=h, j=j: V.tensor_tensor(
                att[(h % 2) * 64:(h % 2) * 64 + 64, h // 2, :], nm_s[j][:, :], b[0:64, :], ALU.mult),
                reads=[("nm", j), kB], writes=[("att", h)], tag="nm_mul")

        for h in range(NH):
            emit_numer(h)
            if h >= 3:
                emit_finish(h - 3)
        for h in range(NH - 3, NH):
            emit_finish(h)

        for oc in range(8 if not FFN_ONLY else 0):
            bank, bkey = pb[4 + oc % 2], PBK[4 + oc % 2][0]
            for kc in range(8):
                S.op("pe", lambda P, b=bank, oc=oc, kc=kc: P.matmul(
                    b[:, :], wout_s[:, kc, oc * 128:(oc + 1) * 128], att[:, kc, :],
                    start=(kc == 0), stop=(kc == 7)),
                    reads=[("wout",), ("att", kc * 2), ("att", kc * 2 + 1)],
                    writes=[bkey], tag="op_mm")
            S.op("dve", lambda V, b=bank, oc=oc: V.tensor_tensor(
                x_s[:, oc, :], x_s[:, oc, :], b[:, :], ALU.add),
                reads=[("x",), bkey], writes=[("y",)], tag="op_res")
            if oc >= 1:
                emit_ln2_chunk(oc - 1)
        if not FFN_ONLY:
            emit_ln2_chunk(7)
            emit_ln2_tail()

        def emit_gate():
            for kc in range(8):
                S.op("pe", lambda P, kc=kc: P.matmul(
                    pb[6][0:8, :], rtr_s[:, kc, :], ln2[:, kc, :], start=(kc == 0), stop=(kc == 7)),
                    reads=[("rtr",), ("ln2",)], writes=[PBK[6][0]], tag="rt_mm")
            S.op("dve", lambda V: V.tensor_copy(rfm[:, :], pb[6][0:8, :]),
                 reads=[PBK[6][0]], writes=[("rfm",)], tag="rt_ev")
            for tc in range(TC):
                S.op("pe", lambda P, tc=tc: P.transpose(
                    pb[7][0:128, 0:8], rfm[:, tc * 128:(tc + 1) * 128], id8_s[:, :]),
                    reads=[("rfm",), ("id8",)], writes=[PBK[7][0]], tag="rt_tr")
                S.op("dve", lambda V: V.tensor_copy(rtk[:, :], pb[7][0:128, 0:8]),
                     reads=[PBK[7][0]], writes=[("rtk",)], tag="tk_cp")
                S.op("dve", lambda V: V.reduce_max(tscr[:, 0:1], rtk[:, :], axis=AX),
                     reads=[("rtk",)], writes=[("tscr",)], tag="tk_m1")
                S.op("dve", lambda V: V.tensor_scalar(
                    gtk[:, :], rtk[:, :], tscr[:, 0:1], None, ALU.is_equal),
                    reads=[("rtk",), ("tscr",)], writes=[("gtk",)], tag="tk_eq1")
                S.op("dve", lambda V: V.scalar_tensor_tensor(
                    rtk[:, :], gtk[:, :], -1e30, rtk[:, :], ALU.mult, ALU.add),
                    reads=[("gtk",), ("rtk",)], writes=[("rtk",)], tag="tk_rm")
                S.op("dve", lambda V: V.reduce_max(tscr[:, 1:2], rtk[:, :], axis=AX),
                     reads=[("rtk",)], writes=[("tscr",)], tag="tk_m2")
                S.op("dve", lambda V: V.tensor_tensor(
                    tscr[:, 2:3], tscr[:, 1:2], tscr[:, 0:1], ALU.subtract),
                    reads=[("tscr",)], writes=[("tscr",)], tag="tk_d21")
                S.op("act", lambda A: A.activation(tscr[:, 3:4], tscr[:, 2:3], AF.Exp),
                     reads=[("tscr",)], writes=[("tscr",)], tag="tk_e21")
                S.op("dve", lambda V: V.tensor_scalar(
                    tscr[:, 4:5], tscr[:, 3:4], 1.0, None, ALU.add),
                    reads=[("tscr",)], writes=[("tscr",)], tag="tk_e21p")
                S.op("dve", lambda V: V.reciprocal(tscr[:, 5:6], tscr[:, 4:5]),
                     reads=[("tscr",)], writes=[("tscr",)], tag="tk_w1")
                S.op("dve", lambda V: V.tensor_tensor(
                    tscr[:, 6:7], tscr[:, 3:4], tscr[:, 5:6], ALU.mult),
                    reads=[("tscr",)], writes=[("tscr",)], tag="tk_w2")
                S.op("dve", lambda V: V.tensor_scalar(
                    etk[:, :], rtk[:, :], tscr[:, 1:2], None, ALU.is_equal),
                    reads=[("rtk",), ("tscr",)], writes=[("etk",)], tag="tk_eq2")
                S.op("dve", lambda V: V.tensor_scalar(
                    gtk[:, :], gtk[:, :], tscr[:, 5:6], None, ALU.mult),
                    reads=[("gtk",), ("tscr",)], writes=[("gtk",)], tag="tk_gw1")
                S.op("dve", lambda V: V.scalar_tensor_tensor(
                    gtk[:, :], etk[:, :], tscr[:, 6:7], gtk[:, :], ALU.mult, ALU.add),
                    reads=[("etk",), ("tscr",), ("gtk",)], writes=[("gtk",)], tag="tk_gate")
                S.op("pe", lambda P, tc=tc: P.transpose(
                    pb[7][0:8, 0:128], gtk[:, :], id128_s[:, :]),
                    reads=[("gtk",), ("id128",)], writes=[PBK[7][0]], tag="tk_tr")
                S.op("dve", lambda V, tc=tc: V.tensor_copy(gfm[:, tc * 128:(tc + 1) * 128],
                                                           pb[7][0:8, 0:128]),
                     reads=[PBK[7][0]], writes=[("gfm",)], tag="tk_gfm")
            for e in range(NEXP):
                bank, bkey = pb[4 + e % 2], PBK[4 + e % 2][0]
                S.op("pe", lambda P, b=bank, e=e: P.matmul(
                    b[:, :], i8b[:, e * 128:(e + 1) * 128], gfm[:, :], start=True, stop=True),
                    reads=[("gfm",), ("i8b",)], writes=[bkey], tag="gb_mm")
                S.op("act", lambda A, b=bank, e=e: A.activation(
                    gateb[:, e, :], b[:, :], AF.Identity),
                    reads=[bkey], writes=[("gateb", e)], tag="gb_ev")

        for e in range(NEXP):
            for ft in range(8):
                wp = (e * 8 + ft) % 2
                if FFN_NODMA:
                    pass
                else:
                    S.op("sp", lambda Y, e=e, ft=ft, wp=wp: Y.dma_start(
                    w1s[wp][:, :, :],
                    w1bf.ap()[e].rearrange("(c p) f -> p c f", p=128)[:, :, ft * 512:(ft + 1) * 512]),
                    reads=[("w1bf", e)], writes=[("w1s", wp)], dma=f"d_w1{'ab'[wp]}", tag="w1_ld")
                for fcl in range(4):
                    bank, bkey = pb[fcl % 2], PBK[fcl % 2][0]
                    fc = ft * 4 + fcl
                    for kc in range(8):
                        S.op("pe", lambda P, b=bank, wp=wp, fcl=fcl, kc=kc: P.matmul(
                            b[:, :], w1s[wp][:, kc, fcl * 128:(fcl + 1) * 128], ln2[:, kc, :],
                            start=(kc == 0), stop=(kc == 7)),
                            reads=[("w1s", wp), ("ln2",)], writes=[bkey], tag="h1_mm")
                    S.op("act", lambda A, b=bank, e=e, fc=fc: A.activation(
                        gsb[:, fc, :], b[:, :], AF.Gelu, bias=b1s[:, e, fc:fc + 1], scale=1.0),
                        reads=[bkey, ("b1",)], writes=[("gsb", fc)], tag="h1_gelu")
            if e == 0 and not FFN_ONLY:
                emit_gate()
            for hc in range(8):
                wp = (e * 8 + hc) % 2
                if FFN_NODMA:
                    pass
                else:
                    S.op("sp", lambda Y, e=e, hc=hc, wp=wp: Y.dma_start(
                    w2s[wp][:, :, :],
                    w2bf.ap()[e].rearrange("(c p) o -> p c o", p=128)[:, :, hc * 128:(hc + 1) * 128]),
                    reads=[("w2bf", e)], writes=[("w2s", wp)], dma=f"d_w2{'ab'[wp]}", tag="w2_ld")
                bank, bkey = pb[2 + hc % 2], PBK[2 + hc % 2][0]
                for fc in range(32):
                    S.op("pe", lambda P, b=bank, wp=wp, fc=fc: P.matmul(
                        b[:, :], w2s[wp][:, fc, :], gsb[:, fc, :],
                        start=(fc == 0), stop=(fc == 31)),
                        reads=[("w2s", wp), ("gsb", fc)], writes=[bkey], tag="o2_mm")
                if e == 0:
                    S.op("dve", lambda V, b=bank, e=e, hc=hc: V.scalar_tensor_tensor(
                        acc[:, hc, :], b[:, :], b2s[:, e, hc:hc + 1], gateb[:, e, :],
                        ALU.add, ALU.mult),
                        reads=[bkey, ("b2",), ("gateb", e)], writes=[("acc", hc)], tag="o2_ev0")
                else:
                    j = hc % 2
                    S.op("dve", lambda V, b=bank, e=e, hc=hc, j=j: V.scalar_tensor_tensor(
                        tmp2f[j][:, :], b[:, :], b2s[:, e, hc:hc + 1], gateb[:, e, :],
                        ALU.add, ALU.mult),
                        reads=[bkey, ("b2",), ("gateb", e)], writes=[("o2t", j)], tag="o2_ev")
                    S.op("dve", lambda V, hc=hc, j=j: V.tensor_tensor(
                        acc[:, hc, :], acc[:, hc, :], tmp2f[j][:, :], ALU.add),
                        reads=[("o2t", j), ("acc", hc)], writes=[("acc", hc)], tag="o2_acc")

        for oc in range(8):
            S.op("dve", lambda V, oc=oc: V.tensor_tensor(
                x_s[:, oc, :], x_s[:, oc, :], acc[:, oc, :], ALU.add),
                reads=[("y",), ("acc", oc)], writes=[("y",)], tag="fin_add")
        S.op("gp", lambda G, t0=t0: G.dma_start(
            outT.ap().rearrange("(c p) t -> p c t", p=128)[:, :, t0:t0 + T],
            x_s[:, :, :]),
            reads=[("y",)], dma="d_out0", tag="p2_out")

    final_waits = {"d_out0": 16 * S.dma_count.get("d_out0", 0)}
    final_waits = {k: v for k, v in final_waits.items() if v}
    run_engine = S.finalize(eng_sems, dma_sems, final_waits)

    with nc.Block() as block, nc.allow_low_precision(
            reason="bf16 matmul operand staging; fp32 accumulation everywhere"):
        block.tensor(lambda eng: run_engine(eng, "pe"))
        block.scalar(lambda eng: run_engine(eng, "act"))
        block.vector(lambda eng: run_engine(eng, "dve"))
        block.sync(lambda eng: run_engine(eng, "sp"))
        block.gpsimd(lambda eng: run_engine(eng, "gp"))

    es2.close()
    es.close()
    return nc


def _host_constants():
    def fwht(v):
        shape = v.shape
        Pn = shape[-1]
        h = 1
        while h < Pn:
            v = v.reshape(shape[:-1] + (-1, 2 * h))
            a, b = v[..., :h], v[..., h:]
            v = np.concatenate([a + b, a - b], axis=-1).reshape(shape)
            h *= 2
        return v

    W64 = fwht(np.eye(64, dtype=np.float64))
    rowperm = np.concatenate([np.arange(0, 64, 2), np.arange(1, 64, 2)])
    perm = np.zeros(1024, dtype=np.int64)
    n = 0
    for g in range(4):
        for parity in range(2):
            for hp in range(4):
                for j in range(32):
                    perm[n] = (4 * g + hp) * 64 + 2 * j + parity
                    n += 1
    freqs = 10000.0 ** (-np.arange(0, 64, 2) / 64.0)
    tpos = np.arange(L)
    ang = np.where(tpos[None, :] >= 1,
                   (tpos[None, :] - 1) * freqs[np.arange(128) % 32, None], 0.0)
    cos_t = np.cos(ang)
    sin_t = np.sin(ang)
    cos_t[:, 0] = 1.0
    sin_t[:, 0] = 0.0

    i4b = np.zeros((128, 4), dtype=np.float32)
    for d in range(128):
        i4b[d, d // 32] = 1.0
    i4q = np.zeros((4, 4 * 128), dtype=np.float32)
    for hp in range(4):
        i4q[hp, hp * 128:(hp + 1) * 128] = 1.0
    i8b = np.zeros((8, 8 * 128), dtype=np.float32)
    for e in range(8):
        i8b[e, e * 128:(e + 1) * 128] = 1.0

    return {
        "W64": W64, "rowperm": rowperm, "perm": perm,
        "costab": cos_t.astype(BF), "sintab": sin_t.astype(BF),
        "wm64": W64.astype(np.float32),
        "hpT": np.ascontiguousarray(W64[rowperm, :].T).astype(np.float32),
        "onescol": np.ones((128, 1), dtype=BF),
        "onesrow": np.ones((1, 128), dtype=BF),
        "ind4b": i4b.astype(BF), "ind4q": i4q.astype(BF), "ind8b": i8b.astype(BF),
        "id8": np.eye(8, dtype=np.float32), "id128": np.eye(128, dtype=np.float32),
        "c4096": np.full((1, 64), 1.0 / 4096.0, dtype=np.float32),
        "actconst": np.tile(np.array([[LNEPS, -0.5, -1.0]], dtype=np.float32), (128, 1)),
    }


_CACHE = {}


def kernel(L0_attn_x, L0_attn_w_qkv, L0_attn_w_out, L0_attn_sorf_signs,
           L0_attn_sorf_scales, L0_ffn_router_w, L0_ffn_w1, L0_ffn_b1,
           L0_ffn_w2, L0_ffn_b2):
    x = np.ascontiguousarray(np.asarray(L0_attn_x, dtype=np.float32))
    w_qkv = np.asarray(L0_attn_w_qkv, dtype=np.float32)
    w_out = np.asarray(L0_attn_w_out, dtype=np.float32)
    signs = np.asarray(L0_attn_sorf_signs, dtype=np.float32)
    scales = np.asarray(L0_attn_sorf_scales, dtype=np.float32)
    router_w = np.asarray(L0_ffn_router_w, dtype=np.float32)
    w1 = np.asarray(L0_ffn_w1, dtype=np.float32)
    b1 = np.asarray(L0_ffn_b1, dtype=np.float32)
    w2 = np.asarray(L0_ffn_w2, dtype=np.float32)
    b2 = np.asarray(L0_ffn_b2, dtype=np.float32)

    cst = _host_constants()
    perm = cst["perm"]

    wqkvT = np.ascontiguousarray(w_qkv.T)
    wq = wqkvT[:, 0:1024][:, perm]
    wk = wqkvT[:, 1024:2048][:, perm]
    wv = wqkvT[:, 2048:3072]
    WQKVT = np.ascontiguousarray(np.concatenate([wq, wk, wv], axis=1))

    shared = {
        "wqkvT": WQKVT,
        "woutT": np.ascontiguousarray(w_out.T),
        "w1T": np.ascontiguousarray(w1.transpose(0, 2, 1)),
        "w2T": np.ascontiguousarray(w2.transpose(0, 2, 1)),
        "b1T": np.ascontiguousarray(b1.reshape(8, 32, 128).transpose(0, 2, 1)),
        "b2T": np.ascontiguousarray(b2.reshape(8, 8, 128).transpose(0, 2, 1)),
        "routerT": np.ascontiguousarray(router_w.T),
        "costab": cst["costab"], "sintab": cst["sintab"],
        "wm64": cst["wm64"], "hpT": cst["hpT"],
        "s1pT": np.ascontiguousarray(
            signs[:, :, 0, :][..., cst["rowperm"]].reshape(64, 64).T).astype(np.float32),
        "s2T": np.ascontiguousarray(signs[:, :, 1, :].reshape(64, 64).T).astype(np.float32),
        "s3T": np.ascontiguousarray(signs[:, :, 2, :].reshape(64, 64).T).astype(np.float32),
        "gflat": np.ascontiguousarray(scales.reshape(1, 64 * 64)).astype(np.float32),
        "onescol": cst["onescol"], "onesrow": cst["onesrow"],
        "ind4b": cst["ind4b"], "ind4q": cst["ind4q"], "ind8b": cst["ind8b"],
        "id8": cst["id8"], "id128": cst["id128"], "c4096": cst["c4096"],
        "actconst": cst["actconst"],
    }

    if "nc" not in _CACHE:
        _CACHE["nc"] = build_program()
    nc = _CACHE["nc"]

    in_maps = []
    for c in range(C):
        m = dict(shared)
        m["xT"] = np.ascontiguousarray(x[c].T)
        in_maps.append(m)

    res = run_bass_kernel_spmd(nc, in_maps, core_ids=list(range(C)))
    out = np.empty((C, L, E), dtype=np.float32)
    for c in range(C):
        out[c] = res.results[c]["outT"].T
    return out


# version 05
# speedup vs baseline: 2.6621x; 1.0564x over previous
import contextlib
import numpy as np
import ml_dtypes

import concourse.bass as bass
import concourse.mybir as mybir
from concourse.bass_utils import run_bass_kernel_spmd
from concourse.alu_op_type import AluOpType as ALU

F32 = mybir.dt.float32
BF16 = mybir.dt.bfloat16
AF = mybir.ActivationFunctionType
AX = mybir.AxisListType.X
BF = ml_dtypes.bfloat16

import os
C, L, E = 8, 8192, 1024
NH, HD = 16, 64
T = 512
NT = L // T
NT_EMIT = int(os.environ.get("KERNEL_NT", str(NT)))
P1_ON = os.environ.get("KERNEL_P1", "1") == "1"
P2_ON = os.environ.get("KERNEL_P2", "1") == "1"
FFN_ONLY = os.environ.get("KERNEL_FFN_ONLY", "0") == "1"
FFN_NODMA = os.environ.get("KERNEL_FFN_NODMA", "0") == "1"
TC = T // 128
NEXP, FF = 8, 4096
LN16 = float(np.log(16.0))
EPS = 1e-6
LNEPS = 1e-5

ENGINES = ("pe", "act", "dve", "pool", "sp", "gp")


class _Op:
    __slots__ = ("eng", "fn", "dma", "dma_ord", "inc", "pos", "deps", "dma_waits",
                 "tag", "_bar_dma")

    def __init__(self, eng, fn, dma, pos, tag):
        self.eng = eng
        self.fn = fn
        self.dma = dma
        self.dma_ord = 0
        self.inc = False
        self.pos = pos
        self.deps = []
        self.dma_waits = {}
        self.tag = tag
        self._bar_dma = None


class Sched:

    def __init__(self):
        self.ops = {e: [] for e in ENGINES}
        self.reg_w = {}
        self.reg_r = {}
        self.dma_count = {}
        self.bar_pending = {e: None for e in ENGINES}

    def op(self, eng, fn, reads=(), writes=(), dma=None, tag=""):
        o = _Op(eng, fn, dma, len(self.ops[eng]), tag)
        bp = self.bar_pending.get(eng)
        if bp is not None:
            prods, dma_snap = bp
            for p in prods:
                if p.eng != eng:
                    o.deps.append(p)
            o._bar_dma = dma_snap
            self.bar_pending[eng] = None

        def add(p):
            if p is None or p is o:
                return
            if p.eng == "pe" and eng == "pe" and p.dma is None and dma is None:
                return
            if p.dma is not None:
                o.dma_waits[p.dma] = max(o.dma_waits.get(p.dma, 0),
                                         16 * self.dma_count.get(p.dma, 0))
            else:
                o.deps.append(p)
                p.inc = True

        for k in reads:
            add(self.reg_w.get(k))
        for k in writes:
            add(self.reg_w.get(k))
            rd = self.reg_r.get(k)
            if rd:
                for r in rd.values():
                    add(r)
        for k in writes:
            self.reg_w[k] = o
            self.reg_r[k] = {}
        for k in reads:
            self.reg_r.setdefault(k, {})[eng] = o
        if dma is not None:
            self.dma_count[dma] = self.dma_count.get(dma, 0) + 1
            o.dma_ord = self.dma_count[dma]
        self.ops[eng].append(o)
        return o

    def barrier(self):
        prods = []
        for e in ENGINES:
            for o in reversed(self.ops[e]):
                if o.dma is None:
                    o.inc = True
                    prods.append(o)
                    break
        dma_snap = dict(self.dma_count)
        for e in ENGINES:
            self.bar_pending[e] = (prods, dma_snap)

    def finalize(self, eng_sems, dma_sems, final_dma_waits):
        val = {}
        for e in ENGINES:
            cnt = 0
            for o in self.ops[e]:
                if o.inc:
                    cnt += 1
                val[o] = cnt

        def run_engine(engobj, e):
            marks = {}
            for o in self.ops[e]:
                waits = {}
                if o._bar_dma:
                    for dn, dc in o._bar_dma.items():
                        waits[("d", dn)] = max(waits.get(("d", dn), 0), 16 * dc)
                for dn, v in o.dma_waits.items():
                    waits[("d", dn)] = max(waits.get(("d", dn), 0), v)
                for p in o.deps:
                    waits[("e", p.eng)] = max(waits.get(("e", p.eng), 0), val[p])
                for key, v in waits.items():
                    if marks.get(key, 0) >= v:
                        continue
                    marks[key] = v
                    sem = dma_sems[key[1]] if key[0] == "d" else eng_sems[key[1]]
                    engobj.wait_ge(sem, v)
                inst = o.fn(engobj)
                if o.dma is not None:
                    inst.then_inc(dma_sems[o.dma], 16)
                elif o.inc:
                    if inst is None:
                        inst = engobj.nop()
                    inst.then_inc(eng_sems[e], 1)
            if e == "gp":
                for dn, v in final_dma_waits.items():
                    engobj.wait_ge(dma_sems[dn], v)
        return run_engine


def build_program():
    nc = bass.Bass("TRN2", target_bir_lowering=False)
    es = contextlib.ExitStack()

    def din(name, shape, dt):
        return nc.dram_tensor(name, list(shape), dt, kind="ExternalInput")

    xT = din("xT", (E, L), F32)
    wqkvT = din("wqkvT", (E, 3 * E), F32)
    woutT = din("woutT", (E, E), F32)
    w1T = din("w1T", (NEXP, E, FF), F32)
    w2T = din("w2T", (NEXP, FF, E), F32)
    b1T = din("b1T", (NEXP, 128, FF // 128), F32)
    b2T = din("b2T", (NEXP, 128, E // 128), F32)
    routerT = din("routerT", (E, NEXP), F32)
    costab = din("costab", (128, L), BF16)
    sintab = din("sintab", (128, L), BF16)
    wm64 = din("wm64", (64, 64), F32)
    hpT = din("hpT", (64, 64), F32)
    s1pT = din("s1pT", (64, 64), F32)
    s2T = din("s2T", (64, 64), F32)
    s3T = din("s3T", (64, 64), F32)
    gflat = din("gflat", (1, 64 * 64), F32)
    onescol = din("onescol", (128, 1), BF16)
    onesrow = din("onesrow", (1, 128), BF16)
    ind4b = din("ind4b", (128, 4), BF16)
    ind4q = din("ind4q", (4, 4 * 128), BF16)
    ind8b = din("ind8b", (8, 8 * 128), BF16)
    id8 = din("id8", (8, 8), F32)
    id128 = din("id128", (128, 128), F32)
    c4096 = din("c4096", (1, 64), F32)
    actconst = din("actconst", (128, 3), F32)

    outT = nc.dram_tensor("outT", [E, L], F32, kind="ExternalOutput")

    w1bf = nc.dram_tensor("w1bf", [NEXP, E, FF], BF16, kind="Internal")
    w2bf = nc.dram_tensor("w2bf", [NEXP, FF, E], BF16, kind="Internal")
    qphi_d = nc.dram_tensor("qphi_d", [4096, L], BF16, kind="Internal")

    eng_sems = {e: es.enter_context(nc.semaphore(f"s_{e}")) for e in ENGINES}
    dma_names = ["d_init", "d_wc", "d_mpl", "d_cs0", "d_cs1",
                 "d_xb0", "d_xb1", "d_x0", "d_x1", "d_qsp0", "d_qsp1", "d_qsp2",
                 "d_qh0", "d_qh1", "d_w1a", "d_w1b", "d_w2a", "d_w2b",
                 "d_out0", "d_out1"]
    dma_sems = {n: es.enter_context(nc.semaphore(n)) for n in dma_names}

    S = Sched()

    pb = [es.enter_context(nc.psum_tensor(f"pb{i}", [128, 512], F32)) for i in range(8)]
    PBK = [(f"pb{i}",) for i in range(8)]
    ones_c = es.enter_context(nc.sbuf_tensor("ones_c", [128, 1], BF16))
    ones_r = es.enter_context(nc.sbuf_tensor("ones_r", [1, 128], BF16))
    i4b = es.enter_context(nc.sbuf_tensor("i4b", [128, 4], BF16))
    i4q = es.enter_context(nc.sbuf_tensor("i4q", [4, 4 * 128], BF16))
    i8b = es.enter_context(nc.sbuf_tensor("i8b", [8, 8 * 128], BF16))
    id8_s = es.enter_context(nc.sbuf_tensor("id8_s", [8, 8], F32))
    id128_s = es.enter_context(nc.sbuf_tensor("id128_s", [128, 128], F32))
    Sbf = es.enter_context(nc.sbuf_tensor("Sbf", [128, NH, 130], BF16))
    acst = es.enter_context(nc.sbuf_tensor("acst", [128, 3], F32))

    def g_init(dst_ap, src_ap, tag, writes):
        S.op("gp", lambda G, d=dst_ap, s=src_ap: G.dma_start(d, s),
             writes=writes, dma="d_init", tag=tag)

    g_init(ones_c[:, :], onescol.ap(), "ld_onesc", [("ones_c",)])
    g_init(ones_r[:, :], onesrow.ap(), "ld_onesr", [("ones_r",)])
    g_init(i4b[:, :], ind4b.ap(), "ld_i4b", [("i4b",)])
    g_init(i4q[:, :], ind4q.ap(), "ld_i4q", [("i4q",)])
    g_init(i8b[:, :], ind8b.ap(), "ld_i8b", [("i8b",)])
    g_init(id8_s[:, :], id8.ap(), "ld_id8", [("id8",)])
    g_init(id128_s[:, :], id128.ap(), "ld_id128", [("id128",)])
    g_init(acst[:, :], actconst.ap(), "ld_acst", [("acst",)])

    es1 = contextlib.ExitStack()
    wqkv_s = es1.enter_context(nc.sbuf_tensor("wqkv_s", [128, 8, 3 * E], BF16))
    Mble = [es1.enter_context(nc.sbuf_tensor(f"Mble{g}", [128, 1024], BF16)) for g in range(4)]
    Mblo = [es1.enter_context(nc.sbuf_tensor(f"Mblo{g}", [128, 1024], BF16)) for g in range(4)]
    cos_t = [es1.enter_context(nc.sbuf_tensor(f"cos{p}", [128, T], BF16)) for p in range(2)]
    sin_t = [es1.enter_context(nc.sbuf_tensor(f"sin{p}", [128, T], BF16)) for p in range(2)]
    xbf2 = [es1.enter_context(nc.sbuf_tensor(f"xbf{p}", [128, 8, T], BF16)) for p in range(2)]
    ln1 = es1.enter_context(nc.sbuf_tensor("ln1", [128, 8, T], BF16))
    qbf = es1.enter_context(nc.sbuf_tensor("qbf", [128, 8, T], BF16))
    kbf = es1.enter_context(nc.sbuf_tensor("kbf", [128, 8, T], BF16))
    ropeEk = es1.enter_context(nc.sbuf_tensor("ropeEk", [128, T], BF16))
    ropeOk = es1.enter_context(nc.sbuf_tensor("ropeOk", [128, T], BF16))
    ropeEq = es1.enter_context(nc.sbuf_tensor("ropeEq", [128, T], BF16))
    ropeOq = es1.enter_context(nc.sbuf_tensor("ropeOq", [128, T], BF16))
    sq_s = [es1.enter_context(nc.sbuf_tensor(f"sq{j}", [128, T], BF16)) for j in range(2)]
    xsq_s = [es1.enter_context(nc.sbuf_tensor(f"xsq{j}", [128, T], BF16)) for j in range(2)]
    tmp_bf = [es1.enter_context(nc.sbuf_tensor(f"tmpbf{j}", [128, T], BF16)) for j in range(2)]
    tmp_f = [es1.enter_context(nc.sbuf_tensor(f"tmpf{j}", [128, T], F32)) for j in range(2)]
    kphi_s = es1.enter_context(nc.sbuf_tensor("kphi_s", [128, TC, 1024], BF16))
    qstage = [es1.enter_context(nc.sbuf_tensor(f"qstage{j}", [128, 8, T], BF16)) for j in range(3)]
    vtok = es1.enter_context(nc.sbuf_tensor("vtok", [128, TC, NH, 65], BF16))
    xnk_s = es1.enter_context(nc.sbuf_tensor("xnk_s", [128, 4, 16], F32))
    xnq_s = es1.enter_context(nc.sbuf_tensor("xnq_s", [4, 4, T], BF16))
    row1 = es1.enter_context(nc.sbuf_tensor("row1", [1, 3 * 512], F32))
    rowb1 = es1.enter_context(nc.sbuf_tensor("rowb1", [1, 2 * 512], BF16))
    Sacc = es1.enter_context(nc.sbuf_tensor("Sacc", [128, NH, 130], F32))

    esM = contextlib.ExitStack()
    wm_s = esM.enter_context(nc.sbuf_tensor("wm_s", [64, 64], F32))
    hpT_s = esM.enter_context(nc.sbuf_tensor("hpT_s", [64, 64], F32))
    s1p_s = esM.enter_context(nc.sbuf_tensor("s1p_s", [64, 64], F32))
    s2_s = esM.enter_context(nc.sbuf_tensor("s2_s", [64, 64], F32))
    s3_s = esM.enter_context(nc.sbuf_tensor("s3_s", [64, 64], F32))
    gfl_s = esM.enter_context(nc.sbuf_tensor("gfl_s", [1, 1024], F32))
    gb_s = esM.enter_context(nc.sbuf_tensor("gb_s", [64, 1024], F32))
    mstage = esM.enter_context(nc.sbuf_tensor("mstage", [64, 256], F32))
    mtmp = [esM.enter_context(nc.sbuf_tensor(f"mtmp{j}", [64, 64], F32)) for j in range(3)]
    c4096_s = esM.enter_context(nc.sbuf_tensor("c4096_s", [1, 64], F32))

    g_init(wm_s[:, :], wm64.ap(), "ld_wm", [("wm",)])
    g_init(hpT_s[:, :], hpT.ap(), "ld_hpT", [("hpT",)])
    g_init(s1p_s[:, :], s1pT.ap(), "ld_s1p", [("s1p",)])
    g_init(s2_s[:, :], s2T.ap(), "ld_s2", [("s2",)])
    g_init(s3_s[:, :], s3T.ap(), "ld_s3", [("s3",)])
    g_init(c4096_s[:, :], c4096.ap(), "ld_c4096", [("c4096",)])
    if os.environ.get("KERNEL_WQKV", "1") == "1":
        g_init(wqkv_s[:, :, :], wqkvT.ap().rearrange("(c p) o -> p c o", p=128),
               "ld_wqkv", [("wqkv",)])

    S.op("dve", lambda V: V.memset(vtok[:, :, :, 64:65], 1.0), writes=[("vtok_ones",)], tag="ms_v")
    S.op("dve", lambda V: V.memset(Sacc[:, :, :], 0.0),
         writes=[("Sacc", h) for h in range(NH)], tag="ms_S")
    for g in range(4):
        S.op("dve", lambda V, g=g: V.memset(Mble[g][:, :], 0.0), writes=[("Mble0", g)], tag="ms_Me")
        S.op("dve", lambda V, g=g: V.memset(Mblo[g][:, :], 0.0), writes=[("Mblo0", g)], tag="ms_Mo")

    for b in range(4 if os.environ.get("KERNEL_MB", "1") == "1" else 0):
        S.op("gp", lambda G, b=b: G.dma_start(gfl_s[:, :], gflat.ap()[:, b * 1024:(b + 1) * 1024]),
             writes=[("gfl",)], dma="d_init", tag="ld_gfl")
        for j in range(2 if "g" in os.environ.get("KERNEL_MPARTS", "gmp") else 0):
            bank, bkey = pb[j], PBK[j]
            S.op("pe", lambda P, bk=bank, j=j: P.matmul(
                bk[0:64, :], c4096_s[0:1, 0:64], gfl_s[0:1, j * 512:(j + 1) * 512],
                start=True, stop=True),
                reads=[("c4096",), ("gfl",)], writes=[bkey], tag="gbmm")
            S.op("dve", lambda V, bk=bank, j=j: V.tensor_copy(
                gb_s[:, j * 512:(j + 1) * 512], bk[0:64, :]),
                reads=[bkey], writes=[("gb",)], tag="gbcp")
        MPARTS = os.environ.get("KERNEL_MPARTS", "gmp")
        for hh in range(4 if "m" in MPARTS else 0):
            h = b * 4 + hh
            g, hp = h // 4, h % 4
            for r in range(4):
                idx = 4 * h + r
                li = idx - b * 16
                pbank, pkey = pb[2 + (r % 2)], PBK[2 + (r % 2)]
                S.op("dve", lambda V, i=idx: V.tensor_scalar(
                    mtmp[0][:, :], wm_s[:, :], s3_s[:, i:i + 1], None, ALU.mult),
                    reads=[("wm",), ("s3",)], writes=[("mt0",)], tag="m_s3")
                S.op("dve", lambda V, li=li: V.tensor_tensor(
                    mtmp[1][:, :], mtmp[0][:, :], gb_s[:, li * 64:(li + 1) * 64], ALU.mult),
                    reads=[("mt0",), ("gb",)], writes=[("mt1",)], tag="m_g")
                S.op("pe", lambda P, bk=pbank: P.matmul(bk[0:64, 0:64], wm_s[:, :], mtmp[1][:, :],
                                                        start=True, stop=True),
                     reads=[("wm",), ("mt1",)], writes=[pkey], tag="m_mm1")
                S.op("dve", lambda V, bk=pbank, i=idx: V.tensor_scalar(
                    mtmp[2][:, :], bk[0:64, 0:64], s2_s[:, i:i + 1], None, ALU.mult),
                    reads=[pkey, ("s2",)], writes=[("mt2",)], tag="m_s2")
                S.op("pe", lambda P, bk=pbank: P.matmul(bk[0:64, 0:64], hpT_s[:, :], mtmp[2][:, :],
                                                        start=True, stop=True),
                     reads=[("hpT",), ("mt2",)], writes=[pkey], tag="m_mm2")
                S.op("dve", lambda V, bk=pbank, i=idx, r=r: V.tensor_scalar(
                    mstage[:, r * 64:(r + 1) * 64], bk[0:64, 0:64], s1p_s[:, i:i + 1],
                    None, ALU.mult),
                    reads=[pkey, ("s1p",)], writes=[("Mst", r)], tag="m_s1")
            if "p" not in MPARTS:
                continue
            rkeys = [("Mst", r) for r in range(4)]
            S.op("gp", lambda G, g=g, hp=hp: G.dma_start(
                Mble[g][32 * hp:32 * hp + 32, 256 * hp:256 * hp + 256],
                mstage[0:32, :]),
                reads=rkeys + [("Mble0", g)], writes=[("Mble", g, hp)], dma="d_mpl", tag="mple")
            S.op("gp", lambda G, g=g, hp=hp: G.dma_start(
                Mblo[g][32 * hp:32 * hp + 32, 256 * hp:256 * hp + 256],
                mstage[32:64, :]),
                reads=rkeys + [("Mblo0", g)], writes=[("Mblo", g, hp)], dma="d_mpl", tag="mplo")

    S.barrier()
    esM.close()

    MKE = [("Mble", g, hp) for g in range(4) for hp in range(4)]

    def MKO(g):
        return [("Mblo", g, hp) for hp in range(4)]

    def MKEg(g):
        return [("Mble", g, hp) for hp in range(4)]

    def emit_ln(xkey, xtile, lnkey, lntile, rowf, rowb, kp, tag):
        for c in range(8):
            S.op("act", lambda A, c=c: A.activation(xsq_s[c % 2][:, :], xtile[:, c, :], AF.Square),
                 reads=[xkey], writes=[(kp + "xsq", c % 2)], tag=f"{tag}_sq")
            S.op("pe", lambda P, c=c: P.matmul(
                pb[6][0:1, :], ones_c[:, :], xtile[:, c, :], start=(c == 0), stop=(c == 7)),
                reads=[xkey, ("ones_c",)], writes=[PBK[6][0]], tag=f"{tag}_sum")
            S.op("pe", lambda P, c=c: P.matmul(
                pb[7][0:1, :], ones_c[:, :], xsq_s[c % 2][:, :], start=(c == 0), stop=(c == 7)),
                reads=[(kp + "xsq", c % 2), ("ones_c",)], writes=[PBK[7][0]], tag=f"{tag}_ssq")
        S.op("dve", lambda V: V.tensor_scalar(rowb[0:1, 0:512], pb[6][0:1, :],
                                              1.0 / 1024.0, None, ALU.mult),
             reads=[PBK[6][0]], writes=[(kp + "meanb",)], tag=f"{tag}_mean")
        S.op("dve", lambda V: V.tensor_tensor(rowf[0:1, 0:512], rowb[0:1, 0:512],
                                              rowb[0:1, 0:512], ALU.mult),
             reads=[(kp + "meanb",)], writes=[(kp + "msq",)], tag=f"{tag}_msq")
        S.op("dve", lambda V: V.scalar_tensor_tensor(
            rowf[0:1, 512:1024], pb[7][0:1, :], 1.0 / 1024.0, rowf[0:1, 0:512],
            ALU.mult, ALU.subtract),
            reads=[PBK[7][0], (kp + "msq",)], writes=[(kp + "var",)], tag=f"{tag}_var")
        S.op("act", lambda A: A.activation(rowf[0:1, 1024:1536], rowf[0:1, 512:1024],
                                           AF.Ln, bias=acst[0:1, 0:1], scale=1.0),
             reads=[(kp + "var",), ("acst",)], writes=[(kp + "lnv",)], tag=f"{tag}_ln")
        S.op("act", lambda A: A.activation(rowb[0:1, 512:1024], rowf[0:1, 1024:1536],
                                           AF.Exp, bias=0.0, scale=acst[0:1, 1:2]),
             reads=[(kp + "lnv",), ("acst",)], writes=[(kp + "rstd",)], tag=f"{tag}_rstd")
        S.op("pe", lambda P: P.matmul(pb[6][:, :], ones_r[0:1, :], rowb[0:1, 0:512],
                                      start=True, stop=True),
             reads=[(kp + "meanb",), ("ones_r",)], writes=[PBK[6][0]], tag=f"{tag}_bm")
        S.op("pe", lambda P: P.matmul(pb[7][:, :], ones_r[0:1, :], rowb[0:1, 512:1024],
                                      start=True, stop=True),
             reads=[(kp + "rstd",), ("ones_r",)], writes=[PBK[7][0]], tag=f"{tag}_br")
        for c in range(8):
            S.op("dve", lambda V, c=c: V.tensor_tensor(
                tmp_f[c % 2][:, :], xtile[:, c, :], pb[6][:, :], ALU.subtract),
                reads=[xkey, PBK[6][0]], writes=[(kp + "tmp", c % 2)], tag=f"{tag}_sub")
            S.op("dve", lambda V, c=c: V.tensor_tensor(
                lntile[:, c, :], tmp_f[c % 2][:, :], pb[7][:, :], ALU.mult),
                reads=[(kp + "tmp", c % 2), PBK[7][0]], writes=[lnkey], tag=f"{tag}_mul")

    def emit_p1x(j):
        jp = j % 2
        S.op("gp", lambda G, jp=jp, jt0=j * T: G.dma_start(
            xbf2[jp][:, :, :],
            xT.ap().rearrange("(c p) t -> p c t", p=128)[:, :, jt0:jt0 + T]),
            writes=[("xbf", jp)], dma=f"d_xb{jp}", tag="p1_x")

    NP1 = NT_EMIT if P1_ON else 0
    for j in range(min(2, NP1)):
        emit_p1x(j)
    for i in range(NP1):
        par = i % 2
        t0 = i * T
        xbf = xbf2[par]
        if i + 2 < NP1:
            emit_p1x(i + 2)
        S.op("sp", lambda Y, par=par, t0=t0: Y.dma_start(
            cos_t[par][:, :], costab.ap()[:, t0:t0 + T]),
            writes=[("cos", par)], dma=f"d_cs{par}", tag="p1_cos")
        S.op("sp", lambda Y, par=par, t0=t0: Y.dma_start(
            sin_t[par][:, :], sintab.ap()[:, t0:t0 + T]),
            writes=[("sin", par)], dma=f"d_cs{par}", tag="p1_sin")
        wc_list = [i] if NT_EMIT >= 16 else (list(range(16)) if i == 0 else [])
        for ci in wc_list:
            if ci < 8:
                S.op("gp", lambda G, e=ci: G.dma_start(w1bf.ap()[e], w1T.ap()[e]),
                     writes=[("w1bf", ci)], dma="d_wc", tag="wcast1")
            else:
                S.op("gp", lambda G, e=ci - 8: G.dma_start(w2bf.ap()[e], w2T.ap()[e]),
                     writes=[("w2bf", ci - 8)], dma="d_wc", tag="wcast2")

        emit_ln(("xbf", par), xbf, ("ln1",), ln1, row1, rowb1, "l1", "ln1")

        for oc in range(16):
            bank, bkey = pb[oc % 2], PBK[oc % 2]
            dst = qbf if oc < 8 else kbf
            dkey = ("qbf", oc) if oc < 8 else ("kbf", oc - 8)
            for kc in range(8):
                S.op("pe", lambda P, b=bank, oc=oc, kc=kc: P.matmul(
                    b[:, :], wqkv_s[:, kc, oc * 128:(oc + 1) * 128], ln1[:, kc, :],
                    start=(kc == 0), stop=(kc == 7)),
                    reads=[("wqkv",), ("ln1",)], writes=[bkey], tag="qk_mm")
            S.op("act", lambda A, b=bank, d=dst, c=oc % 8: A.activation(
                d[:, c, :], b[:, :], AF.Identity),
                reads=[bkey], writes=[dkey], tag="qk_ev")
        for tc in range(TC):
            for vh in range(2):
                j = (tc * 2 + vh) % 2
                bank, bkey = pb[j], PBK[j]
                for kc in range(8):
                    S.op("pe", lambda P, b=bank, tc=tc, vh=vh, kc=kc: P.matmul(
                        b[:, :], ln1[:, kc, tc * 128:(tc + 1) * 128],
                        wqkv_s[:, kc, 2048 + vh * 512:2048 + (vh + 1) * 512],
                        start=(kc == 0), stop=(kc == 7)),
                        reads=[("wqkv",), ("ln1",)], writes=[bkey], tag="v_mm")
                S.op("act", lambda A, b=bank, tc=tc, vh=vh: A.activation(
                    vtok[:, tc, vh * 8:(vh + 1) * 8, 0:64],
                    b[:, :].rearrange("p (h d) -> p h d", d=64), AF.Identity),
                    reads=[bkey], writes=[("vtok", tc, vh)], tag="v_ev")

        for g in range(4):
            S.op("act", lambda A, g=g: A.activation(sq_s[0][:, :], kbf[:, 2 * g, :], AF.Square),
                 reads=[("kbf", 2 * g)], writes=[("sqE",)], tag="ksqE")
            S.op("act", lambda A, g=g: A.activation(sq_s[1][:, :], kbf[:, 2 * g + 1, :], AF.Square),
                 reads=[("kbf", 2 * g + 1)], writes=[("sqO",)], tag="ksqO")
            for tc in range(TC):
                S.op("pe", lambda P, tc=tc: P.matmul(
                    pb[2][:, tc * 4:tc * 4 + 4], sq_s[0][:, tc * 128:(tc + 1) * 128],
                    i4b[:, :], start=(tc == 0), stop=False),
                    reads=[("sqE",), ("i4b",)], writes=[PBK[2][0]], tag="xnk_mmE")
                S.op("pe", lambda P, tc=tc: P.matmul(
                    pb[2][:, tc * 4:tc * 4 + 4], sq_s[1][:, tc * 128:(tc + 1) * 128],
                    i4b[:, :], start=False, stop=(tc == 3)),
                    reads=[("sqO",), ("i4b",)], writes=[PBK[2][0]], tag="xnk_mmO")
            S.op("dve", lambda V, g=g: V.tensor_scalar(
                xnk_s[:, g, :], pb[2][:, 0:16], -0.5, -LN16, ALU.mult, ALU.add),
                reads=[PBK[2][0]], writes=[("xnk", g)], tag="xnk_ev")
            S.op("act", lambda A, g=g: A.activation(sq_s[0][:, :], qbf[:, 2 * g, :], AF.Square),
                 reads=[("qbf", 2 * g)], writes=[("sqE",)], tag="qsqE")
            S.op("act", lambda A, g=g: A.activation(sq_s[1][:, :], qbf[:, 2 * g + 1, :], AF.Square),
                 reads=[("qbf", 2 * g + 1)], writes=[("sqO",)], tag="qsqO")
            S.op("pe", lambda P: P.matmul(pb[2][0:4, :], i4b[:, :], sq_s[0][:, :],
                                          start=True, stop=False),
                 reads=[("sqE",), ("i4b",)], writes=[PBK[2][0]], tag="xnq_mmE")
            S.op("pe", lambda P: P.matmul(pb[2][0:4, :], i4b[:, :], sq_s[1][:, :],
                                          start=False, stop=True),
                 reads=[("sqO",), ("i4b",)], writes=[PBK[2][0]], tag="xnq_mmO")
            S.op("dve", lambda V, g=g: V.tensor_scalar(
                xnq_s[:, g, :], pb[2][0:4, :], -0.5, -LN16, ALU.mult, ALU.add),
                reads=[PBK[2][0]], writes=[("xnq", g)], tag="xnq_ev")

            for (src, k0, k1, dE, dO, dkE, dkO) in (
                (kbf, ("kbf", 2 * g), ("kbf", 2 * g + 1), ropeEk, ropeOk, ("rEk",), ("rOk",)),
                (qbf, ("qbf", 2 * g), ("qbf", 2 * g + 1), ropeEq, ropeOq, ("rEq",), ("rOq",)),
            ):
                S.op("dve", lambda V, s=src, g=g, par=par: V.tensor_tensor(
                    tmp_bf[0][:, :], s[:, 2 * g, :], cos_t[par][:, :], ALU.mult),
                    reads=[k0, ("cos", par)], writes=[("rt0",)], tag="rp1")
                S.op("dve", lambda V, s=src, g=g, par=par: V.tensor_tensor(
                    tmp_bf[1][:, :], s[:, 2 * g + 1, :], sin_t[par][:, :], ALU.mult),
                    reads=[k1, ("sin", par)], writes=[("rt1",)], tag="rp2")
                S.op("dve", lambda V, d=dE: V.tensor_tensor(
                    d[:, :], tmp_bf[0][:, :], tmp_bf[1][:, :], ALU.subtract),
                    reads=[("rt0",), ("rt1",)], writes=[dkE], tag="rp3")
                S.op("dve", lambda V, s=src, g=g, par=par: V.tensor_tensor(
                    tmp_bf[0][:, :], s[:, 2 * g, :], sin_t[par][:, :], ALU.mult),
                    reads=[k0, ("sin", par)], writes=[("rt0",)], tag="rp4")
                S.op("dve", lambda V, s=src, g=g, par=par: V.tensor_tensor(
                    tmp_bf[1][:, :], s[:, 2 * g + 1, :], cos_t[par][:, :], ALU.mult),
                    reads=[k1, ("cos", par)], writes=[("rt1",)], tag="rp5")
                S.op("dve", lambda V, d=dO: V.tensor_tensor(
                    d[:, :], tmp_bf[0][:, :], tmp_bf[1][:, :], ALU.add),
                    reads=[("rt0",), ("rt1",)], writes=[dkO], tag="rp6")

            for tc in range(TC):
                bA, bB = (pb[3], pb[4]) if tc % 2 == 0 else (pb[5], pb[6])
                kA, kB = (PBK[3][0], PBK[4][0]) if tc % 2 == 0 else (PBK[5][0], PBK[6][0])
                for half, (bank, bkey) in enumerate(((bA, kA), (bB, kB))):
                    S.op("pe", lambda P, b=bank, g=g, tc=tc, hf=half: P.matmul(
                        b[:, :], ropeEk[:, tc * 128:(tc + 1) * 128],
                        Mble[g][:, hf * 512:(hf + 1) * 512], start=True, stop=False),
                        reads=[("rEk",)] + MKEg(g), writes=[bkey], tag="phk_mmE")
                    S.op("pe", lambda P, b=bank, g=g, tc=tc, hf=half: P.matmul(
                        b[:, :], ropeOk[:, tc * 128:(tc + 1) * 128],
                        Mblo[g][:, hf * 512:(hf + 1) * 512], start=False, stop=True),
                        reads=[("rOk",)] + MKO(g), writes=[bkey], tag="phk_mmO")
                for hp in range(4):
                    bank, bkey = (bA, kA) if hp < 2 else (bB, kB)
                    off = (hp % 2) * 256
                    S.op("act", lambda A, b=bank, g=g, tc=tc, hp=hp, off=off: A.activation(
                        kphi_s[:, tc, hp * 256:(hp + 1) * 256], b[:, off:off + 256],
                        AF.Exp, bias=xnk_s[:, g, tc * 4 + hp:tc * 4 + hp + 1], scale=1.0),
                        reads=[bkey, ("xnk", g)], writes=[("kphi", tc, hp)], tag="phk_exp")
            for hp in range(4):
                h = 4 * g + hp
                n = 0
                for mc in range(2):
                    for tc in range(TC):
                        S.op("pe", lambda P, tc=tc, hp=hp, mc=mc, h=h,
                             first=(n == 0), last=(n == 7): P.matmul(
                            pb[7][:, mc * 65:mc * 65 + 65],
                            kphi_s[:, tc, hp * 256 + mc * 128:hp * 256 + (mc + 1) * 128],
                            vtok[:, tc, h, 0:65], start=first, stop=last),
                            reads=[("kphi", tc, hp), ("vtok", tc, h // 8), ("vtok_ones",)],
                            writes=[PBK[7][0]], tag="S_mm")
                        n += 1
                S.op("dve", lambda V, h=h: V.tensor_tensor(
                    Sacc[:, h, :], Sacc[:, h, :], pb[7][:, 0:130], ALU.add),
                    reads=[PBK[7][0], ("Sacc", h)], writes=[("Sacc", h)], tag="S_fl")

            qsp = (i * 4 + g) % 3
            for mc in range(8):
                bank, bkey = pb[mc % 2], PBK[mc % 2]
                hp = mc // 2
                S.op("pe", lambda P, b=bank, g=g, mc=mc: P.matmul(
                    b[:, :], Mble[g][:, mc * 128:(mc + 1) * 128], ropeEq[:, :],
                    start=True, stop=False),
                    reads=[("rEq",)] + MKEg(g), writes=[bkey], tag="phq_mmE")
                S.op("pe", lambda P, b=bank, g=g, mc=mc: P.matmul(
                    b[:, :], Mblo[g][:, mc * 128:(mc + 1) * 128], ropeOq[:, :],
                    start=False, stop=False),
                    reads=[("rOq",)] + MKO(g), writes=[bkey], tag="phq_mmO")
                S.op("pe", lambda P, b=bank, g=g, hp=hp: P.matmul(
                    b[:, :], i4q[:, hp * 128:(hp + 1) * 128], xnq_s[:, g, :],
                    start=False, stop=True),
                    reads=[("xnq", g), ("i4q",)], writes=[bkey], tag="phq_r1")
                S.op("act", lambda A, b=bank, qsp=qsp, mc=mc: A.activation(
                    qstage[qsp][:, mc, :], b[:, :], AF.Exp),
                    reads=[bkey], writes=[("qst", qsp, mc)], tag="phq_exp")
            S.op("gp", lambda G, qsp=qsp, g=g, t0=t0: G.dma_start(
                qphi_d.ap().rearrange("(r p) t -> p r t", p=128)[:, 8 * g:8 * g + 8, t0:t0 + T],
                qstage[qsp][:, :, :]),
                reads=[("qst", qsp, mc) for mc in range(8)], dma=f"d_qsp{qsp}", tag="q_spill")

    S.op("dve", lambda V: V.tensor_copy(Sbf[:, :, :], Sacc[:, :, :]),
         reads=[("Sacc", h) for h in range(NH)], writes=[("Sbf",)], tag="S_cast")

    S.barrier()
    es1.close()

    es2 = contextlib.ExitStack()
    wout_s = es2.enter_context(nc.sbuf_tensor("wout_s", [128, 8, E], BF16))
    b1s = es2.enter_context(nc.sbuf_tensor("b1s", [128, NEXP, 32], F32))
    b2s = es2.enter_context(nc.sbuf_tensor("b2s", [128, NEXP, 8], F32))
    rtr_s = es2.enter_context(nc.sbuf_tensor("rtr_s", [128, 8, 8], BF16))
    xs2 = [es2.enter_context(nc.sbuf_tensor(f"x_s{p}", [128, 8, T], F32)) for p in range(2)]
    qph = [es2.enter_context(nc.sbuf_tensor(f"qph{p}", [128, 8, T], BF16)) for p in range(2)]
    ln2 = es2.enter_context(nc.sbuf_tensor("ln2", [128, 8, T], BF16))
    att = es2.enter_context(nc.sbuf_tensor("att", [128, 8, T], BF16))
    gsb = es2.enter_context(nc.sbuf_tensor("gsb", [128, 32, T], BF16))
    w1s = [es2.enter_context(nc.sbuf_tensor(f"w1s{p}", [128, 8, T], BF16)) for p in range(2)]
    w2s = [es2.enter_context(nc.sbuf_tensor(f"w2s{p}", [128, 32, 128], BF16)) for p in range(2)]
    gateb = es2.enter_context(nc.sbuf_tensor("gateb", [128, 8, T], BF16))
    gfm = es2.enter_context(nc.sbuf_tensor("gfm", [8, T], BF16))
    rfm = es2.enter_context(nc.sbuf_tensor("rfm", [8, T], F32))
    rtk = es2.enter_context(nc.sbuf_tensor("rtk", [128, 8], F32))
    etk = es2.enter_context(nc.sbuf_tensor("etk", [128, 8], F32))
    gtk = es2.enter_context(nc.sbuf_tensor("gtk", [128, 8], F32))
    tscr = es2.enter_context(nc.sbuf_tensor("tscr", [128, 8], F32))
    nm_s = [es2.enter_context(nc.sbuf_tensor(f"nm{j}", [64, T], BF16)) for j in range(4)]
    dn_s = es2.enter_context(nc.sbuf_tensor("dn_s", [1, 4, 512], F32))
    rc_s = es2.enter_context(nc.sbuf_tensor("rc_s", [1, 4, 512], BF16))
    lnr_s = es2.enter_context(nc.sbuf_tensor("lnr_s", [1, 4, 512], F32))
    tmp2f = [es2.enter_context(nc.sbuf_tensor(f"tmp2f{j}", [128, T], F32)) for j in range(2)]
    row2 = es2.enter_context(nc.sbuf_tensor("row2", [1, 3 * 512], F32))
    rowb2 = es2.enter_context(nc.sbuf_tensor("rowb2", [1, 2 * 512], BF16))
    ysq2 = [es2.enter_context(nc.sbuf_tensor(f"ysq2{j}", [128, T], BF16)) for j in range(2)]
    ybf2 = [es2.enter_context(nc.sbuf_tensor(f"ybf2{j}", [128, T], BF16)) for j in range(2)]

    S.op("gp", lambda G: G.dma_start(wout_s[:, :, :],
                                     woutT.ap().rearrange("(c p) o -> p c o", p=128)),
         writes=[("wout",)], dma="d_init", tag="ld_wout")
    S.op("gp", lambda G: G.dma_start(b1s[:, :, :], b1T.ap().rearrange("e p c -> p e c")),
         writes=[("b1",)], dma="d_init", tag="ld_b1")
    S.op("gp", lambda G: G.dma_start(b2s[:, :, :], b2T.ap().rearrange("e p c -> p e c")),
         writes=[("b2",)], dma="d_init", tag="ld_b2")
    S.op("gp", lambda G: G.dma_start(rtr_s[:, :, :],
                                     routerT.ap().rearrange("(c p) e -> p c e", p=128)),
         writes=[("rtr",)], dma="d_init", tag="ld_rtr")

    def emit_ln2_chunk(c, x_s, ykey):
        kp = "l2"
        S.op("dve", lambda V, c=c, x_s=x_s: V.tensor_copy(ybf2[c % 2][:, :], x_s[:, c, :]),
             reads=[ykey], writes=[(kp + "ybf", c % 2)], tag="l2_cp")
        S.op("act", lambda A, c=c: A.activation(
            ysq2[c % 2][:, :], ybf2[c % 2][:, :], AF.Square),
            reads=[(kp + "ybf", c % 2)], writes=[(kp + "ysq", c % 2)], tag="l2_sq")
        S.op("pe", lambda P, c=c: P.matmul(
            pb[6][0:1, :], ones_c[:, :], ybf2[c % 2][:, :], start=(c == 0), stop=(c == 7)),
            reads=[(kp + "ybf", c % 2), ("ones_c",)], writes=[PBK[6][0]], tag="l2_sum")
        S.op("pe", lambda P, c=c: P.matmul(
            pb[7][0:1, :], ones_c[:, :], ysq2[c % 2][:, :], start=(c == 0), stop=(c == 7)),
            reads=[(kp + "ysq", c % 2), ("ones_c",)], writes=[PBK[7][0]], tag="l2_ssq")

    def emit_ln2_tail(x_s, ykey):
        kp = "l2"
        S.op("dve", lambda V: V.tensor_scalar(rowb2[0:1, 0:512], pb[6][0:1, :],
                                              1.0 / 1024.0, None, ALU.mult),
             reads=[PBK[6][0]], writes=[(kp + "meanb",)], tag="l2_mean")
        S.op("dve", lambda V: V.tensor_tensor(row2[0:1, 0:512], rowb2[0:1, 0:512],
                                              rowb2[0:1, 0:512], ALU.mult),
             reads=[(kp + "meanb",)], writes=[(kp + "msq",)], tag="l2_msq")
        S.op("dve", lambda V: V.scalar_tensor_tensor(
            row2[0:1, 512:1024], pb[7][0:1, :], 1.0 / 1024.0, row2[0:1, 0:512],
            ALU.mult, ALU.subtract),
            reads=[PBK[7][0], (kp + "msq",)], writes=[(kp + "var",)], tag="l2_var")
        S.op("act", lambda A: A.activation(row2[0:1, 1024:1536], row2[0:1, 512:1024],
                                           AF.Ln, bias=acst[0:1, 0:1], scale=1.0),
             reads=[(kp + "var",), ("acst",)], writes=[(kp + "lnv",)], tag="l2_ln")
        S.op("act", lambda A: A.activation(rowb2[0:1, 512:1024], row2[0:1, 1024:1536],
                                           AF.Exp, bias=0.0, scale=acst[0:1, 1:2]),
             reads=[(kp + "lnv",), ("acst",)], writes=[(kp + "rstd",)], tag="l2_rstd")
        S.op("pe", lambda P: P.matmul(pb[6][:, :], ones_r[0:1, :], rowb2[0:1, 0:512],
                                      start=True, stop=True),
             reads=[(kp + "meanb",), ("ones_r",)], writes=[PBK[6][0]], tag="l2_bm")
        S.op("pe", lambda P: P.matmul(pb[7][:, :], ones_r[0:1, :], rowb2[0:1, 512:1024],
                                      start=True, stop=True),
             reads=[(kp + "rstd",), ("ones_r",)], writes=[PBK[7][0]], tag="l2_br")
        for c in range(8):
            S.op("dve", lambda V, c=c, x_s=x_s: V.tensor_tensor(
                tmp2f[c % 2][:, :], x_s[:, c, :], pb[6][:, :], ALU.subtract),
                reads=[ykey, PBK[6][0]], writes=[("o2t", c % 2)], tag="l2_sub")
            S.op("dve", lambda V, c=c: V.tensor_tensor(
                ln2[:, c, :], tmp2f[c % 2][:, :], pb[7][:, :], ALU.mult),
                reads=[("o2t", c % 2), PBK[7][0]], writes=[("ln2",)], tag="l2_mul")

    for i in range(NT_EMIT if P2_ON else 0):
        t0 = i * T
        par = i % 2
        x_s = xs2[par]
        xkey, ykey = ("x", par), ("y", par)
        S.op("gp", lambda G, t0=t0, par=par: G.dma_start(
            xs2[par][:, :, :],
            xT.ap().rearrange("(c p) t -> p c t", p=128)[:, :, t0:t0 + T]),
            writes=[xkey, ykey], dma=f"d_x{par}", tag="p2_x")
        if not FFN_ONLY and i == 0:
            S.op("sp", lambda Y, t0=t0: Y.dma_start(
                qph[0][:, :, :],
                qphi_d.ap().rearrange("(r p) t -> p r t", p=128)[:, 0:8, t0:t0 + T]),
                writes=[("qph", 0)], dma="d_qh0", tag="p2_qphi")

        if FFN_ONLY:
            S.op("dve", lambda V: V.memset(ln2[:, :, :], 0.001), writes=[("ln2",)], tag="ms_ln2")
            S.op("dve", lambda V: V.memset(gateb[:, :, :], 0.25),
                 writes=[("gateb", e) for e in range(NEXP)], tag="ms_gb")
            S.op("dve", lambda V, x_s=x_s: V.memset(x_s[:, :, :], 0.0), reads=[xkey],
                 writes=[ykey], tag="ms_y")
        def emit_numer(h):
            qtr = h // 4
            qb = qtr % 2
            if h % 4 == 0:
                qn = qtr + 1
                tn = t0 if qn < 4 else t0 + T
                qn = qn % 4
                if tn < NT_EMIT * T:
                    qb2 = qn % 2
                    S.op("sp", lambda Y, q=qn, qb2=qb2, tn=tn: Y.dma_start(
                        qph[qb2][:, :, :],
                        qphi_d.ap().rearrange("(r p) t -> p r t", p=128)[:, 8 * q:8 * q + 8, tn:tn + T]),
                        writes=[("qph", qb2)], dma=f"d_qh{qb2}", tag="p2_qphi")
            q0 = (h % 4) * 2
            bA, kA = pb[h % 4], PBK[h % 4][0]
            for mc in range(2):
                S.op("pe", lambda P, b=bA, h=h, qb=qb, q0=q0, mc=mc: P.matmul(
                    b[0:65, :], Sbf[:, h, mc * 65:(mc + 1) * 65], qph[qb][:, q0 + mc, :],
                    start=(mc == 0), stop=(mc == 1)),
                    reads=[("Sbf",), ("qph", qb)], writes=[kA], tag="nm_mm")

        def emit_finish(h):
            j, j2 = h % 4, h % 2
            bA, kA = pb[j], PBK[j][0]
            bB, kB = pb[4 + j2], PBK[4 + j2][0]
            S.op("dve", lambda V, b=bA, j=j: V.tensor_scalar(
                dn_s[0:1, j, :], b[64:65, :], EPS, None, ALU.max),
                reads=[kA], writes=[("dn", j)], tag="nm_max")
            S.op("act", lambda A, j=j: A.activation(lnr_s[0:1, j, :], dn_s[0:1, j, :], AF.Ln),
                 reads=[("dn", j)], writes=[("lnr", j)], tag="nm_ln")
            S.op("act", lambda A, j=j: A.activation(rc_s[0:1, j, :], lnr_s[0:1, j, :],
                                                    AF.Exp, bias=0.0, scale=acst[0:1, 2:3]),
                 reads=[("lnr", j), ("acst",)], writes=[("rc", j)], tag="nm_rc")
            S.op("pe", lambda P, b=bB, j=j: P.matmul(
                b[0:64, :], ones_r[0:1, 0:64], rc_s[0:1, j, :], start=True, stop=True),
                reads=[("rc", j), ("ones_r",)], writes=[kB], tag="nm_bc")
            S.op("act", lambda A, b=bA, j=j: A.activation(
                nm_s[j][:, :], b[0:64, :], AF.Identity),
                reads=[kA], writes=[("nm", j)], tag="nm_ev")
            S.op("dve", lambda V, b=bB, h=h, j=j: V.tensor_tensor(
                att[(h % 2) * 64:(h % 2) * 64 + 64, h // 2, :], nm_s[j][:, :], b[0:64, :], ALU.mult),
                reads=[("nm", j), kB], writes=[("att", h)], tag="nm_mul")

        for h in range(NH):
            emit_numer(h)
            if h >= 3:
                emit_finish(h - 3)
        for h in range(NH - 3, NH):
            emit_finish(h)

        for oc in range(8 if not FFN_ONLY else 0):
            bank, bkey = pb[4 + oc % 2], PBK[4 + oc % 2][0]
            for kc in range(8):
                S.op("pe", lambda P, b=bank, oc=oc, kc=kc: P.matmul(
                    b[:, :], wout_s[:, kc, oc * 128:(oc + 1) * 128], att[:, kc, :],
                    start=(kc == 0), stop=(kc == 7)),
                    reads=[("wout",), ("att", kc * 2), ("att", kc * 2 + 1)],
                    writes=[bkey], tag="op_mm")
            S.op("dve", lambda V, b=bank, oc=oc, x_s=x_s: V.tensor_tensor(
                x_s[:, oc, :], x_s[:, oc, :], b[:, :], ALU.add),
                reads=[xkey, bkey], writes=[ykey], tag="op_res")
            if oc >= 1:
                emit_ln2_chunk(oc - 1, x_s, ykey)
        if not FFN_ONLY:
            emit_ln2_chunk(7, x_s, ykey)
            emit_ln2_tail(x_s, ykey)

        def emit_gate_head():
            for kc in range(8):
                S.op("pe", lambda P, kc=kc: P.matmul(
                    pb[6][0:8, :], rtr_s[:, kc, :], ln2[:, kc, :], start=(kc == 0), stop=(kc == 7)),
                    reads=[("rtr",), ("ln2",)], writes=[PBK[6][0]], tag="rt_mm")
            S.op("dve", lambda V: V.tensor_copy(rfm[:, :], pb[6][0:8, :]),
                 reads=[PBK[6][0]], writes=[("rfm",)], tag="rt_ev")

        def emit_gate_tc(tc):
            if True:
                S.op("pe", lambda P, tc=tc: P.transpose(
                    pb[7][0:128, 0:8], rfm[:, tc * 128:(tc + 1) * 128], id8_s[:, :]),
                    reads=[("rfm",), ("id8",)], writes=[PBK[7][0]], tag="rt_tr")
                S.op("dve", lambda V: V.tensor_copy(rtk[:, :], pb[7][0:128, 0:8]),
                     reads=[PBK[7][0]], writes=[("rtk",)], tag="tk_cp")
                S.op("dve", lambda V: V.reduce_max(tscr[:, 0:1], rtk[:, :], axis=AX),
                     reads=[("rtk",)], writes=[("tscr",)], tag="tk_m1")
                S.op("dve", lambda V: V.tensor_scalar(
                    gtk[:, :], rtk[:, :], tscr[:, 0:1], None, ALU.is_equal),
                    reads=[("rtk",), ("tscr",)], writes=[("gtk",)], tag="tk_eq1")
                S.op("dve", lambda V: V.scalar_tensor_tensor(
                    rtk[:, :], gtk[:, :], -1e30, rtk[:, :], ALU.mult, ALU.add),
                    reads=[("gtk",), ("rtk",)], writes=[("rtk",)], tag="tk_rm")
                S.op("dve", lambda V: V.reduce_max(tscr[:, 1:2], rtk[:, :], axis=AX),
                     reads=[("rtk",)], writes=[("tscr",)], tag="tk_m2")
                S.op("dve", lambda V: V.tensor_tensor(
                    tscr[:, 2:3], tscr[:, 1:2], tscr[:, 0:1], ALU.subtract),
                    reads=[("tscr",)], writes=[("tscr",)], tag="tk_d21")
                S.op("act", lambda A: A.activation(tscr[:, 3:4], tscr[:, 2:3], AF.Exp),
                     reads=[("tscr",)], writes=[("tscr",)], tag="tk_e21")
                S.op("dve", lambda V: V.tensor_scalar(
                    tscr[:, 4:5], tscr[:, 3:4], 1.0, None, ALU.add),
                    reads=[("tscr",)], writes=[("tscr",)], tag="tk_e21p")
                S.op("dve", lambda V: V.reciprocal(tscr[:, 5:6], tscr[:, 4:5]),
                     reads=[("tscr",)], writes=[("tscr",)], tag="tk_w1")
                S.op("dve", lambda V: V.tensor_tensor(
                    tscr[:, 6:7], tscr[:, 3:4], tscr[:, 5:6], ALU.mult),
                    reads=[("tscr",)], writes=[("tscr",)], tag="tk_w2")
                S.op("dve", lambda V: V.tensor_scalar(
                    etk[:, :], rtk[:, :], tscr[:, 1:2], None, ALU.is_equal),
                    reads=[("rtk",), ("tscr",)], writes=[("etk",)], tag="tk_eq2")
                S.op("dve", lambda V: V.tensor_scalar(
                    gtk[:, :], gtk[:, :], tscr[:, 5:6], None, ALU.mult),
                    reads=[("gtk",), ("tscr",)], writes=[("gtk",)], tag="tk_gw1")
                S.op("dve", lambda V: V.scalar_tensor_tensor(
                    gtk[:, :], etk[:, :], tscr[:, 6:7], gtk[:, :], ALU.mult, ALU.add),
                    reads=[("etk",), ("tscr",), ("gtk",)], writes=[("gtk",)], tag="tk_gate")
                S.op("pe", lambda P, tc=tc: P.transpose(
                    pb[7][0:8, 0:128], gtk[:, :], id128_s[:, :]),
                    reads=[("gtk",), ("id128",)], writes=[PBK[7][0]], tag="tk_tr")
                S.op("dve", lambda V, tc=tc: V.tensor_copy(gfm[:, tc * 128:(tc + 1) * 128],
                                                           pb[7][0:8, 0:128]),
                     reads=[PBK[7][0]], writes=[("gfm",)], tag="tk_gfm")

        def emit_gate_tail():
            for e in range(NEXP):
                bank, bkey = pb[4 + e % 2], PBK[4 + e % 2][0]
                S.op("pe", lambda P, b=bank, e=e: P.matmul(
                    b[:, :], i8b[:, e * 128:(e + 1) * 128], gfm[:, :], start=True, stop=True),
                    reads=[("gfm",), ("i8b",)], writes=[bkey], tag="gb_mm")
                S.op("act", lambda A, b=bank, e=e: A.activation(
                    gateb[:, e, :], b[:, :], AF.Identity),
                    reads=[bkey], writes=[("gateb", e)], tag="gb_ev")

        for e in range(NEXP):
            for ft in range(8):
                wp = (e * 8 + ft) % 2
                if FFN_NODMA:
                    pass
                else:
                    S.op("sp", lambda Y, e=e, ft=ft, wp=wp: Y.dma_start(
                    w1s[wp][:, :, :],
                    w1bf.ap()[e].rearrange("(c p) f -> p c f", p=128)[:, :, ft * 512:(ft + 1) * 512]),
                    reads=[("w1bf", e)], writes=[("w1s", wp)], dma=f"d_w1{'ab'[wp]}", tag="w1_ld")
                for fcl in range(4):
                    bank, bkey = pb[fcl % 2], PBK[fcl % 2][0]
                    fc = ft * 4 + fcl
                    for kc in range(8):
                        S.op("pe", lambda P, b=bank, wp=wp, fcl=fcl, kc=kc: P.matmul(
                            b[:, :], w1s[wp][:, kc, fcl * 128:(fcl + 1) * 128], ln2[:, kc, :],
                            start=(kc == 0), stop=(kc == 7)),
                            reads=[("w1s", wp), ("ln2",)], writes=[bkey], tag="h1_mm")
                    S.op("act", lambda A, b=bank, e=e, fc=fc: A.activation(
                        gsb[:, fc, :], b[:, :], AF.Gelu, bias=b1s[:, e, fc:fc + 1], scale=1.0),
                        reads=[bkey, ("b1",)], writes=[("gsb", fc)], tag="h1_gelu")
                if e == 0 and not FFN_ONLY:
                    if ft == 0:
                        emit_gate_head()
                    elif ft <= 4:
                        emit_gate_tc(ft - 1)
                    elif ft == 5:
                        emit_gate_tail()
            for hc in range(8):
                wp = (e * 8 + hc) % 2
                if FFN_NODMA:
                    pass
                else:
                    S.op("sp", lambda Y, e=e, hc=hc, wp=wp: Y.dma_start(
                    w2s[wp][:, :, :],
                    w2bf.ap()[e].rearrange("(c p) o -> p c o", p=128)[:, :, hc * 128:(hc + 1) * 128]),
                    reads=[("w2bf", e)], writes=[("w2s", wp)], dma=f"d_w2{'ab'[wp]}", tag="w2_ld")
                bank, bkey = pb[2 + hc % 2], PBK[2 + hc % 2][0]
                for fc in range(32):
                    S.op("pe", lambda P, b=bank, wp=wp, fc=fc: P.matmul(
                        b[:, :], w2s[wp][:, fc, :], gsb[:, fc, :],
                        start=(fc == 0), stop=(fc == 31)),
                        reads=[("w2s", wp), ("gsb", fc)], writes=[bkey], tag="o2_mm")
                j = hc % 2
                S.op("dve", lambda V, b=bank, e=e, hc=hc, j=j: V.scalar_tensor_tensor(
                    tmp2f[j][:, :], b[:, :], b2s[:, e, hc:hc + 1], gateb[:, e, :],
                    ALU.add, ALU.mult),
                    reads=[bkey, ("b2",), ("gateb", e)], writes=[("o2t", j)], tag="o2_ev")
                S.op("dve", lambda V, hc=hc, j=j, x_s=x_s: V.tensor_tensor(
                    x_s[:, hc, :], x_s[:, hc, :], tmp2f[j][:, :], ALU.add),
                    reads=[("o2t", j), ykey], writes=[ykey], tag="o2_acc")

        S.op("gp", lambda G, t0=t0, x_s=x_s: G.dma_start(
            outT.ap().rearrange("(c p) t -> p c t", p=128)[:, :, t0:t0 + T],
            x_s[:, :, :]),
            reads=[ykey], dma="d_out0", tag="p2_out")

    final_waits = {"d_out0": 16 * S.dma_count.get("d_out0", 0)}
    final_waits = {k: v for k, v in final_waits.items() if v}
    run_engine = S.finalize(eng_sems, dma_sems, final_waits)

    with nc.Block() as block, nc.allow_low_precision(
            reason="bf16 matmul operand staging; fp32 accumulation everywhere"):
        block.tensor(lambda eng: run_engine(eng, "pe"))
        block.scalar(lambda eng: run_engine(eng, "act"))
        block.vector(lambda eng: run_engine(eng, "dve"))
        block.sync(lambda eng: run_engine(eng, "sp"))
        block.gpsimd(lambda eng: run_engine(eng, "gp"))

    es2.close()
    es.close()
    return nc


def _host_constants():
    def fwht(v):
        shape = v.shape
        Pn = shape[-1]
        h = 1
        while h < Pn:
            v = v.reshape(shape[:-1] + (-1, 2 * h))
            a, b = v[..., :h], v[..., h:]
            v = np.concatenate([a + b, a - b], axis=-1).reshape(shape)
            h *= 2
        return v

    W64 = fwht(np.eye(64, dtype=np.float64))
    rowperm = np.concatenate([np.arange(0, 64, 2), np.arange(1, 64, 2)])
    perm = np.zeros(1024, dtype=np.int64)
    n = 0
    for g in range(4):
        for parity in range(2):
            for hp in range(4):
                for j in range(32):
                    perm[n] = (4 * g + hp) * 64 + 2 * j + parity
                    n += 1
    freqs = 10000.0 ** (-np.arange(0, 64, 2) / 64.0)
    tpos = np.arange(L)
    ang = np.where(tpos[None, :] >= 1,
                   (tpos[None, :] - 1) * freqs[np.arange(128) % 32, None], 0.0)
    cos_t = np.cos(ang)
    sin_t = np.sin(ang)
    cos_t[:, 0] = 1.0
    sin_t[:, 0] = 0.0

    i4b = np.zeros((128, 4), dtype=np.float32)
    for d in range(128):
        i4b[d, d // 32] = 1.0
    i4q = np.zeros((4, 4 * 128), dtype=np.float32)
    for hp in range(4):
        i4q[hp, hp * 128:(hp + 1) * 128] = 1.0
    i8b = np.zeros((8, 8 * 128), dtype=np.float32)
    for e in range(8):
        i8b[e, e * 128:(e + 1) * 128] = 1.0

    return {
        "W64": W64, "rowperm": rowperm, "perm": perm,
        "costab": cos_t.astype(BF), "sintab": sin_t.astype(BF),
        "wm64": W64.astype(np.float32),
        "hpT": np.ascontiguousarray(W64[rowperm, :].T).astype(np.float32),
        "onescol": np.ones((128, 1), dtype=BF),
        "onesrow": np.ones((1, 128), dtype=BF),
        "ind4b": i4b.astype(BF), "ind4q": i4q.astype(BF), "ind8b": i8b.astype(BF),
        "id8": np.eye(8, dtype=np.float32), "id128": np.eye(128, dtype=np.float32),
        "c4096": np.full((1, 64), 1.0 / 4096.0, dtype=np.float32),
        "actconst": np.tile(np.array([[LNEPS, -0.5, -1.0]], dtype=np.float32), (128, 1)),
    }


_CACHE = {}


def kernel(L0_attn_x, L0_attn_w_qkv, L0_attn_w_out, L0_attn_sorf_signs,
           L0_attn_sorf_scales, L0_ffn_router_w, L0_ffn_w1, L0_ffn_b1,
           L0_ffn_w2, L0_ffn_b2):
    x = np.ascontiguousarray(np.asarray(L0_attn_x, dtype=np.float32))
    w_qkv = np.asarray(L0_attn_w_qkv, dtype=np.float32)
    w_out = np.asarray(L0_attn_w_out, dtype=np.float32)
    signs = np.asarray(L0_attn_sorf_signs, dtype=np.float32)
    scales = np.asarray(L0_attn_sorf_scales, dtype=np.float32)
    router_w = np.asarray(L0_ffn_router_w, dtype=np.float32)
    w1 = np.asarray(L0_ffn_w1, dtype=np.float32)
    b1 = np.asarray(L0_ffn_b1, dtype=np.float32)
    w2 = np.asarray(L0_ffn_w2, dtype=np.float32)
    b2 = np.asarray(L0_ffn_b2, dtype=np.float32)

    cst = _host_constants()
    perm = cst["perm"]

    wqkvT = np.ascontiguousarray(w_qkv.T)
    wq = wqkvT[:, 0:1024][:, perm]
    wk = wqkvT[:, 1024:2048][:, perm]
    wv = wqkvT[:, 2048:3072]
    WQKVT = np.ascontiguousarray(np.concatenate([wq, wk, wv], axis=1))

    shared = {
        "wqkvT": WQKVT,
        "woutT": np.ascontiguousarray(w_out.T),
        "w1T": np.ascontiguousarray(w1.transpose(0, 2, 1)),
        "w2T": np.ascontiguousarray(w2.transpose(0, 2, 1)),
        "b1T": np.ascontiguousarray(b1.reshape(8, 32, 128).transpose(0, 2, 1)),
        "b2T": np.ascontiguousarray(b2.reshape(8, 8, 128).transpose(0, 2, 1)),
        "routerT": np.ascontiguousarray(router_w.T),
        "costab": cst["costab"], "sintab": cst["sintab"],
        "wm64": cst["wm64"], "hpT": cst["hpT"],
        "s1pT": np.ascontiguousarray(
            signs[:, :, 0, :][..., cst["rowperm"]].reshape(64, 64).T).astype(np.float32),
        "s2T": np.ascontiguousarray(signs[:, :, 1, :].reshape(64, 64).T).astype(np.float32),
        "s3T": np.ascontiguousarray(signs[:, :, 2, :].reshape(64, 64).T).astype(np.float32),
        "gflat": np.ascontiguousarray(scales.reshape(1, 64 * 64)).astype(np.float32),
        "onescol": cst["onescol"], "onesrow": cst["onesrow"],
        "ind4b": cst["ind4b"], "ind4q": cst["ind4q"], "ind8b": cst["ind8b"],
        "id8": cst["id8"], "id128": cst["id128"], "c4096": cst["c4096"],
        "actconst": cst["actconst"],
    }

    if "nc" not in _CACHE:
        _CACHE["nc"] = build_program()
    nc = _CACHE["nc"]

    in_maps = []
    for c in range(C):
        m = dict(shared)
        m["xT"] = np.ascontiguousarray(x[c].T)
        in_maps.append(m)

    res = run_bass_kernel_spmd(nc, in_maps, core_ids=list(range(C)))
    out = np.empty((C, L, E), dtype=np.float32)
    for c in range(C):
        out[c] = res.results[c]["outT"].T
    return out
